# version 05 (final)
# speedup vs baseline: 1.0206x; 1.0051x over previous
import os
import sys

os.environ.setdefault("TRN_ALLOW_ICI_ALL_TO_ALL", "1")
try:
    import concourse.bass as bass
except ImportError:
    sys.path.insert(0, "/opt/trn_rl_repo")
    import concourse.bass as bass

import contextlib
import numpy as np
import concourse.mybir as mybir
from concourse.bass_utils import run_bass_kernel_spmd

F32 = mybir.dt.float32
BF16 = mybir.dt.bfloat16
AF = mybir.ActivationFunctionType
ALU = mybir.AluOpType
AX = mybir.AxisListType.X

MAGIC = 12582912.0


class Cfg:
    def __init__(self, S=4096, n_layers=1):
        self.S = S
        self.B = 4
        self.D = 1024
        self.T = self.B * S // 8
        self.TT = self.T // 128
        self.TC = self.T // 512
        self.ST = S // 128
        self.W = min(512, self.T)
        self.TC = self.T // self.W
        self.SCW = min(1024, S)
        self.SC = S // self.SCW
        self.n_layers = n_layers


def _blockdiag_tiles(w, transpose=False):
    NHP, O, I_ = w.shape
    per = 128 // I_
    n_t = (NHP * I_) // 128
    tiles = np.zeros((n_t, 128, 128), dtype=np.float32)
    for jt in range(n_t):
        for b in range(per):
            h = jt * per + b
            blk = w[h].T
            if transpose:
                blk = blk.T
            tiles[jt, b * I_:(b + 1) * I_, b * O:(b + 1) * O] = blk
    return tiles


def host_prepare(inputs, cfg):
    S, B, D, T = cfg.S, cfg.B, cfg.D, cfg.T
    f32 = np.float32
    g = {k: np.ascontiguousarray(np.asarray(v, dtype=f32)) for k, v in inputs.items()}

    sh = {}
    sh["w_qkv"] = g["L0_attn_w_qkv"]
    sh["l0_woutb"] = np.ascontiguousarray(
        np.vstack([g["L0_attn_w_out"], g["L0_attn_b_out"][None, :]]))
    sh["l0_rhsT"] = np.ascontiguousarray(g["L0_ffn_rhs"].T)
    sh["ident"] = np.eye(128, dtype=f32)

    hd = 64
    freqs = 10000.0 ** (-np.arange(0, hd, 2, dtype=np.float64) / hd)
    ang = np.arange(S, dtype=np.float64)[:, None] * freqs
    sin = np.sin(ang); cos = np.cos(ang)
    cosf = np.stack([cos, cos], axis=-1).reshape(S, hd)
    sinf = np.stack([sin, sin], axis=-1).reshape(S, hd)
    sin_signed = sinf.copy()
    sin_signed[:, :hd // 2] *= -1.0
    sh["rope_cos"] = np.ascontiguousarray(cosf.T.astype(f32))
    sh["rope_sin"] = np.ascontiguousarray(sin_signed.T.astype(f32))

    if "L2_attn_Wq" in g:
        sh["l2_wT"] = np.ascontiguousarray(np.hstack(
            [g["L2_attn_Wq"].T, g["L2_attn_Wk"].T,
             g["L2_attn_Wv"].T, g["L2_attn_Wu"].T]))
        b4 = np.concatenate([g["L2_attn_bq"], g["L2_attn_bk"],
                             g["L2_attn_bv"], g["L2_attn_bu"]])
        sh["l2_b4pm"] = np.ascontiguousarray(b4.reshape(32, 128).T)
        sh["l2_nb4pm"] = np.ascontiguousarray((-b4).reshape(32, 128).T)
        sh["l2_scale_pm"] = np.ascontiguousarray(
            g["L2_attn_scale"].reshape(8, 128).T)
        sh["l2_woT"] = np.ascontiguousarray(g["L2_attn_Wo"].T)
        sh["l2_bo"] = np.ascontiguousarray(g["L2_attn_bo"][None, :])
        sh["l2f_w1T"] = np.ascontiguousarray(g["L2_ffn_w1"].T)
        sh["l2f_b1"] = np.ascontiguousarray(g["L2_ffn_b1"][None, :])
        sh["l2f_w2T"] = np.ascontiguousarray(g["L2_ffn_w2"].T)
        sh["l2f_b2"] = np.ascontiguousarray(g["L2_ffn_b2"][None, :])

    l1p = {}
    if "L1_attn_w_up" in g:
        sh["l1f_w1T"] = np.ascontiguousarray(g["L1_ffn_w1"].T)
        sh["l1f_w2T"] = np.ascontiguousarray(g["L1_ffn_w2"].T)
        sh["l1f_b1"] = np.ascontiguousarray(g["L1_ffn_b1"][None, :])
        sh["l1f_b2"] = np.ascontiguousarray(g["L1_ffn_b2"][None, :])
        sh["l1f_gamma"] = np.ascontiguousarray(g["L1_ffn_gamma"][None, :])
        sh["l1f_beta"] = np.ascontiguousarray(g["L1_ffn_beta"][None, :])
        m4 = np.zeros((128, 4, 512), dtype=f32)
        for q in range(4):
            t_idx = np.arange(128)[:, None] + 128 * q
            s_idx = np.arange(512)[None, :]
            m4[:, q, :] = (t_idx <= s_idx)
        sh["masks4"] = np.ascontiguousarray(m4.reshape(128, 4 * 512))

        w_upT_g = g["L1_attn_w_up"].T
        wig_g, wfg_g = g["L1_attn_w_ig"], g["L1_attn_w_fg"]
        pm8 = lambda a: np.ascontiguousarray(
            a.reshape(16, 128, 8).transpose(1, 0, 2).reshape(128, 128))
        tile2h = lambda ts: np.ascontiguousarray(
            ts.transpose(1, 0, 2).reshape(128, ts.shape[0] * 128))
        for p in (0, 1):
            perm = [0, 1, 2, 3] if p == 0 else [2, 3, 0, 1]
            fperm = np.concatenate([np.arange(512 * h, 512 * h + 512)
                                    for h in perm])
            bperm = np.concatenate([np.arange(128 * h, 128 * h + 128)
                                    for h in perm])
            d = {}
            d["w_upT"] = np.ascontiguousarray(
                w_upT_g[:, np.concatenate([fperm, 2048 + fperm])])
            cw = g["L1_attn_conv_w"][:, 0, :][fperm]
            conv_diag = np.zeros((16, 4, 128, 128), dtype=f32)
            for mc in range(16):
                for j in range(4):
                    np.fill_diagonal(conv_diag[mc, j],
                                     cw[mc * 128:(mc + 1) * 128, j])
            d["conv_diag_h"] = np.ascontiguousarray(
                conv_diag.transpose(2, 1, 0, 3).reshape(128, 64 * 128))
            d["conv_b_pm"] = np.ascontiguousarray(
                g["L1_attn_conv_b"][fperm].reshape(16, 128).T)
            wq = g["L1_attn_w_q"][bperm]
            wk = g["L1_attn_w_k"][bperm]
            wv = g["L1_attn_w_v"][bperm]
            d["hw_q"] = tile2h(_blockdiag_tiles(wq))
            d["hw_k"] = tile2h(_blockdiag_tiles(wk))
            d["hw_v"] = tile2h(_blockdiag_tiles(wv))
            d["hw_qT"] = tile2h(_blockdiag_tiles(wq, transpose=True))
            d["hw_kT"] = tile2h(_blockdiag_tiles(wk, transpose=True))
            d["hw_vT"] = tile2h(_blockdiag_tiles(wv, transpose=True))
            wig = wig_g[perm][:, np.concatenate(
                [fperm, 2048 + fperm, 4096 + fperm])]
            wfg = wfg_g[perm][:, np.concatenate(
                [fperm, 2048 + fperm, 4096 + fperm])]
            d["wig_q8"] = pm8(np.hstack([wig[:, 0:2048].T, wfg[:, 0:2048].T]))
            d["wig_k8"] = pm8(np.hstack([wig[:, 2048:4096].T, wfg[:, 2048:4096].T]))
            d["wig_v8"] = pm8(np.hstack([wig[:, 4096:6144].T, wfg[:, 4096:6144].T]))
            d["big8"] = np.ascontiguousarray(np.concatenate(
                [g["L1_attn_b_ig"][perm], g["L1_attn_b_fg"][perm]])[None, :])
            d["skip_pm"] = np.ascontiguousarray(
                g["L1_attn_skip"][fperm].reshape(16, 128).T)
            d["w_downT"] = np.ascontiguousarray(
                g["L1_attn_w_down"].T[fperm])
            d["lnw_u"] = np.ascontiguousarray(
                g["L1_attn_ln_w"][perm[:2]])
            l1p[p] = d

    l3p = {}
    if "L3_attn_W_B" in g:
        sh["l3_wB"] = np.ascontiguousarray(g["L3_attn_W_B"].T)
        sh["l3_wC"] = np.ascontiguousarray(g["L3_attn_W_C"].T)
        sh["l3_wdt1"] = np.ascontiguousarray(g["L3_attn_W_dt1"].T)
        sh["l3_dskip"] = np.ascontiguousarray(g["L3_attn_D_skip"][None, :])
        sh["l3f_w1T"] = np.ascontiguousarray(g["L3_ffn_w1"])
        sh["l3f_b1"] = np.ascontiguousarray(g["L3_ffn_b1"][None, :])
        sh["l3f_w2T"] = np.ascontiguousarray(g["L3_ffn_w2"])
        sh["l3f_b2"] = np.ascontiguousarray(g["L3_ffn_b2"][None, :])
        for p in (0, 1):
            fperm4 = np.concatenate(
                [np.arange(512 * p, 512 * p + 512),
                 np.arange(512 * (1 - p), 512 * (1 - p) + 512)])
            d = {}
            d["l3_wdt2T"] = np.ascontiguousarray(
                g["L3_attn_W_dt2"].T[:, fperm4])
            d["l3_bdt2pm"] = np.ascontiguousarray(
                g["L3_attn_b_dt2"][fperm4].reshape(8, 128).T)
            d["l3_logA"] = np.ascontiguousarray(
                g["L3_attn_log_A"][512 * p:512 * p + 512, :])
            l3p[p] = d

    per_core = []
    x = g["L0_attn_x"].reshape(B * S, D)
    for c in range(8):
        m = dict(sh)
        m["x0"] = np.ascontiguousarray(x[c * T:(c + 1) * T, :])
        m["halo_mask"] = np.full((128, 1), float(c % 2 == 1), dtype=f32)
        if l1p:
            m.update(l1p[c % 2])
            m["pidx"] = np.array([[c % 2]], dtype=np.uint32)
            m["opidx"] = np.array([[1 - c % 2]], dtype=np.uint32)
        if l3p:
            m.update(l3p[c % 2])
        per_core.append(m)
    return per_core


class Sem:
    def __init__(self, handle):
        self.h = handle
        self.n = 0

    def inc(self, ins, k):
        ins.then_inc(self.h, k)
        self.n += k
        return ins


class Arena:

    def __init__(self, bld, name, cols, dt):
        self.h = bld.sb(name, [128, cols], dt)
        self.cols = cols
        self.off = 0

    def alloc(self, parts, n):
        a = self.off
        self.off = a + ((n + 63) // 64) * 64
        assert self.off <= self.cols, (self.off, self.cols)
        return self.h[0:parts, a:a + n]

    def reset(self):
        self.off = 0


class Bld:
    ENGS = ("tensor", "vector", "scalar", "gpsimd", "sync")

    def __init__(self, cfg):
        self.cfg = cfg
        self.nc = bass.Bass("TRN2", target_bir_lowering=False, num_devices=8)
        self.ctx = contextlib.ExitStack()
        self.t = {}
        self.d = {}
        self.blk = None
        for name in ("tm", "vv", "ss", "gq", "syq", "scq", "cc", "gx", "sye", "syo", "rde", "rdo", "gsx", "vtq", "qpe", "qpo", "sce", "sco", "gpe", "gpo", "kq"):
            setattr(self, name, Sem(self.ctx.enter_context(self.nc.semaphore(name))))

    def dram(self, name, shape, dt=F32, kind="Internal"):
        ap = self.nc.dram_tensor(name, shape, dt, kind=kind).ap()
        self.d[name] = ap
        return ap

    def sb(self, name, shape, dt=BF16):
        h = self.ctx.enter_context(self.nc.sbuf_tensor(name, shape, dt))
        self.t[name] = h
        return h

    def psum(self, name, shape, dt=F32):
        h = self.ctx.enter_context(self.nc.psum_tensor(name, shape, dt))
        self.t[name] = h
        return h

    def T(self, f):
        self.blk.tensor(f)

    def V(self, f):
        self.blk.vector(f)

    def S(self, f):
        self.blk.scalar(f)

    def G(self, f):
        self.blk.gpsimd(f)

    def Y(self, f):
        self.blk.sync(f)

    def barrier(self):
        sems = [self.tm, self.vv, self.ss, self.gq, self.syq, self.scq, self.cc,
                self.gx, self.sye, self.syo, self.rde, self.rdo, self.gsx,
                self.vtq, self.qpe, self.qpo, self.sce, self.sco,
                self.gpe, self.gpo, self.kq]
        tgt = [(s.h, s.n) for s in sems if s.n > 0]

        def mk(e):
            for h, n in tgt:
                e.wait_ge(h, n)
        for name in self.ENGS:
            getattr(self.blk, name)(mk)


def emit_ln(E, waits, out_bf_ap, out_f32_ap=None, eps=1e-5, src_ap=None,
            gb=None):
    t = E.t
    D = E.cfg.D
    ss, vv = E.ss, E.vv
    src_ap = t["ln_r"][:, :] if src_ap is None else src_ap

    vv_ent = vv.n

    def s_f(s):
        for h, n in waits:
            s.wait_ge(h, n)
        s.wait_ge(vv.h, vv_ent)
        ss.inc(s.activation(out=t["sq_scr"][:, :], in_=src_ap,
                            func=AF.Square, accum_out=t["ln_q"][:, 0:1]), 1)
    E.S(s_f)
    ss_sq = ss.n

    def v_f(v):
        for h, n in waits:
            v.wait_ge(h, n)
        vv.inc(v.tensor_reduce(t["ln_s"][:, 0:1], src_ap, AX, ALU.add), 1)
        v.wait_ge(ss.h, ss_sq)
        v.wait_ge(vv.h, vv.n)
        vv.inc(v.tensor_scalar(t["ln_a"][:, 0:1], t["ln_s"][:, 0:1],
                               1.0 / D, None, ALU.mult), 1)
        v.wait_ge(vv.h, vv.n)
        vv.inc(v.tensor_scalar(t["ln_b"][:, 0:1], t["ln_q"][:, 0:1],
                               1.0 / D, None, ALU.mult), 1)
        v.wait_ge(vv.h, vv.n)
        vv.inc(v.scalar_tensor_tensor(t["ln_q"][:, 0:1], t["ln_a"][:, 0:1],
                                      -1.0, t["ln_a"][:, 0:1],
                                      ALU.mult, ALU.mult), 1)
        v.wait_ge(vv.h, vv.n)
        vv.inc(v.tensor_tensor(t["ln_b"][:, 0:1], t["ln_b"][:, 0:1],
                               t["ln_q"][:, 0:1], ALU.add), 1)
        v.wait_ge(vv.h, vv.n)
        vv.inc(v.tensor_scalar(t["ln_b"][:, 0:1], t["ln_b"][:, 0:1],
                               eps, None, ALU.add), 1)
    E.V(v_f)
    vv_var = E.vv.n

    def s_rstd(s):
        s.wait_ge(vv.h, vv_var)
        ss.inc(s.activation(out=t["ln_q"][:, 0:1], in_=t["ln_b"][:, 0:1],
                            func=AF.Sqrt), 1)
    E.S(s_rstd)
    ss_sqrt = ss.n

    def v_app(v):
        v.wait_ge(ss.h, ss_sqrt)
        vv.inc(v.reciprocal(t["ln_b"][:, 0:1], t["ln_q"][:, 0:1]), 1)
        v.wait_ge(vv.h, vv.n)
        if gb is not None:
            g_bc, b_bc = gb
            vv.inc(v.tensor_scalar(t["ln_f"][:, :], src_ap,
                                   t["ln_a"][:, 0:1], t["ln_b"][:, 0:1],
                                   ALU.subtract, ALU.mult), 1)
            v.wait_ge(vv.h, vv.n)
            vv.inc(v.tensor_tensor(t["ln_f"][:, :], t["ln_f"][:, :], g_bc,
                                   ALU.mult), 1)
            v.wait_ge(vv.h, vv.n)
            if out_f32_ap is not None:
                vv.inc(v.tensor_tensor(out_f32_ap, t["ln_f"][:, :], b_bc,
                                       ALU.add), 1)
                v.wait_ge(vv.h, vv.n)
                vv.inc(v.tensor_copy(out_bf_ap, out_f32_ap), 1)
            else:
                vv.inc(v.tensor_tensor(out_bf_ap, t["ln_f"][:, :], b_bc,
                                       ALU.add), 1)
        elif out_f32_ap is not None:
            vv.inc(v.tensor_scalar(out_f32_ap, src_ap,
                                   t["ln_a"][:, 0:1], t["ln_b"][:, 0:1],
                                   ALU.subtract, ALU.mult), 1)
            v.wait_ge(vv.h, vv.n)
            vv.inc(v.tensor_copy(out_bf_ap, out_f32_ap), 1)
        else:
            vv.inc(v.tensor_scalar(out_bf_ap, src_ap,
                                   t["ln_a"][:, 0:1], t["ln_b"][:, 0:1],
                                   ALU.subtract, ALU.mult), 1)
    E.V(v_app)


def emit_ystore(E, i, par):
    t, dd = E.t, E.d
    vv, syq = E.vv, E.syq
    D = E.cfg.D
    vv_ln = vv.n

    def sy_y(sy, i=i, par=par, vv_ln=vv_ln):
        sy.wait_ge(vv.h, vv_ln)
        syq.inc(sy.dma_start(out=dd["y"][i * 128:(i + 1) * 128, :],
                             in_=t["yout"][:, par * D:(par + 1) * D]), 16)
    E.Y(sy_y)
    sy_after = syq.n

    def v_gate(v, sy_after=sy_after):
        v.wait_ge(syq.h, sy_after)
        vv.inc(v.engine_nop(), 1)
    E.V(v_gate)


def emit_stream_to_fm(E, x_fm):
    t, dd = E.t, E.d
    syq = E.syq
    D, T, TT = E.cfg.D, E.cfg.T, E.cfg.TT

    def sy_f(sy):
        sy.wait_ge(E.vv.h, E.vv.n)
        for i in range(TT):
            syq.inc(sy.dma_start(out=dd["xds"][i * 128:(i + 1) * 128, :],
                                 in_=t["stream"][:, i * D:(i + 1) * D]), 16)
        sy.wait_ge(syq.h, syq.n)
        for kc in range(8):
            syq.inc(sy.dma_start(out=x_fm[:, kc * T:(kc + 1) * T],
                                 in_=dd["xds"][:, kc * 128:(kc + 1) * 128],
                                 transpose=True), 16)
    E.Y(sy_f)
    return syq.n


def build_program(cfg):
    E = Bld(cfg)
    S, D, T, TT, TC, ST, SC = cfg.S, cfg.D, cfg.T, cfg.TT, cfg.TC, cfg.ST, cfg.SC
    W, SCW = cfg.W, cfg.SCW
    nc = E.nc
    t = E.t
    dd = E.d
    tm, vv, ss, gq, syq, scq, cc = E.tm, E.vv, E.ss, E.gq, E.syq, E.scq, E.cc
    gx, sye, syo, rde, rdo = E.gx, E.sye, E.syo, E.rde, E.rdo

    for nm, shp in [("x0", [T, D]), ("w_qkv", [D, 3 * D]),
                    ("l0_woutb", [D + 1, D]), ("l0_rhsT", [D, D]),
                    ("rope_cos", [64, S]), ("rope_sin", [64, S]),
                    ("ident", [128, 128])]:
        E.dram(nm, shp, kind="ExternalInput")
    E.dram("y", [T, D], kind="ExternalOutput")
    if cfg.n_layers >= 2:
        for nm, shp in [("w_upT", [D, 4 * D]), ("conv_diag_h", [128, 64 * 128]),
                        ("conv_b_pm", [128, 16]),
                        ("hw_q", [128, 16 * 128]), ("hw_k", [128, 16 * 128]),
                        ("hw_v", [128, 16 * 128]),
                        ("hw_qT", [128, 16 * 128]), ("hw_kT", [128, 16 * 128]),
                        ("hw_vT", [128, 16 * 128]),
                        ("wig_q8", [128, 128]), ("wig_k8", [128, 128]),
                        ("wig_v8", [128, 128]), ("big8", [1, 8]),
                        ("skip_pm", [128, 16]), ("w_downT", [2 * D, D]),
                        ("l1f_w1T", [D, 4 * D]), ("l1f_w2T", [4 * D, D]),
                        ("l1f_b1", [1, 4 * D]), ("l1f_b2", [1, D]),
                        ("l1f_gamma", [1, D]), ("l1f_beta", [1, D]),
                        ("lnw_u", [2, 512]), ("masks4", [128, 4 * 512]),
                        ("halo_mask", [128, 1])]:
            E.dram(nm, shp, kind="ExternalInput")
        for nm in ("pidx", "opidx"):
            E.d[nm] = E.nc.dram_tensor(
                nm, [1, 1], mybir.dt.uint32, kind="ExternalInput").ap()
        R1 = 1538
        E.dram("xmds", [2 * D, T], BF16)
        E.dram("zds", [2 * D, T], BF16)
        E.dram("xcds", [2 * D, T], BF16)
        E.dram("halo_in", [2 * D, 4], BF16)
        E.dram("halo_out", [2 * 2 * D, 4], BF16)
        E.dram("keepst", [2 * R1, T], BF16)
        E.dram("agin", [2 * R1, T], BF16)
        E.dram("agout", [4 * R1, T], BF16)
        E.dram("a2a1o", [4 * R1, T], BF16)
        E.dram("a2a1ri", [4 * T, 512], BF16)
        E.dram("ri_in", [2 * T, 512], BF16)
        E.dram("ri_out", [4 * T, 512], BF16)
        E.dram("hret", [4 * T, 512], BF16)
        E.dram("gds", [2 * cfg.S], F32)
        E.dram("nfds", [cfg.S], BF16)
        E.dram("hsds", [2 * D, T], BF16)
        E.dram("hds", [T, 4 * D], BF16)
        E.dram("xq2ds", [T, 4 * D], BF16)
        E.dram("w1qds", [D, 4 * D], BF16)
        E.dram("w2qds", [4 * D, D], BF16)
    if cfg.n_layers >= 3:
        for nm, shp in [("l2_wT", [D, 4 * D]), ("l2_b4pm", [128, 32]),
                        ("l2_nb4pm", [128, 32]), ("l2_scale_pm", [128, 8]),
                        ("l2_woT", [D, D]), ("l2_bo", [1, D]),
                        ("l2f_w1T", [D, 4 * D]), ("l2f_b1", [1, 4 * D]),
                        ("l2f_w2T", [4 * D, D]), ("l2f_b2", [1, D])]:
            E.dram(nm, shp, kind="ExternalInput")
        E.dram("a2a2i", [8 * 384, T], BF16)
        E.dram("a2a2o", [8 * 384, T], BF16)
        E.dram("a2a2ri", [8 * T, 128], BF16)
        E.dram("a2a2ro", [8 * T, 128], BF16)
        E.dram("ufds", [D, T], BF16)
        E.dram("sgn1ds", [D, 4 * D], BF16)
        E.dram("sgn2ds", [4 * D, D], BF16)
        E.dram("alds", [4 * D], F32)
    if cfg.n_layers >= 4:
        R3 = 544
        for nm, shp in [("l3_wB", [D, 16]), ("l3_wC", [D, 16]),
                        ("l3_wdt1", [D, 64]), ("l3_wdt2T", [64, D]),
                        ("l3_bdt2pm", [128, 8]), ("l3_logA", [512, 16]),
                        ("l3_dskip", [1, D]),
                        ("l3f_w1T", [D, 4 * D]), ("l3f_b1", [1, 4 * D]),
                        ("l3f_w2T", [4 * D, D]), ("l3f_b2", [1, D])]:
            E.dram(nm, shp, kind="ExternalInput")
        PK = "ExternalOutput" if os.environ.get("KPROBE") else "Internal"
        E.dram("keep3", [2 * R3, T], BF16, kind=PK)
        E.dram("agin3", [2 * R3, T], BF16)
        E.dram("agout3", [4 * R3, T], BF16)
        E.dram("a2a3o", [4 * R3, T], BF16, kind=PK)
        E.dram("a2ri3", [4 * 256, T], BF16, kind=PK)
        E.dram("ri3_in", [512, T], BF16)
        E.dram("ri3_out", [2 * 512, T], BF16)
        E.dram("yret", [D, T], BF16, kind=PK)
        E.dram("xfds", [D, T], BF16, kind=PK)
        E.dram("dtds", [D, T], BF16, kind=PK)
    E.dram("xds", [T, D], BF16)
    E.dram("a2a0i", [8 * 384, T], BF16)
    E.dram("a2a0o", [8 * 384, T], BF16)
    E.dram("a2a0ri", [8 * 130, T], BF16)
    E.dram("a2a0ro", [8 * 130, T], BF16)
    E.dram("xqds", [T, D], BF16)
    E.dram("rqds", [D, D], BF16)
    E.dram("rswds", [D], F32)
    E.dram("densds", [16, T], F32)

    E.sb("stream", [128, TT * D])
    E.sb("ones_row", [1, max(S, 1024)])
    E.sb("onescol64", [64, 1])
    E.sb("onescol128", [128, 1])
    E.sb("onescol128f", [128, 1], F32)
    E.sb("ln_r", [128, D], F32)
    E.sb("ln_f", [128, D], F32)
    E.sb("ln_s", [128, 1], F32)
    E.sb("ln_q", [128, 1], F32)
    E.sb("ln_a", [128, 1], F32)
    E.sb("ln_b", [128, 1], F32)
    E.sb("sq_scr", [128, D], F32)
    E.sb("ident_s", [128, 128], F32)
    E.sb("rsw_bc", [128, D], F32)
    E.sb("lsp", [128, TT], F32)
    E.sb("rqp", [128, 1], F32)
    E.sb("mred", [1, 4], F32)
    E.sb("m8", [1, 1], F32)
    E.sb("tsc", [128, 128], F32)
    ABa = Arena(E, "AB", 45056, BF16)
    AFa = Arena(E, "AF", 8192, F32)
    E.ABa, E.AFa = ABa, AFa

    def ar(name, parts, n, fr=ABa):
        t[name] = fr.alloc(parts, n)

    ar("x_fm", 128, 8 * T)
    ar("wq_sb", 128, 8 * 3 * D)
    ar("stg", 128, 2 * T)
    t["rhsT_f"] = t["wq_sb"][:, 0:8 * D]
    t["rhq_s"] = t["wq_sb"][:, 8 * D:16 * D]
    ar("rswv", 128, 8, AFa)
    ar("rsrec", 128, 8, AFa)

    ABa.reset()
    ar("ropec", 64, S)
    ar("ropes", 64, S)
    ar("q_ext", 65, S)
    ar("k_ext", 65, S)
    ar("rot", 64, S)
    ar("v2e", 128, ST * 160)
    ar("qsq", 64, S)
    t["rtmp"] = t["qsq"]
    ar("msq", 1, S)
    ar("pbuf", 128, 2 * SCW)
    ar("v_tmp", 128, 2 * 128)
    ar("numb", 64, S)
    ar("denb", 1, S)

    ABa.reset()
    ar("att_fm", 128, 8 * T)
    ar("xq_tm", 128, TT * D)
    ar("l0w_sb", 128, 9 * D)
    t["xq_fm"] = t["att_fm"][:, 0:8 * T]
    t["rhsq_fm"] = t["l0w_sb"][:, 0:8 * D]
    AFa.reset()
    ar("dens", 16, T, AFa)
    ar("rdbc", 128, 2 * T, AFa)
    ar("yout", 128, 2 * D, AFa)

    E.psum("ps8", [128, 8 * 512])
    ps8 = t["ps8"]

    def bank(i):
        return ps8[:, i * 512:(i + 1) * 512]

    with nc.Block() as blk:
        E.blk = blk

        def g_init(g):
            for i in range(TT):
                gx.inc(g.dma_start(out=t["stream"][:, i * D:(i + 1) * D],
                                   in_=dd["x0"][i * 128:(i + 1) * 128, :]), 16)
            for kc in range(8):
                gq.inc(g.dma_start(out=t["wq_sb"][:, kc * 3 * D:(kc + 1) * 3 * D],
                                   in_=dd["w_qkv"][kc * 128:(kc + 1) * 128, :]), 16)
            gq.inc(g.dma_start(out=t["ident_s"][:, :], in_=dd["ident"][:, :]), 16)
        E.G(g_init)
        g_stream = gx.n

        def v_init(v):
            vv.inc(v.memset(t["ones_row"][:, :], 1.0), 1)
            vv.inc(v.memset(t["onescol64"][:, :], 1.0), 1)
            vv.inc(v.memset(t["onescol128"][:, :], 1.0), 1)
            vv.inc(v.memset(t["onescol128f"][:, :], 1.0), 1)
        E.V(v_init)

        def sy_init(sy):
            sy.wait_ge(gx.h, g_stream)
            for i in range(TT):
                syq.inc(sy.dma_start(out=dd["xds"][i * 128:(i + 1) * 128, :],
                                     in_=t["stream"][:, i * D:(i + 1) * D]), 16)
            sy.wait_ge(syq.h, syq.n)
            for kc in range(8):
                syq.inc(sy.dma_start(out=t["x_fm"][:, kc * T:(kc + 1) * T],
                                     in_=dd["xds"][:, kc * 128:(kc + 1) * 128],
                                     transpose=True), 16)
        E.Y(sy_init)

        E.barrier()

        slab_sy = {}
        ss_evac = {}
        for mc in range(24):
            par = mc % 2

            def t_mm(te, mc=mc, par=par):
                if mc >= 2:
                    te.wait_ge(ss.h, ss_evac[mc - 2])
                for kc in range(8):
                    for tc in range(TC):
                        ins = te.matmul(
                            bank(par * TC + tc)[:, 0:W],
                            t["wq_sb"][:, kc * 3 * D + mc * 128:kc * 3 * D + (mc + 1) * 128],
                            t["x_fm"][:, kc * T + tc * W:kc * T + (tc + 1) * W],
                            start=(kc == 0), stop=(kc == 7))
                        if kc == 7:
                            tm.inc(ins, 1)
            E.T(t_mm)
            tm_after = tm.n

            def s_evac(s, mc=mc, par=par, tm_after=tm_after):
                if mc >= 2:
                    s.wait_ge((sye if mc % 2 == 0 else syo).h, slab_sy[mc - 2])
                for tc in range(TC):
                    s.wait_ge(tm.h, tm_after - TC + tc + 1)
                    ss.inc(s.activation(
                        out=t["stg"][:, par * T + tc * W:par * T + (tc + 1) * W],
                        in_=bank(par * TC + tc)[:, 0:W], func=AF.Copy), 1)
            E.S(s_evac)
            ss_evac[mc] = ss.n

            psem = sye if par == 0 else syo

            def sy_st(sy, mc=mc, par=par, ssn=ss.n, psem=psem):
                sy.wait_ge(ss.h, ssn)
                p, hh = mc // 8, mc % 8
                for rr in (0, 1):
                    h = 2 * hh + rr
                    slot, jj = h % 8, h // 8
                    r0 = slot * 384 + 128 * p + 64 * jj
                    psem.inc(sy.dma_start(
                        out=dd["a2a0i"][r0:r0 + 64, :],
                        in_=t["stg"][64 * rr:64 * rr + 64, par * T:(par + 1) * T]), 16)
            E.Y(sy_st)
            slab_sy[mc] = psem.n

        def g_a2a0(g):
            g.wait_ge(sye.h, sye.n)
            g.wait_ge(syo.h, syo.n)
            cc.inc(g.collective_compute(
                "AllToAll", ALU.bypass, replica_groups=[list(range(8))],
                ins=[dd["a2a0i"][:, :]], outs=[dd["a2a0o"][:, :]]), 1)
        E.G(g_a2a0)
        cc_fwd = cc.n

        tm_qkv_all = tm.n

        def g_rhs(g):
            g.wait_ge(tm.h, tm_qkv_all)
            for kc in range(8):
                gq.inc(g.dma_start(out=t["rhsT_f"][:, kc * D:(kc + 1) * D],
                                   in_=dd["l0_rhsT"][kc * 128:(kc + 1) * 128, :]), 16)
        E.G(g_rhs)

        def v_rq(v):
            v.wait_ge(gq.h, gq.n)
            for c in range(8):
                vv.inc(v.tensor_reduce(t["rswv"][:, c:c + 1],
                                       t["rhsT_f"][:, c * D:(c + 1) * D], AX,
                                       ALU.max, apply_absolute_value=True), 1)
            v.wait_ge(vv.h, vv.n)
            vv.inc(v.tensor_scalar(t["rswv"][:, :], t["rswv"][:, :],
                                   1e-12, None, ALU.max), 1)
            v.wait_ge(vv.h, vv.n)
            vv.inc(v.reciprocal(t["rsrec"][:, :], t["rswv"][:, :]), 1)
            v.wait_ge(vv.h, vv.n)
            vv.inc(v.tensor_scalar(t["rsrec"][:, :], t["rsrec"][:, :],
                                   127.0, None, ALU.mult), 1)
            vv.inc(v.tensor_scalar(t["rswv"][:, :], t["rswv"][:, :],
                                   1.0 / 127.0, None, ALU.mult), 1)
            v.wait_ge(vv.h, vv.n)
            for c in range(8):
                vv.inc(v.tensor_scalar(t["sq_scr"][:, :],
                                       t["rhsT_f"][:, c * D:(c + 1) * D],
                                       t["rsrec"][:, c:c + 1], MAGIC,
                                       ALU.mult, ALU.add), 1)
                v.wait_ge(vv.h, vv.n)
                vv.inc(v.tensor_scalar(t["rhq_s"][:, c * D:(c + 1) * D],
                                       t["sq_scr"][:, :], MAGIC, None,
                                       ALU.subtract), 1)
                v.wait_ge(vv.h, vv.n)
        E.V(v_rq)
        vv_rq = vv.n

        def t_rsw(te):
            te.wait_ge(vv.h, vv_rq)
            te.wait_ge(gq.h, gq.n)
            te.wait_ge(ss.h, ss_evac[23])
            tm.inc(te.transpose(ps8[0:8, 0:128], t["rswv"][:, 0:8],
                                t["ident_s"][:, :]), 1)
        E.T(t_rsw)
        tm_rsw = tm.n

        def v_rsw(v):
            v.wait_ge(tm.h, tm_rsw)
            vv.inc(v.tensor_copy(t["tsc"][0:8, 0:128], ps8[0:8, 0:128]), 1)
        E.V(v_rsw)
        vv_rsw = vv.n

        def sy_rsw(sy):
            sy.wait_ge(vv.h, vv_rsw)
            syq.inc(sy.dma_start(out=dd["rswds"].rearrange("(c p) -> c p", p=128),
                                 in_=t["tsc"][0:8, 0:128]), 16)
            sy.wait_ge(syq.h, syq.n)
            syq.inc(sy.dma_start(
                out=t["rsw_bc"][:, :],
                in_=dd["rswds"][None, :].partition_broadcast(128)), 16)
        E.Y(sy_rsw)

        def sc_rq(s):
            s.wait_ge(vv.h, vv_rq)
            for c in range(8):
                scq.inc(s.dma_start(out=dd["rqds"][c * 128:(c + 1) * 128, :],
                                    in_=t["rhq_s"][:, c * D:(c + 1) * D]), 16)
        E.S(sc_rq)

        E.barrier()

        def g_rope(g):
            gq.inc(g.dma_start(out=t["ropec"][:, :], in_=dd["rope_cos"][:, :]), 16)
            gq.inc(g.dma_start(out=t["ropes"][:, :], in_=dd["rope_sin"][:, :]), 16)
        E.G(g_rope)
        g_rope_n = gq.n

        def v_ones2(v):
            v3 = t["v2e"][:, :].rearrange("p (n e) -> p n e", e=160)
            vv.inc(v.memset(v3[:, :, 64:65], 1.0), 1)
            vv.inc(v.memset(v3[:, :, 144:145], 1.0), 1)
            vv.inc(v.memset(t["k_ext"][64:65, :], 1.0), 1)
        E.V(v_ones2)

        units = [(b, j) for b in range(4) for j in (0, 1)]
        for ui, (b, j) in enumerate(units):
            def sy_qk(sy, b=b, j=j, ui=ui):
                sy.wait_ge(tm.h, tm.n)
                sy.wait_ge(vv.h, vv.n)
                sy.wait_ge(ss.h, ss.n)
                for half, r in enumerate((2 * b, 2 * b + 1)):
                    r0 = r * 384
                    sl = slice(half * T, (half + 1) * T)
                    syq.inc(sy.dma_start(
                        out=t["q_ext"][0:64, sl],
                        in_=dd["a2a0o"][r0 + 128 * 0 + j * 64:r0 + 128 * 0 + (j + 1) * 64, :]), 16)
                    syq.inc(sy.dma_start(
                        out=t["k_ext"][0:64, sl],
                        in_=dd["a2a0o"][r0 + 128 + j * 64:r0 + 128 + (j + 1) * 64, :]), 16)
                    q0 = r0 + j * 64
                    syq.inc(sy.dma_start(out=t["rot"][0:32, sl],
                                         in_=dd["a2a0o"][q0 + 1:q0 + 64:2, :]), 16)
                    syq.inc(sy.dma_start(out=t["rot"][32:64, sl],
                                         in_=dd["a2a0o"][q0:q0 + 64:2, :]), 16)
            E.Y(sy_qk)
            sy_qk_n = syq.n

            if j == 0:
                for st in range(ST):
                    def sy_v(sy, b=b, st=st):
                        half = (st * 128) // T
                        r = 2 * b + half
                        c0 = (st * 128) % T
                        slot = st % 2
                        if st >= 2:
                            sy.wait_ge(vv.h, vasm[st - 2])
                        (rde if st % 2 == 0 else rdo).inc(sy.dma_start(
                            out=t["v_tmp"][:, slot * 128:(slot + 1) * 128],
                            in_=dd["a2a0o"][r * 384 + 256:r * 384 + 384, c0:c0 + 128],
                            transpose=True), 16)
                    E.Y(sy_v)
                    if st == 0:
                        vasm = {}
                    vsem_n = (rde if st % 2 == 0 else rdo).n

                    def v_asm(v, st=st, vsem_n=vsem_n):
                        v.wait_ge((rde if st % 2 == 0 else rdo).h, vsem_n)
                        slot = st % 2
                        dst = t["v2e"][:, st * 160:(st + 1) * 160].rearrange(
                            "p (n e) -> p n e", e=80)[:, :, 0:64]
                        s3 = t["v_tmp"][:, slot * 128:(slot + 1) * 128].rearrange(
                            "p (n e) -> p n e", e=64)
                        vv.inc(v.tensor_copy(dst, s3), 1)
                    E.V(v_asm)
                    vasm[st] = vv.n

            def v_ropeq(v):
                v.wait_ge(syq.h, sy_qk_n)
                v.wait_ge(gq.h, g_rope_n)
                vv.inc(v.tensor_tensor(t["rtmp"][:, :], t["q_ext"][0:64, :],
                                       t["ropec"][:, :], ALU.mult), 1)
                v.wait_ge(vv.h, vv.n)
                vv.inc(v.tensor_tensor(t["q_ext"][0:64, :], t["rot"][:, :],
                                       t["ropes"][:, :], ALU.mult), 1)
                v.wait_ge(vv.h, vv.n)
                vv.inc(v.tensor_tensor(t["q_ext"][0:64, :], t["q_ext"][0:64, :],
                                       t["rtmp"][:, :], ALU.add), 1)
            E.V(v_ropeq)
            vv_ropeq = vv.n

            def sy_rotk(sy, b=b, j=j):
                sy.wait_ge(vv.h, vv_ropeq)
                for half, r in enumerate((2 * b, 2 * b + 1)):
                    k0 = r * 384 + 128 + j * 64
                    sl = slice(half * T, (half + 1) * T)
                    syq.inc(sy.dma_start(out=t["rot"][0:32, sl],
                                         in_=dd["a2a0o"][k0 + 1:k0 + 64:2, :]), 16)
                    syq.inc(sy.dma_start(out=t["rot"][32:64, sl],
                                         in_=dd["a2a0o"][k0:k0 + 64:2, :]), 16)
            E.Y(sy_rotk)
            sy_rotk_n = syq.n

            def v_ropek(v):
                v.wait_ge(syq.h, sy_rotk_n)
                vv.inc(v.tensor_tensor(t["rtmp"][:, :], t["k_ext"][0:64, :],
                                       t["ropec"][:, :], ALU.mult), 1)
                v.wait_ge(vv.h, vv.n)
                vv.inc(v.tensor_tensor(t["k_ext"][0:64, :], t["rot"][:, :],
                                       t["ropes"][:, :], ALU.mult), 1)
                v.wait_ge(vv.h, vv.n)
                vv.inc(v.tensor_tensor(t["k_ext"][0:64, :], t["k_ext"][0:64, :],
                                       t["rtmp"][:, :], ALU.add), 1)
            E.V(v_ropek)
            vv_ropek = vv.n

            for wi, buf in ((0, "q_ext"), (1, "k_ext")):
                def s_sq(s, buf=buf):
                    s.wait_ge(vv.h, vv_ropek)
                    if buf == "k_ext":
                        s.wait_ge(vv.h, vv.n)
                    ss.inc(s.activation(out=t["qsq"][:, :], in_=t[buf][0:64, :],
                                        func=AF.Square), 1)
                E.S(s_sq)
                ss_sq = ss.n
                tm_base = tm.n
                msq_vb = vv.n

                def t_sum(te, ss_sq=ss_sq, tm_base=tm_base, wi=wi, msq_vb=msq_vb):
                    te.wait_ge(ss.h, ss_sq)
                    if wi == 1:
                        te.wait_ge(vv.h, vv.n)
                    for c in range(S // 512):
                        if c >= 2:
                            te.wait_ge(vv.h, msq_vb + c - 1)
                        tm.inc(te.matmul(ps8[0:1, (6 + c % 2) * 512:(6 + c % 2) * 512 + 512],
                                         t["onescol64"][:, :],
                                         t["qsq"][:, c * 512:(c + 1) * 512],
                                         start=True, stop=True), 1)
                E.T(t_sum)

                def v_msq(v, tm_base=tm_base, wi=wi):
                    for c in range(S // 512):
                        v.wait_ge(tm.h, tm_base + c + 1)
                        vv.inc(v.tensor_copy(
                            t["msq"][0:1, c * 512:(c + 1) * 512],
                            ps8[0:1, (6 + c % 2) * 512:(6 + c % 2) * 512 + 512]), 1)
                    v.wait_ge(vv.h, vv.n)
                    vv.inc(v.tensor_reduce(t["mred"][0:1, wi:wi + 1],
                                           t["msq"][0:1, 0:S], AX, ALU.max), 1)
                E.V(v_msq)

            def v_m8(v):
                v.wait_ge(vv.h, vv.n)
                vv.inc(v.tensor_tensor(t["mred"][0:1, 2:3], t["mred"][0:1, 0:1],
                                       t["mred"][0:1, 1:2], ALU.mult), 1)
            E.V(v_m8)
            vv_m8a = vv.n

            def s_m8(s):
                s.wait_ge(vv.h, vv_m8a)
                ss.inc(s.activation(out=t["m8"][0:1, 0:1], in_=t["mred"][0:1, 2:3],
                                    func=AF.Sqrt), 1)
            E.S(s_m8)
            ss_m8 = ss.n

            def v_m8row(v):
                s = t
                v.wait_ge(ss.h, ss_m8)
                vv.inc(v.tensor_scalar(t["m8"][0:1, 0:1], t["m8"][0:1, 0:1],
                                       -1.0, None, ALU.mult), 1)
                v.wait_ge(vv.h, vv.n)
                vv.inc(v.tensor_scalar(t["q_ext"][64:65, 0:S],
                                       t["ones_row"][0:1, 0:S],
                                       t["m8"][0:1, 0:1], None, ALU.mult), 1)
            E.V(v_m8row)
            vv_qrow = vv.n

            for sc in range(SC):
                tm_sc0 = tm.n
                ss_sc0 = ss.n

                def t_sc(te, sc=sc, tm_sc0=tm_sc0, ss_sc0=ss_sc0,
                         vv_qrow=vv_qrow):
                    if sc == 0:
                        te.wait_ge(vv.h, vv_qrow)
                        te.wait_ge(syq.h, syq.n)
                    else:
                        te.wait_ge(ss.h, ss_sc0)
                        te.wait_ge(vv.h, vv.n)
                    HW = SCW // 2
                    for tt in range(ST):
                        par = tt % 2
                        if tt >= 2:
                            te.wait_ge(ss.h, ss_sc0 + tt - 1)
                        for half in range(2):
                            c0 = 1024 + par * 1024 + half * 512
                            tm.inc(te.matmul(
                                ps8[:, c0:c0 + HW],
                                t["k_ext"][:, tt * 128:(tt + 1) * 128],
                                t["q_ext"][:, sc * SCW + half * HW:sc * SCW + (half + 1) * HW],
                                start=True, stop=True), 1)
                        te.wait_ge(ss.h, ss_sc0 + tt + 1)
                        for half in range(2):
                            tm.inc(te.matmul(
                                ps8[0:65, half * 512:half * 512 + HW],
                                t["v2e"][:, tt * 160 + 80 * j:tt * 160 + 80 * j + 65],
                                t["pbuf"][:, par * SCW + half * HW:par * SCW + (half + 1) * HW],
                                start=(tt == 0), stop=(tt == ST - 1)), 1)
                E.T(t_sc)
                tm_sc1 = tm.n

                def s_sc(s, sc=sc, tm_sc0=tm_sc0):
                    for tt in range(ST):
                        par = tt % 2
                        s.wait_ge(tm.h, tm_sc0 + 4 * tt + 2)
                        HW = SCW // 2
                        src_ap = ps8[:, 1024 + par * 1024:1024 + par * 1024 + 1024]
                        src_ap = src_ap.rearrange("p (h c) -> p h c", h=2)[:, :, 0:HW]
                        dst_ap = t["pbuf"][:, par * SCW:(par + 1) * SCW]
                        dst_ap = dst_ap.rearrange("p (h c) -> p h c", h=2)
                        ss.inc(s.activation(
                            out=dst_ap, in_=src_ap,
                            func=AF.Exp, scale=0.125), 1)
                E.S(s_sc)
                ss_sc1 = ss.n

                def s_num(s, sc=sc, tm_sc1=tm_sc1):
                    s.wait_ge(tm.h, tm_sc1)
                    HW = SCW // 2
                    nsrc = ps8[0:64, 0:1024].rearrange("p (h c) -> p h c", h=2)[:, :, 0:HW]
                    ndst = t["numb"][:, sc * SCW:(sc + 1) * SCW].rearrange(
                        "p (h c) -> p h c", h=2)
                    ss.inc(s.activation(out=ndst, in_=nsrc, func=AF.Copy), 1)
                E.S(s_num)

                def v_den(v, sc=sc, tm_sc1=tm_sc1):
                    v.wait_ge(tm.h, tm_sc1)
                    HW = SCW // 2
                    dsrc = ps8[64:65, 0:1024].rearrange("p (h c) -> p h c", h=2)[:, :, 0:HW]
                    ddst = t["denb"][0:1, sc * SCW:(sc + 1) * SCW].rearrange(
                        "p (h c) -> p h c", h=2)
                    vv.inc(v.tensor_copy(ddst, dsrc), 1)
                E.V(v_den)

            ss_unit = ss.n
            vv_unit = vv.n

            def sy_ret(sy, b=b, j=j, ss_unit=ss_unit, vv_unit=vv_unit):
                sy.wait_ge(ss.h, ss_unit)
                sy.wait_ge(vv.h, vv_unit)
                for half, r in enumerate((2 * b, 2 * b + 1)):
                    r0 = r * 130 + 65 * j
                    sl = slice(half * T, (half + 1) * T)
                    syq.inc(sy.dma_start(out=dd["a2a0ri"][r0:r0 + 64, :],
                                         in_=t["numb"][:, sl]), 16)
                    syq.inc(sy.dma_start(out=dd["a2a0ri"][r0 + 64:r0 + 65, :],
                                         in_=t["denb"][0:1, sl]), 16)
            E.Y(sy_ret)

        def g_a2a0r(g):
            g.wait_ge(syq.h, syq.n)
            cc.inc(g.collective_compute(
                "AllToAll", ALU.bypass, replica_groups=[list(range(8))],
                ins=[dd["a2a0ri"][:, :]], outs=[dd["a2a0ro"][:, :]]), 1)
        E.G(g_a2a0r)

        E.barrier()

        def g_l0w(g):
            for kc in range(8):
                gq.inc(g.dma_start(out=t["l0w_sb"][:, kc * D:(kc + 1) * D],
                                   in_=dd["l0_woutb"][kc * 128:(kc + 1) * 128, :]), 16)
            gq.inc(g.dma_start(out=t["l0w_sb"][0:1, 8 * D:9 * D],
                               in_=dd["l0_woutb"][D:D + 1, :]), 16)
        E.G(g_l0w)
        g_l0w_n = gq.n

        def sy_att(sy):
            for h in range(16):
                s_, jj = h % 8, h // 8
                kc, rr = h // 2, h % 2
                r0 = s_ * 130 + 65 * jj
                syq.inc(sy.dma_start(
                    out=t["att_fm"][64 * rr:64 * rr + 64, kc * T:(kc + 1) * T],
                    in_=dd["a2a0ro"][r0:r0 + 64, :]), 16)
        E.Y(sy_att)
        sy_att_n = syq.n

        def g_dens(g):
            gx.inc(g.dma_start(out=t["dens"][0:8, :],
                               in_=dd["a2a0ro"][64:8 * 130:130, :]), 16)
            gx.inc(g.dma_start(out=t["dens"][8:16, :],
                               in_=dd["a2a0ro"][129:8 * 130:130, :]), 16)
        E.G(g_dens)
        g_dens_n = gx.n

        def v_rd(v):
            v.wait_ge(gx.h, g_dens_n)
            vv.inc(v.tensor_scalar(t["dens"][:, :], t["dens"][:, :],
                                   9.1e-13, None, ALU.max), 1)
            v.wait_ge(vv.h, vv.n)
            vv.inc(v.reciprocal(t["dens"][:, :], t["dens"][:, :]), 1)
        E.V(v_rd)
        vv_rd = vv.n

        def sy_densp(sy):
            sy.wait_ge(vv.h, vv_rd)
            syq.inc(sy.dma_start(out=dd["densds"][:, :], in_=t["dens"][:, :]), 16)
        E.Y(sy_densp)
        sy_densp_n = syq.n

        sy_bc = {}
        for h in range(16):
            rsem = rde if h % 2 == 0 else rdo

            def sy_b(sy, h=h, rsem=rsem):
                par = h % 2
                if h == 0:
                    sy.wait_ge(syq.h, sy_densp_n)
                if h >= 2:
                    sy.wait_ge(vv.h, mul_vv[h - 2])
                rsem.inc(sy.dma_start(
                    out=t["rdbc"][:, par * T:(par + 1) * T],
                    in_=dd["densds"][h:h + 1, :].partition_broadcast(128)), 16)
            E.Y(sy_b)
            sy_bc[h] = rsem.n
            if h == 0:
                mul_vv = {}

            def v_mul(v, h=h):
                v.wait_ge((rde if h % 2 == 0 else rdo).h, sy_bc[h])
                par = h % 2
                kc, rr = h // 2, h % 2
                ap = t["att_fm"][64 * rr:64 * rr + 64, kc * T:(kc + 1) * T]
                vv.inc(v.tensor_tensor(
                    ap, ap,
                    t["rdbc"][64 * rr:64 * rr + 64, par * T:(par + 1) * T],
                    ALU.mult), 1)
            E.V(v_mul)
            mul_vv[h] = vv.n

        vv_att = vv.n

        for i in range(TT):
            par = i % 2
            tm_i0 = tm.n

            def t_op(te, i=i, par=par):
                if i == 0:
                    te.wait_ge(vv.h, vv_att)
                    te.wait_ge(gq.h, g_l0w_n)
                for oc in range(2):
                    bk = 2 * par + oc
                    if i >= 2:
                        te.wait_ge(vv.h, op_vv[i - 2])
                    for kc in range(8):
                        te.matmul(bank(bk),
                                  t["att_fm"][:, kc * T + i * 128:kc * T + (i + 1) * 128],
                                  t["l0w_sb"][:, kc * D + oc * 512:kc * D + (oc + 1) * 512],
                                  start=(kc == 0), stop=False)
                    tm.inc(te.matmul(bank(bk), t["ones_row"][0:1, 0:128],
                                     t["l0w_sb"][0:1, 8 * D + oc * 512:8 * D + (oc + 1) * 512],
                                     start=False, stop=True), 1)
            E.T(t_op)
            if i == 0:
                op_vv = {}

            def v_resid(v, i=i, par=par, tm_i0=tm_i0):
                for oc in range(2):
                    v.wait_ge(tm.h, tm_i0 + oc + 1)
                    vv.inc(v.tensor_tensor(
                        t["ln_r"][:, oc * 512:(oc + 1) * 512],
                        ps8[:, (2 * par + oc) * 512:(2 * par + oc + 1) * 512],
                        t["stream"][:, i * D + oc * 512:i * D + (oc + 1) * 512],
                        ALU.add), 1)
            E.V(v_resid)
            vv_r = vv.n

            emit_ln(E, [(vv.h, vv_r)],
                    t["stream"][:, i * D:(i + 1) * D],
                    out_f32_ap=t["ln_f"][:, :])

            def v_q(v, i=i):
                v.wait_ge(vv.h, vv.n)
                vv.inc(v.tensor_reduce(t["ln_a"][:, 0:1], t["ln_f"][:, :], AX,
                                       ALU.max, apply_absolute_value=True), 1)
                v.wait_ge(vv.h, vv.n)
                vv.inc(v.tensor_scalar(t["ln_a"][:, 0:1], t["ln_a"][:, 0:1],
                                       1e-12, None, ALU.max), 1)
                v.wait_ge(vv.h, vv.n)
                vv.inc(v.reciprocal(t["rqp"][:, 0:1], t["ln_a"][:, 0:1]), 1)
                vv.inc(v.tensor_scalar(t["lsp"][:, i:i + 1], t["ln_a"][:, 0:1],
                                       1.0 / 127.0, None, ALU.mult), 1)
                v.wait_ge(vv.h, vv.n)
                vv.inc(v.tensor_scalar(t["rqp"][:, 0:1], t["rqp"][:, 0:1],
                                       127.0, None, ALU.mult), 1)
                v.wait_ge(vv.h, vv.n)
                vv.inc(v.tensor_scalar(t["sq_scr"][:, :], t["ln_f"][:, :],
                                       t["rqp"][:, 0:1], MAGIC,
                                       ALU.mult, ALU.add), 1)
                v.wait_ge(vv.h, vv.n)
                vv.inc(v.tensor_scalar(t["xq_tm"][:, i * D:(i + 1) * D],
                                       t["sq_scr"][:, :], MAGIC, None,
                                       ALU.subtract), 1)
            E.V(v_q)
            op_vv[i] = vv.n
            vq_n = vv.n

            def sy_spill(sy, i=i, vq_n=vq_n):
                sy.wait_ge(vv.h, vq_n)
                syq.inc(sy.dma_start(out=dd["xqds"][i * 128:(i + 1) * 128, :],
                                     in_=t["xq_tm"][:, i * D:(i + 1) * D]), 16)
            E.Y(sy_spill)

        E.barrier()

        def sy_xq(sy):
            for kc in range(8):
                syq.inc(sy.dma_start(out=t["xq_fm"][:, kc * T:(kc + 1) * T],
                                     in_=dd["xqds"][:, kc * 128:(kc + 1) * 128],
                                     transpose=True), 16)
            for kc in range(8):
                syq.inc(sy.dma_start(out=t["rhsq_fm"][:, kc * D:(kc + 1) * D],
                                     in_=dd["rqds"][:, kc * 128:(kc + 1) * 128],
                                     transpose=True), 16)
        E.Y(sy_xq)
        sy_xq_n = syq.n

        for i in range(TT):
            par = i % 2
            tm_i0 = tm.n

            def t_f(te, i=i, par=par):
                if i == 0:
                    te.wait_ge(syq.h, sy_xq_n)
                for oc in range(2):
                    bk = 2 * par + oc
                    if i >= 2:
                        te.wait_ge(vv.h, f_vv[i - 2])
                    for kc in range(8):
                        ins = te.matmul(
                            bank(bk),
                            t["xq_fm"][:, kc * T + i * 128:kc * T + (i + 1) * 128],
                            t["rhsq_fm"][:, kc * D + oc * 512:kc * D + (oc + 1) * 512],
                            start=(kc == 0), stop=(kc == 7))
                    tm.inc(ins, 1)
            E.T(t_f)
            if i == 0:
                f_vv = {}

            def v_f(v, i=i, par=par, tm_i0=tm_i0):
                for oc in range(2):
                    v.wait_ge(tm.h, tm_i0 + oc + 1)
                    vv.inc(v.scalar_tensor_tensor(
                        t["sq_scr"][:, oc * 512:(oc + 1) * 512],
                        ps8[:, (2 * par + oc) * 512:(2 * par + oc + 1) * 512],
                        t["lsp"][:, i:i + 1],
                        t["rsw_bc"][:, oc * 512:(oc + 1) * 512],
                        ALU.mult, ALU.mult), 1)
                    v.wait_ge(vv.h, vv.n)
                    vv.inc(v.tensor_tensor(
                        t["ln_r"][:, oc * 512:(oc + 1) * 512],
                        t["sq_scr"][:, oc * 512:(oc + 1) * 512],
                        t["stream"][:, i * D + oc * 512:i * D + (oc + 1) * 512],
                        ALU.add), 1)
            E.V(v_f)
            vv_r = vv.n

            if cfg.n_layers == 1:
                emit_ln(E, [(vv.h, vv_r)],
                        t["stream"][:, i * D:(i + 1) * D],
                        out_f32_ap=t["yout"][:, par * D:(par + 1) * D])
                emit_ystore(E, i, par)
            else:
                emit_ln(E, [(vv.h, vv_r)],
                        t["stream"][:, i * D:(i + 1) * D])
            f_vv[i] = vv.n

        if cfg.n_layers >= 2:
            E.barrier()
            emit_l1(E, bank)

        def g_fin(g):
            g.wait_ge(syq.h, syq.n)
            g.wait_ge(scq.h, scq.n)
            g.wait_ge(gq.h, gq.n)
        E.G(g_fin)

    E.ctx.close()
    return E




def emit_l1(E, bank):
    cfg = E.cfg
    S, D, T, TT, TC, ST = cfg.S, cfg.D, cfg.T, cfg.TT, cfg.TC, cfg.ST
    W = cfg.W
    t, dd = E.t, E.d
    tm, vv, ss, gq, syq, scq, cc = E.tm, E.vv, E.ss, E.gq, E.syq, E.scq, E.cc
    gx, sye, syo, rde, rdo = E.gx, E.sye, E.syo, E.rde, E.rdo
    ps8 = t["ps8"]
    ABa, AFa = E.ABa, E.AFa
    NH = 16
    R1 = 1538
    LNSQ = float(np.log(np.sqrt(512.0)))

    def ar(name, parts, n, fr=None):
        t[name] = (fr or ABa).alloc(parts, n)

    ABa.reset()
    AFa.reset()
    ar("x_fm", 128, 8 * T)
    ar("wup_sb", 128, 8 * 2 * D)
    ar("stg", 128, 2 * T)
    sy_fm = emit_stream_to_fm(E, t["x_fm"])

    b1_evac = {}
    b1_st = {}
    for half in range(2):
        def g_w(g, half=half):
            if half == 1:
                g.wait_ge(tm.h, tm.n)
            for kc in range(8):
                gq.inc(g.dma_start(
                    out=t["wup_sb"][:, kc * 2 * D:(kc + 1) * 2 * D],
                    in_=dd["w_upT"][kc * 128:(kc + 1) * 128,
                                    half * 2 * D:(half + 1) * 2 * D]), 16)
        E.G(g_w)
        g_w_n = gq.n

        for mc in range(16):
            omc = half * 16 + mc
            par = omc % 2

            def t_mm(te, mc=mc, par=par, omc=omc, g_w_n=g_w_n, half=half):
                if mc == 0:
                    te.wait_ge(gq.h, g_w_n)
                    if half == 0:
                        te.wait_ge(syq.h, sy_fm)
                if omc >= 2:
                    te.wait_ge(ss.h, b1_evac[omc - 2])
                for kc in range(8):
                    for tc in range(TC):
                        ins = te.matmul(
                            bank(par * TC + tc)[:, 0:W],
                            t["wup_sb"][:, kc * 2 * D + mc * 128:kc * 2 * D + (mc + 1) * 128],
                            t["x_fm"][:, kc * T + tc * W:kc * T + (tc + 1) * W],
                            start=(kc == 0), stop=(kc == 7))
                        if kc == 7:
                            tm.inc(ins, 1)
            E.T(t_mm)
            tm_after = tm.n

            def s_ev(s, par=par, omc=omc, tm_after=tm_after):
                if omc >= 2:
                    s.wait_ge((sye if omc % 2 == 0 else syo).h, b1_st[omc - 2])
                for tc in range(TC):
                    s.wait_ge(tm.h, tm_after - TC + tc + 1)
                    ss.inc(s.activation(
                        out=t["stg"][:, par * T + tc * W:par * T + (tc + 1) * W],
                        in_=bank(par * TC + tc)[:, 0:W], func=AF.Copy), 1)
            E.S(s_ev)
            b1_evac[omc] = ss.n
            psem = sye if par == 0 else syo

            def sy_st(sy, mc=mc, par=par, half=half, ssn=ss.n, psem=psem):
                sy.wait_ge(ss.h, ssn)
                dst = dd["xmds"] if half == 0 else dd["zds"]
                psem.inc(sy.dma_start(
                    out=dst[mc * 128:(mc + 1) * 128, :],
                    in_=t["stg"][:, par * T:par * T + T]), 16)
                if half == 0:
                    psem.inc(sy.dma_start(
                        out=dd["halo_in"][((mc + 8) % 16) * 128:
                                          ((mc + 8) % 16 + 1) * 128, 0:4],
                        in_=t["stg"][:, par * T + T - 4:par * T + T]), 16)
            E.Y(sy_st)
            b1_st[omc] = psem.n

    def g_hag(g):
        g.wait_ge(sye.h, sye.n)
        g.wait_ge(syo.h, syo.n)
        cc.inc(g.collective_compute(
            "AllGather", ALU.bypass,
            replica_groups=[[0, 1], [2, 3], [4, 5], [6, 7]],
            ins=[dd["halo_in"][:, :]], outs=[dd["halo_out"][:, :]]), 1)
    E.G(g_hag)

    E.barrier()

    ABa.reset()
    AFa.reset()
    ar("cdiag", 128, 64 * 128)
    ar("hwq", 128, NH * 128)
    ar("hwk", 128, NH * 128)
    ar("hwv", 128, NH * 128)
    ar("hwT", 128, 3 * NH * 128)
    ar("wig8", 128, 3 * 128)
    ar("weff", 128, 2 * NH * 8)
    ar("big8s", 1, 8)
    ar("xmh", 128, 64 + T)
    ar("xcb", 128, T)
    ar("stg2", 128, 3 * T)
    ar("igfg", 8, T)
    ar("halo_sb", 128, 64)
    ar("cbv", 128, 16, AFa)
    ar("hmask", 128, 1, AFa)

    def g_l1b(g):
        gq.inc(g.dma_start(out=t["cdiag"][:, :], in_=dd["conv_diag_h"][:, :]), 16)
        for nm, dst in (("hw_q", "hwq"), ("hw_k", "hwk"), ("hw_v", "hwv")):
            gq.inc(g.dma_start(out=t[dst][:, :], in_=dd[nm][:, :]), 16)
        for i2, nm in enumerate(("hw_qT", "hw_kT", "hw_vT")):
            gq.inc(g.dma_start(out=t["hwT"][:, i2 * NH * 128:(i2 + 1) * NH * 128],
                               in_=dd[nm][:, :]), 16)
        for i2, nm in enumerate(("wig_q8", "wig_k8", "wig_v8")):
            gq.inc(g.dma_start(out=t["wig8"][:, i2 * 128:(i2 + 1) * 128],
                               in_=dd[nm][:, :]), 16)
        gq.inc(g.dma_start(out=t["big8s"][:, :], in_=dd["big8"][:, :]), 16)
        gq.inc(g.dma_start(out=t["cbv"][:, :], in_=dd["conv_b_pm"][:, :]), 16)
        gq.inc(g.dma_start(out=t["hmask"][:, :], in_=dd["halo_mask"][:, :]), 16)
    E.G(g_l1b)
    g_l1b_n = gq.n

    tm_we0 = tm.n
    we_vv = {jc: vv.n + 2 * (jc + 1) for jc in range(NH)}

    def t_weff(te):
        te.wait_ge(gq.h, g_l1b_n)
        for jc in range(NH):
            b = jc % 2
            if jc >= 2:
                te.wait_ge(vv.h, we_vv[jc - 2])
            te.matmul(bank(b)[:, 0:8],
                      t["hwT"][:, jc * 128:(jc + 1) * 128],
                      t["wig8"][:, jc * 8:jc * 8 + 8],
                      start=True, stop=False)
            tm.inc(te.matmul(bank(b)[:, 0:8],
                             t["hwT"][:, NH * 128 + jc * 128:NH * 128 + (jc + 1) * 128],
                             t["wig8"][:, 128 + jc * 8:128 + jc * 8 + 8],
                             start=False, stop=True), 1)
            tm.inc(te.matmul(bank(2 + b)[:, 0:8],
                             t["hwT"][:, 2 * NH * 128 + jc * 128:2 * NH * 128 + (jc + 1) * 128],
                             t["wig8"][:, 2 * 128 + jc * 8:2 * 128 + jc * 8 + 8],
                             start=True, stop=True), 1)
    E.T(t_weff)

    def v_weff(v):
        for jc in range(NH):
            b = jc % 2
            v.wait_ge(tm.h, tm_we0 + 2 * jc + 2)
            vv.inc(v.tensor_copy(t["weff"][:, jc * 8:(jc + 1) * 8],
                                 bank(b)[:, 0:8]), 1)
            vv.inc(v.tensor_copy(t["weff"][:, NH * 8 + jc * 8:NH * 8 + (jc + 1) * 8],
                                 bank(2 + b)[:, 0:8]), 1)
            assert vv.n == we_vv[jc]
    E.V(v_weff)
    vv_weff = vv.n

    pidx_sb = E.sb("pidx_sb", [1, 2], mybir.dt.uint32)

    def g_hoff(g):
        gx.inc(g.dma_start(out=pidx_sb[0:1, 0:1], in_=dd["pidx"][:, :]), 16)
        gx.inc(g.dma_start(out=pidx_sb[0:1, 1:2], in_=dd["opidx"][:, :]), 16)
        g.wait_ge(cc.h, cc.n)
        gx.inc(g.dma_start(
            out=t["halo_sb"].rearrange("p (c f) -> p c f", f=4),
            in_=dd["halo_out"][0:2 * D, 0:4]
                .rearrange("(c p) f -> p c f", p=128)), 16)
    E.G(g_hoff)
    g_hoff_n = gx.n

    def t_igb(te):
        te.wait_ge(vv.h, vv_weff)
        te.wait_ge(gq.h, g_l1b_n)
        for tc in range(TC):
            te.matmul(ps8[0:8, (4 + tc) * 512:(4 + tc) * 512 + W],
                      t["big8s"][0:1, 0:8],
                      t["ones_row"][0:1, 0:W],
                      start=True, stop=False)
    E.T(t_igb)

    b3_used = {}
    hw_ev = {}
    tm_ig_last = None
    for mc in range(NH):
        def g_xm(g, mc=mc):
            if mc == 0:
                g.wait_ge(gx.h, g_hoff_n)
            if mc >= 1:
                g.wait_ge(tm.h, b3_used[mc - 1])
            gx.inc(g.dma_start(out=t["xmh"][:, 64:64 + T],
                               in_=dd["xmds"][mc * 128:(mc + 1) * 128, :]), 16)
        E.G(g_xm)
        g_xm_n = gx.n

        def v_hm(v, mc=mc, g_xm_n=g_xm_n):
            v.wait_ge(gx.h, g_xm_n)
            if mc == 0:
                v.wait_ge(vv.h, vv_weff)
            vv.inc(v.tensor_scalar(t["xmh"][:, 61:64],
                                   t["halo_sb"][:, mc * 4 + 1:mc * 4 + 4],
                                   t["hmask"][:, 0:1], None, ALU.mult), 1)
        E.V(v_hm)
        vv_hm = vv.n

        tmc0 = tm.n
        cv_ev = {tc: ss.n + tc + 1 for tc in range(TC)}

        def t_cv(te, mc=mc, vv_hm=vv_hm):
            te.wait_ge(vv.h, vv_hm)
            if mc >= 1:
                te.wait_ge((sye if (mc - 1) % 2 == 0 else syo).h, xc_sp[mc - 1])
            for tc in range(TC):
                b = tc % 2
                if tc >= 2:
                    te.wait_ge(ss.h, cv_ev[tc - 2])
                for j in range(4):
                    ins = te.matmul(
                        bank(b)[:, 0:W],
                        t["cdiag"][:, (j * NH + mc) * 128:(j * NH + mc + 1) * 128],
                        t["xmh"][:, 61 + j + tc * W:61 + j + tc * W + W],
                        start=(j == 0), stop=(j == 3))
                tm.inc(ins, 1)
        E.T(t_cv)
        tmc1 = tm.n

        def s_cv(s, mc=mc, tmc0=tmc0):
            if mc >= 1:
                s.wait_ge(tm.h, hw_done[mc - 1])
            for tc in range(TC):
                s.wait_ge(tm.h, tmc0 + tc + 1)
                ss.inc(s.activation(out=t["xcb"][:, tc * W:(tc + 1) * W],
                                    in_=bank(tc % 2)[:, 0:W], func=AF.Silu,
                                    bias=t["cbv"][:, mc:mc + 1]), 1)
                assert cv_ev[tc] == ss.n
        E.S(s_cv)
        ss_cv = ss.n
        psem = sye if mc % 2 == 0 else syo

        def sy_xc(sy, mc=mc, ss_cv=ss_cv, psem=psem):
            sy.wait_ge(ss.h, ss_cv)
            psem.inc(sy.dma_start(out=dd["xcds"][mc * 128:(mc + 1) * 128, :],
                                  in_=t["xcb"][:, :]), 16)
        E.Y(sy_xc)
        if mc == 0:
            xc_sp = {}
        xc_sp[mc] = psem.n

        tmh0 = tm.n
        hw_ev = {k: ss.n + k + 1 for k in range(3 * TC)}

        def t_hw(te, mc=mc, ss_cv=ss_cv, hw_ev=hw_ev):
            te.wait_ge(ss.h, ss_cv)
            for tc in range(TC):
                te.matmul(ps8[0:8, (4 + tc) * 512:(4 + tc) * 512 + W],
                          t["weff"][:, mc * 8:(mc + 1) * 8],
                          t["xcb"][:, tc * W:(tc + 1) * W],
                          start=False, stop=False)
                te.matmul(ps8[0:8, (4 + tc) * 512:(4 + tc) * 512 + W],
                          t["weff"][:, NH * 8 + mc * 8:NH * 8 + (mc + 1) * 8],
                          t["xmh"][:, 64 + tc * W:64 + (tc + 1) * W],
                          start=False, stop=(mc == NH - 1))
            for pi in range(3):
                for tc in range(TC):
                    k = pi * TC + tc
                    b = 2 + k % 2
                    if k >= 2:
                        te.wait_ge(ss.h, hw_ev[k - 2])
                    wsb = ("hwq", "hwk", "hwv")[pi]
                    mov = (t["xcb"][:, tc * W:(tc + 1) * W] if pi < 2
                           else t["xmh"][:, 64 + tc * W:64 + (tc + 1) * W])
                    tm.inc(te.matmul(
                        bank(b)[:, 0:W],
                        t[wsb][:, mc * 128:(mc + 1) * 128],
                        mov, start=True, stop=True), 1)
        E.T(t_hw)
        if mc == NH - 1:
            tm_ig_last = tm.n
        if mc == 0:
            hw_done = {}
        hw_done[mc] = tm.n

        def s_hw(s, mc=mc, tmh0=tmh0, hw_ev=hw_ev):
            for k in range(3 * TC):
                s.wait_ge(tm.h, tmh0 + k + 1)
                pi, tc = k // TC, k % TC
                if mc >= 1 and k < 2:
                    s.wait_ge(rde.h, sl_st.get(mc - 1, (0, 0))[0])
                    s.wait_ge(rdo.h, sl_st.get(mc - 1, (0, 0))[1])
                ss.inc(s.activation(
                    out=t["stg2"][:, pi * T + tc * W:pi * T + (tc + 1) * W],
                    in_=bank(2 + k % 2)[:, 0:W], func=AF.Copy), 1)
                assert ss.n == hw_ev[k]
        E.S(s_hw)
        ss_hw = ss.n
        b3_used[mc] = hw_done[mc]

        def sy_sl(sy, mc=mc, ssn=ss.n):
            sy.wait_ge(ss.h, ssn)
            head, ro = mc // 4, (mc % 4) * 128
            dst = dd["keepst"] if head < 2 else dd["agin"]
            u2 = head % 2
            for pi in range(3):
                (rde if pi % 2 == 0 else rdo).inc(sy.dma_start(
                    out=dst[u2 * R1 + pi * 512 + ro:u2 * R1 + pi * 512 + ro + 128, :],
                    in_=t["stg2"][:, pi * T:pi * T + T]), 16)
        E.Y(sy_sl)
        if mc == 0:
            sl_st = {}
        sl_st[mc] = (rde.n, rdo.n)

    def v_ig(v):
        v.wait_ge(tm.h, tm_ig_last)
        for tc in range(TC):
            vv.inc(v.tensor_copy(t["igfg"][:, tc * W:(tc + 1) * W],
                                 ps8[0:8, (4 + tc) * 512:(4 + tc) * 512 + W]), 1)
    E.V(v_ig)
    vv_ig = vv.n

    def sy_ig(sy):
        sy.wait_ge(vv.h, vv_ig)
        for u in range(2):
            for gr, row in ((0, u), (1, 4 + u)):
                syq.inc(sy.dma_start(
                    out=dd["keepst"][u * R1 + 1536 + gr:u * R1 + 1536 + gr + 1, :],
                    in_=t["igfg"][row:row + 1, :]), 16)
            for gr, row in ((0, 2 + u), (1, 6 + u)):
                syq.inc(sy.dma_start(
                    out=dd["agin"][u * R1 + 1536 + gr:u * R1 + 1536 + gr + 1, :],
                    in_=t["igfg"][row:row + 1, :]), 16)
    E.Y(sy_ig)

    def g_asm(g):
        g.wait_ge(rde.h, rde.n)
        g.wait_ge(rdo.h, rdo.n)
        g.wait_ge(syq.h, syq.n)
        cc.inc(g.collective_compute(
            "AllGather", ALU.bypass,
            replica_groups=[[0, 1], [2, 3], [4, 5], [6, 7]],
            ins=[dd["agin"][:, :]], outs=[dd["agout"][:, :]]), 1)
        view = dd["a2a1o"].rearrange("(u s r) t -> u s r t", u=2, s=2)
        with g.register("l1keep") as hr:
            g.reg_load(hr, pidx_sb[0:1, 0:1])
            off = g.snap(hr)
            gx.inc(g.dma_start(
                out=view[:, bass.ds(off, 1), :, :],
                in_=dd["keepst"].rearrange("(u o r) t -> u o r t",
                                           u=2, o=1)), 16)
        g.wait_ge(cc.h, cc.n)
        g.wait_ge(gx.h, gx.n)
        with g.register("l1part") as hr2:
            g.reg_load(hr2, pidx_sb[0:1, 1:2])
            off2 = g.snap(hr2)
            gx.inc(g.dma_start(
                out=view[:, bass.ds(off2, 1), :, :],
                in_=dd["agout"].rearrange("(gg u r) t -> u gg r t",
                                          gg=2, u=2)[:, bass.ds(off2, 1), :, :]),
                16)
    E.G(g_asm)
    gx_asm = gx.n

    E.barrier()

    emit_l1_units(E, bank, gx_asm)

    def g_ret(g):
        g.wait_ge(syq.h, syq.n)
        g.wait_ge(rde.h, rde.n)
        g.wait_ge(rdo.h, rdo.n)
        riv = dd["a2a1ri"].rearrange("(u s t) d -> u s t d", u=2, s=2)
        with g.register("l1rsend") as hr:
            g.reg_load(hr, pidx_sb[0:1, 1:2])
            off = g.snap(hr)
            gx.inc(g.dma_start(
                out=dd["ri_in"].rearrange("(u o t) d -> u o t d", u=2, o=1),
                in_=riv[:, bass.ds(off, 1), :, :]), 16)
        with g.register("l1rloc") as hr2:
            g.reg_load(hr2, pidx_sb[0:1, 0:1])
            off2 = g.snap(hr2)
            gx.inc(g.dma_start(
                out=dd["hret"][0:2 * T, :].rearrange("(u o t) d -> u o t d",
                                                     u=2, o=1),
                in_=riv[:, bass.ds(off2, 1), :, :]), 16)
        g.wait_ge(gx.h, gx.n)
        cc.inc(g.collective_compute(
            "AllGather", ALU.bypass,
            replica_groups=[[0, 1], [2, 3], [4, 5], [6, 7]],
            ins=[dd["ri_in"][:, :]], outs=[dd["ri_out"][:, :]]), 1)
        g.wait_ge(cc.h, cc.n)
        with g.register("l1rpart") as hr3:
            g.reg_load(hr3, pidx_sb[0:1, 1:2])
            off3 = g.snap(hr3)
            gx.inc(g.dma_start(
                out=dd["hret"][2 * T:4 * T, :]
                    .rearrange("(o t) d -> o t d", o=1),
                in_=dd["ri_out"].rearrange("(gg t) d -> gg t d",
                                           gg=2)[bass.ds(off3, 1), :, :]), 16)
    E.G(g_ret)

    E.barrier()

    emit_l1_post(E, bank)

    E.barrier()

    emit_l1_ffn(E, bank)

    if cfg.n_layers >= 3:
        E.barrier()
        emit_l2(E, bank)


def emit_l1_units(E, bank, gx_asm):
    cfg = E.cfg
    S, D, T, TT, TC, ST = cfg.S, cfg.D, cfg.T, cfg.TT, cfg.TC, cfg.ST
    t, dd = E.t, E.d
    tm, vv, ss, gq, syq, scq, cc = E.tm, E.vv, E.ss, E.gq, E.syq, E.scq, E.cc
    gx, sye, syo, rde, rdo = E.gx, E.sye, E.syo, E.rde, E.rdo
    ps8 = t["ps8"]
    ABa, AFa = E.ABa, E.AFa
    R1 = 1538
    NF = 32
    NP = S // NF
    LNSQ = float(np.log(np.sqrt(512.0)))
    NSC = S // 512

    def ar(name, parts, n, fr=None):
        t[name] = (fr or ABa).alloc(parts, n)

    ABa.reset()
    AFa.reset()
    ar("k_fm", 128, 4 * S)
    ar("v_e", 128, ST * 512)
    ar("q_sc", 128, 2 * 2048)
    ar("dbuf", 128, 512)
    ar("cbuf", 128, 512)
    ar("hbuf", 128, 512)
    ar("hbuf2", 128, 512)
    ar("masksb", 128, 4 * 512)
    ar("lnwbc", 128, 2 * 512)
    ar("nft", 128, ST)
    ar("g3", 2, S, AFa)
    ar("g3b", 2, 2 * 512, AFa)
    ar("gw0", NP, NF, AFa)
    ar("gw1", NP, NF, AFa)
    ar("gw2", NP, NF, AFa)
    ar("gw3", NP, NF, AFa)
    ar("gpx", 1, 128, AFa)
    ar("gpx2", 1, 128, AFa)
    ar("gpx3", 1, 128, AFa)
    ar("rd4", 128, 4, AFa)
    ar("gnf", NP, NF, AFa)
    ar("gclamp", 128, 512, AFa)

    def g_msk(g):
        gq.inc(g.dma_start(out=t["masksb"][:, :], in_=dd["masks4"][:, :]), 16)
        for u in range(2):
            gq.inc(g.dma_start(
                out=t["lnwbc"][:, u * 512:(u + 1) * 512],
                in_=dd["lnw_u"][u:u + 1, :].partition_broadcast(128)), 16)
    E.G(g_msk)
    g_msk_n = gq.n

    def vscan(v, a, b, alu):
        for sh in (1, 2, 4, 8, 16):
            if sh >= NF:
                break
            vv.inc(v.tensor_copy(t[b][:, 0:sh], t[a][:, 0:sh]), 1)
            vv.inc(v.tensor_tensor(t[b][:, sh:NF], t[a][:, sh:NF],
                                   t[a][:, 0:NF - sh], alu), 1)
            v.wait_ge(vv.h, vv.n)
            a, b = b, a
        return a

    def cross_prefix(which, alu, neutral):
        gsx = E.gsx

        def sy_a(sy):
            sy.wait_ge(vv.h, vv.n)
            gsx.inc(sy.dma_start(
                out=dd["gds"].rearrange("(a o) -> a o", o=1)[0:NP, 0:1],
                in_=t[which][:, NF - 1:NF]), 16)
            sy.wait_ge(gsx.h, gsx.n)
            gsx.inc(sy.dma_start(out=t["gpx"][0:1, 0:NP],
                                 in_=dd["gds"][None, 0:NP]), 16)
        E.Y(sy_a)
        sy_a_n = gsx.n

        def v_b(v):
            v.wait_ge(gsx.h, sy_a_n)
            a, b = "gpx", "gpx2"
            for sh in (1, 2, 4, 8, 16, 32, 64):
                if sh >= NP:
                    break
                vv.inc(v.tensor_copy(t[b][0:1, 0:sh], t[a][0:1, 0:sh]), 1)
                vv.inc(v.tensor_tensor(t[b][0:1, sh:NP], t[a][0:1, sh:NP],
                                       t[a][0:1, 0:NP - sh], alu), 1)
                v.wait_ge(vv.h, vv.n)
                a, b = b, a
            vv.inc(v.memset(t["gpx3"][0:1, 0:1], neutral), 1)
            vv.inc(v.tensor_copy(t["gpx3"][0:1, 1:NP], t[a][0:1, 0:NP - 1]), 1)
            v.wait_ge(vv.h, vv.n)
        E.V(v_b)

        def sy_c(sy):
            sy.wait_ge(vv.h, vv.n)
            gsx.inc(sy.dma_start(out=dd["gds"][None, 0:NP],
                                 in_=t["gpx3"][0:1, 0:NP]), 16)
            sy.wait_ge(gsx.h, gsx.n)
            gsx.inc(sy.dma_start(
                out=t["gw3"][:, 0:1],
                in_=dd["gds"].rearrange("(a o) -> a o", o=1)[0:NP, 0:1]), 16)
        E.Y(sy_c)
        return gsx.n

    for u in range(2):
        slots = (2 * u, 2 * u + 1)

        def sy_k(sy, u=u, slots=slots):
            sy.wait_ge(gx.h, gx_asm)
            if u == 1:
                sy.wait_ge(ss.h, ss.n)
                sy.wait_ge(tm.h, tm.n)
                sy.wait_ge(vv.h, vv.n)
            for dc in range(4):
                for half, r in enumerate(slots):
                    E.kq.inc(sy.dma_start(
                        out=t["k_fm"][:, dc * S + half * T:dc * S + half * T + T],
                        in_=dd["a2a1o"][r * R1 + 512 + dc * 128:r * R1 + 512 + (dc + 1) * 128, :]), 16)
            for tt in range(ST):
                half = (tt * 128) // T
                r = slots[half]
                c0 = (tt * 128) % T
                for dc in range(4):
                    E.vtq.inc(sy.dma_start(
                        out=t["v_e"][:, tt * 512 + dc * 128:tt * 512 + (dc + 1) * 128],
                        in_=dd["a2a1o"][r * R1 + 1024 + dc * 128:r * R1 + 1024 + (dc + 1) * 128,
                                        c0:c0 + 128],
                        transpose=True), 16)
        E.Y(sy_k)
        sy_k_n = E.kq.n
        vt_n = E.vtq.n

        def g_gates(g, u=u, slots=slots):
            if u == 0:
                g.wait_ge(gq.h, g_msk_n)
            else:
                g.wait_ge(vv.h, vv.n)
            for gr, ws in ((0, "gw0"), (1, "gw1")):
                for half, r in enumerate(slots):
                    gx.inc(g.dma_start(
                        out=t[ws][(NP // 2) * half:(NP // 2) * (half + 1), :],
                        in_=dd["a2a1o"][r * R1 + 1536 + gr:r * R1 + 1537 + gr, :]
                            .rearrange("o (p f) -> (o p) f", f=NF)), 16)
        E.G(g_gates)
        g_g_n = gx.n

        def s_ls(s, g_g_n=g_g_n):
            s.wait_ge(gx.h, g_g_n)
            ss.inc(s.activation(out=t["gw2"][:, :], in_=t["gw1"][:, :],
                                func=AF.Abs), 1)
            s.wait_ge(ss.h, ss.n)
            ss.inc(s.activation(out=t["gw2"][:, :], in_=t["gw2"][:, :],
                                func=AF.Exp, scale=-1.0), 1)
            s.wait_ge(ss.h, ss.n)
            ss.inc(s.activation(out=t["gw2"][:, :], in_=t["gw2"][:, :],
                                func=AF.Ln, bias=1.0), 1)
        E.S(s_ls)
        ss_ls = ss.n

        cs_name = [None]

        def v_lf(v):
            v.wait_ge(ss.h, ss_ls)
            vv.inc(v.scalar_tensor_tensor(t["gw1"][:, :], t["gw1"][:, :], 0.0,
                                          t["gw2"][:, :], ALU.min, ALU.subtract), 1)
            v.wait_ge(vv.h, vv.n)
            cs_name[0] = vscan(v, "gw1", "gw2", ALU.add)
        E.V(v_lf)
        spx = cross_prefix(cs_name[0], ALU.add, 0.0)
        cm_name = [None]

        def v_lc(v, spx=spx):
            v.wait_ge(E.gsx.h, spx)
            lcn = cs_name[0]
            vv.inc(v.tensor_scalar(t[lcn][:, :], t[lcn][:, :],
                                   t["gw3"][:, 0:1], None, ALU.add), 1)
            v.wait_ge(vv.h, vv.n)
            vv.inc(v.tensor_tensor(t["gw0"][:, :], t["gw0"][:, :],
                                   t[lcn][:, :], ALU.subtract), 1)
            v.wait_ge(vv.h, vv.n)
            oth = "gw2" if lcn == "gw1" else "gw1"
            vv.inc(v.tensor_copy(t[oth][:, :], t["gw0"][:, :]), 1)
            v.wait_ge(vv.h, vv.n)
            cm_name[0] = vscan(v, oth, "gw3", ALU.max)
            if cm_name[0] == "gw3":
                vv.inc(v.tensor_copy(t[oth][:, :], t["gw3"][:, :]), 1)
                v.wait_ge(vv.h, vv.n)
                cm_name[0] = oth
        E.V(v_lc)
        spx2 = cross_prefix(cm_name[0], ALU.max, -1e30)

        def v_m(v, spx2=spx2):
            v.wait_ge(E.gsx.h, spx2)
            cmn = cm_name[0]
            lcn = cs_name[0]
            vv.inc(v.tensor_scalar(t[cmn][:, :], t[cmn][:, :],
                                   t["gw3"][:, 0:1], None, ALU.max), 1)
            v.wait_ge(vv.h, vv.n)
            vv.inc(v.tensor_tensor(t["gnf"][:, :], t[lcn][:, :],
                                   t[cmn][:, :], ALU.add), 1)
            v.wait_ge(vv.h, vv.n)
            vv.inc(v.tensor_scalar(t[cmn][:, :], t[cmn][:, :],
                                   LNSQ, -1.0, ALU.add, ALU.mult), 1)
        E.V(v_m)
        vv_m = vv.n

        def s_nf(s):
            s.wait_ge(vv.h, vv_m)
            ss.inc(s.activation(out=t["gnf"][:, :], in_=t["gnf"][:, :],
                                func=AF.Exp, scale=-1.0), 1)
        E.S(s_nf)
        ss_nf = ss.n

        tm_pre = tm.n
        gsx_pre = E.gsx.n

        def v_pre(v, u=u, tm_pre=tm_pre, gsx_pre=gsx_pre):
            if u == 1:
                v.wait_ge(tm.h, tm_pre)
                v.wait_ge(E.gsx.h, gsx_pre)
            vv.inc(v.memset(t["g3"][0:2, :], 1.0), 1)
            vv.inc(v.memset(t["g3b"][0:2, :], 1.0), 1)
        E.V(v_pre)
        vv_pre = vv.n

        def sy_exp(sy):
            gsx = E.gsx
            sy.wait_ge(ss.h, ss_nf)
            gsx.inc(sy.dma_start(
                out=dd["gds"][0:S].rearrange("(p f) -> p f", f=NF),
                in_=t["gw0"][:, :]), 16)
            gsx.inc(sy.dma_start(
                out=dd["gds"][S:2 * S].rearrange("(p f) -> p f", f=NF),
                in_=t[cm_name[0]][:, :]), 16)
            sy.wait_ge(gsx.h, gsx.n)
            sy.wait_ge(vv.h, vv_pre)
            gsx.inc(sy.dma_start(out=t["g3"][0:1, :], in_=dd["gds"][None, 0:S]), 16)
        E.Y(sy_exp)
        sy_exp_n = E.gsx.n

        def g_nft(g, u=u):
            g.wait_ge(ss.h, ss_nf)
            gx.inc(g.dma_start(
                out=dd["nfds"].rearrange("(p f) -> p f", f=NF),
                in_=t["gnf"][:, :]), 16)
            g.wait_ge(gx.h, gx.n)
            for a in range(ST):
                gx.inc(g.dma_start(
                    out=t["nft"][:, a:a + 1],
                    in_=dd["nfds"][a * 128:(a + 1) * 128]
                        .rearrange("(p o) -> p o", o=1)), 16)
        E.G(g_nft)
        g_nft_n = gx.n



        q_used = {}
        for sc in range(NSC):
            NTT = 4 * sc + 4

            qsem = E.qpe if sc % 2 == 0 else E.qpo

            def sy_q(sy, sc=sc, slots=slots, qsem=qsem):
                par = sc % 2
                if sc < 2:
                    sy.wait_ge(E.gsx.h, sy_exp_n)
                if sc >= 2:
                    sy.wait_ge(tm.h, q_used[sc - 2])
                qsem.inc(sy.dma_start(
                    out=t["g3b"][1:2, par * 512:(par + 1) * 512],
                    in_=dd["gds"][None, S + sc * 512:S + (sc + 1) * 512]), 16)
                seg = 0
                while seg < 512:
                    gtok = sc * 512 + seg
                    half = gtok // T
                    r = slots[half]
                    c0 = gtok % T
                    ln = min(512 - seg, T - c0)
                    for dc in range(4):
                        qsem.inc(sy.dma_start(
                            out=t["q_sc"][:, par * 2048 + dc * 512 + seg:
                                          par * 2048 + dc * 512 + seg + ln],
                            in_=dd["a2a1o"][r * R1 + dc * 128:r * R1 + (dc + 1) * 128,
                                            c0:c0 + ln]), 16)
                    seg += ln
            E.Y(sy_q)
            sy_q_n = qsem.n

            tm_sc0 = tm.n
            ss_sc0 = ss.n
            vv_sc0 = vv.n
            vcnt = []
            tot = 0
            for tt in range(NTT):
                tot += 3 if tt >= 4 * sc else 1
                vcnt.append(tot)

            def t_sc(te, sc=sc, NTT=NTT, sy_q_n=sy_q_n, tm_sc0=tm_sc0,
                     ss_sc0=ss_sc0, vv_sc0=vv_sc0, vcnt=vcnt, qsem=qsem):
                par = sc % 2
                if sc == 0:
                    te.wait_ge(E.kq.h, sy_k_n)
                    te.wait_ge(E.vtq.h, vt_n)
                    te.wait_ge(E.gsx.h, sy_exp_n)
                    te.wait_ge(gx.h, g_nft_n)
                te.wait_ge(qsem.h, sy_q_n)
                if sc >= 1:
                    te.wait_ge(vv.h, vv_sc0)
                    te.wait_ge(ss.h, ss_sc0)
                for tt in range(NTT):
                    sb = 5 + tt % 2
                    if tt >= 2:
                        te.wait_ge(vv.h, vv_sc0 + vcnt[tt - 2])
                    for dc in range(4):
                        ins = te.matmul(
                            bank(sb),
                            t["k_fm"][:, dc * S + tt * 128:dc * S + (tt + 1) * 128],
                            t["q_sc"][:, par * 2048 + dc * 512:par * 2048 + dc * 512 + 512],
                            start=(dc == 0), stop=(dc == 3))
                    tm.inc(ins, 1)
                    if tt >= 1:
                        te.wait_ge(ss.h, ss_sc0 + tt)
                    tm.inc(te.matmul(
                        bank(7),
                        t["g3"][0:2, tt * 128:(tt + 1) * 128],
                        t["g3b"][0:2, par * 512:par * 512 + 512],
                        start=True, stop=True), 1)
                    te.wait_ge(vv.h, vv_sc0 + vcnt[tt])
                    for k in range(4):
                        te.matmul(bank(k),
                                  t["cbuf"][:, k * 128:(k + 1) * 128],
                                  t["v_e"][:, tt * 512:(tt + 1) * 512],
                                  start=(tt == 0), stop=(tt == NTT - 1))
                        ins = te.matmul(
                            ps8[:, 4 * 512 + k:4 * 512 + k + 1],
                            t["cbuf"][:, k * 128:(k + 1) * 128],
                            t["onescol128"][:, :],
                            start=(tt == 0 and k == 0),
                            stop=(tt == NTT - 1 and k == 3))
                    tm.inc(ins, 1)
            E.T(t_sc)
            tm_sc1 = tm.n

            def s_sc(s, sc=sc, NTT=NTT, tm_sc0=tm_sc0, vv_sc0=vv_sc0,
                     vcnt=vcnt):
                for tt in range(NTT):
                    s.wait_ge(tm.h, tm_sc0 + 3 * tt + 2)
                    if tt >= 1:
                        s.wait_ge(vv.h, vv_sc0 + vcnt[tt - 1])
                    if tt >= 4 * sc:
                        s.wait_ge(vv.h, vv_sc0 + (vcnt[tt - 1] if tt > 0 else 0) + 1)
                        ss.inc(s.activation(out=t["dbuf"][:, :],
                                            in_=t["gclamp"][:, :],
                                            func=AF.Exp), 1)
                    else:
                        ss.inc(s.activation(out=t["dbuf"][:, :], in_=bank(7),
                                            func=AF.Exp), 1)
            E.S(s_sc)

            def v_sc(v, sc=sc, NTT=NTT, tm_sc0=tm_sc0, ss_sc0=ss_sc0,
                     vv_sc0=vv_sc0, vcnt=vcnt):
                for tt in range(NTT):
                    sb = 5 + tt % 2
                    if tt >= 4 * sc:
                        v.wait_ge(tm.h, tm_sc0 + 3 * tt + 2)
                        vv.inc(v.tensor_scalar(t["gclamp"][:, :], bank(7),
                                               0.0, None, ALU.min), 1)
                        v.wait_ge(ss.h, ss_sc0 + tt + 1)
                        q = tt - 4 * sc
                        vv.inc(v.tensor_tensor(t["dbuf"][:, :], t["dbuf"][:, :],
                                               t["masksb"][:, q * 512:(q + 1) * 512],
                                               ALU.mult), 1)
                        v.wait_ge(vv.h, vv.n)
                    else:
                        v.wait_ge(ss.h, ss_sc0 + tt + 1)
                    v.wait_ge(tm.h, tm_sc0 + 3 * tt + 1)
                    if tt >= 1:
                        v.wait_ge(tm.h, tm_sc0 + 3 * tt)
                    vv.inc(v.tensor_tensor(t["cbuf"][:, :], bank(sb),
                                           t["dbuf"][:, :], ALU.mult), 1)
            E.V(v_sc)
            q_used[sc] = tm_sc1

            emit_l1_sc_evac(E, bank, u, sc, slots, tm_sc1)






def emit_l1_sc_evac(E, bank, u, sc, slots, tm_sc1):
    cfg = E.cfg
    T = cfg.T
    t, dd = E.t, E.d
    tm, vv, ss, syq = E.tm, E.vv, E.ss, E.syq
    rde, rdo = E.rde, E.rdo
    ps8 = t["ps8"]

    def v_rd(v, sc=sc, tm_sc1=tm_sc1):
        v.wait_ge(tm.h, tm_sc1)
        vv.inc(v.tensor_scalar(t["rd4"][:, :], ps8[:, 4 * 512:4 * 512 + 4],
                               -1.0, None, ALU.mult), 1)
        v.wait_ge(vv.h, vv.n)
        vv.inc(v.tensor_tensor(t["rd4"][:, :], t["rd4"][:, :],
                               ps8[:, 4 * 512:4 * 512 + 4], ALU.max), 1)
        v.wait_ge(vv.h, vv.n)
        vv.inc(v.tensor_tensor(t["rd4"][:, :], t["rd4"][:, :],
                               t["nft"][:, 4 * sc:4 * sc + 4], ALU.max), 1)
        v.wait_ge(vv.h, vv.n)
        vv.inc(v.tensor_scalar(t["rd4"][:, :], t["rd4"][:, :],
                               1e-6, None, ALU.add), 1)
        v.wait_ge(vv.h, vv.n)
        vv.inc(v.reciprocal(t["rd4"][:, :], t["rd4"][:, :]), 1)
    E.V(v_rd)

    for k in range(4):
        def v_h(v, k=k):
            v.wait_ge(vv.h, vv.n)
            vv.inc(v.tensor_scalar(t["hbuf"][:, :], bank(k),
                                   t["rd4"][:, k:k + 1], None, ALU.mult), 1)
            v.wait_ge(vv.h, vv.n)
            vv.inc(v.tensor_reduce(t["ln_s"][:, 0:1], t["hbuf"][:, :], AX,
                                   ALU.add), 1)
        E.V(v_h)
        vv_h = vv.n

        def s_h(s, k=k, vv_h=vv_h):
            s.wait_ge(vv.h, vv_h)
            ss.inc(s.activation(out=t["sq_scr"][:, 0:512], in_=t["hbuf"][:, :],
                                func=AF.Square, accum_out=t["ln_q"][:, 0:1]), 1)
        E.S(s_h)
        ss_h = ss.n

        def v_st(v, k=k, ss_h=ss_h):
            v.wait_ge(ss.h, ss_h)
            vv.inc(v.tensor_scalar(t["ln_a"][:, 0:1], t["ln_s"][:, 0:1],
                                   1.0 / 512, None, ALU.mult), 1)
            vv.inc(v.tensor_scalar(t["ln_b"][:, 0:1], t["ln_q"][:, 0:1],
                                   1.0 / 512, None, ALU.mult), 1)
            v.wait_ge(vv.h, vv.n)
            vv.inc(v.scalar_tensor_tensor(t["ln_q"][:, 0:1], t["ln_a"][:, 0:1],
                                          -1.0, t["ln_a"][:, 0:1],
                                          ALU.mult, ALU.mult), 1)
            v.wait_ge(vv.h, vv.n)
            vv.inc(v.tensor_tensor(t["ln_b"][:, 0:1], t["ln_b"][:, 0:1],
                                   t["ln_q"][:, 0:1], ALU.add), 1)
            v.wait_ge(vv.h, vv.n)
            vv.inc(v.tensor_scalar(t["ln_b"][:, 0:1], t["ln_b"][:, 0:1],
                                   1e-5, None, ALU.add), 1)
        E.V(v_st)
        vv_var = vv.n

        def s_sq(s, vv_var=vv_var):
            s.wait_ge(vv.h, vv_var)
            ss.inc(s.activation(out=t["ln_q"][:, 0:1], in_=t["ln_b"][:, 0:1],
                                func=AF.Sqrt), 1)
        E.S(s_sq)
        ss_sq2 = ss.n

        def v_ap(v, k=k, sc=sc, ss_sq2=ss_sq2):
            v.wait_ge(ss.h, ss_sq2)
            vv.inc(v.reciprocal(t["ln_b"][:, 0:1], t["ln_q"][:, 0:1]), 1)
            v.wait_ge(vv.h, vv.n)
            vv.inc(v.tensor_scalar(t["hbuf2"][:, :], t["hbuf"][:, :],
                                   t["ln_a"][:, 0:1], t["ln_b"][:, 0:1],
                                   ALU.subtract, ALU.mult), 1)
            v.wait_ge(vv.h, vv.n)
            vv.inc(v.tensor_tensor(t["hbuf2"][:, :], t["hbuf2"][:, :],
                                   t["lnwbc"][:, u * 512:(u + 1) * 512],
                                   ALU.mult), 1)
        E.V(v_ap)
        vv_fin = vv.n
        gtok = sc * 512 + k * 128
        half = gtok // T
        r = slots[half]
        lt = gtok % T
        rsem = rde if (sc * 4 + k) % 2 == 0 else rdo

        def sy_h(sy, r=r, lt=lt, vv_fin=vv_fin, rsem=rsem):
            sy.wait_ge(vv.h, vv_fin)
            rsem.inc(sy.dma_start(
                out=dd["a2a1ri"][r * T + lt:r * T + lt + 128, :],
                in_=t["hbuf2"][:, :]), 16)
        E.Y(sy_h)
        rsem_n = rsem.n

        def v_gate(v, rsem=rsem, rsem_n=rsem_n):
            v.wait_ge(rsem.h, rsem_n)
            vv.inc(v.engine_nop(), 1)
        E.V(v_gate)


def emit_l1_post(E, bank):
    cfg = E.cfg
    S, D, T, TT, TC = cfg.S, cfg.D, cfg.T, cfg.TT, cfg.TC
    W = cfg.W
    t, dd = E.t, E.d
    tm, vv, ss, gq, syq, scq, cc = E.tm, E.vv, E.ss, E.gq, E.syq, E.scq, E.cc
    sye, syo = E.sye, E.syo
    ps8 = t["ps8"]
    ABa, AFa = E.ABa, E.AFa

    def ar(name, parts, n, fr=None):
        t[name] = (fr or ABa).alloc(parts, n)

    ABa.reset()
    AFa.reset()
    ar("hfb", 128, T)
    ar("xcc", 128, T)
    ar("zcc", 128, T)
    ar("zs", 128, T)
    ar("hst", 128, T)
    ar("skp", 128, 16, AFa)
    ar("wd_sb", 128, 16 * D)
    ar("hs_sl", 128, 2 * 512)

    def g_wd(g):
        for kc in range(16):
            gq.inc(g.dma_start(out=t["wd_sb"][:, kc * D:(kc + 1) * D],
                               in_=dd["w_downT"][kc * 128:(kc + 1) * 128, :]), 16)
        gx = E.gx
        gx.inc(g.dma_start(out=t["skp"][:, :], in_=dd["skip_pm"][:, :]), 16)
    E.G(g_wd)
    g_wd_n = gq.n
    gx_skp = E.gx.n

    for kc in range(16):
        slot, dcol = kc // 4, (kc % 4) * 128

        def sy_hf(sy, kc=kc, slot=slot, dcol=dcol):
            sy.wait_ge(cc.h, cc.n)
            if kc >= 1:
                sy.wait_ge(vv.h, hst_vv[kc - 1])
            syq.inc(sy.dma_start(
                out=t["hfb"][:, :],
                in_=dd["hret"][slot * T:(slot + 1) * T, dcol:dcol + 128],
                transpose=True), 16)
        E.Y(sy_hf)
        if kc == 0:
            hst_vv = {}
        sy_hf_n = syq.n

        def s_xz(s, kc=kc):
            if kc >= 1:
                s.wait_ge(vv.h, hst_vv[kc - 1])
            scq.inc(s.dma_start(out=t["xcc"][:, :],
                                in_=dd["xcds"][kc * 128:(kc + 1) * 128, :]), 16)
            scq.inc(s.dma_start(out=t["zcc"][:, :],
                                in_=dd["zds"][kc * 128:(kc + 1) * 128, :]), 16)
            s.wait_ge(scq.h, scq.n)
            ss.inc(s.activation(out=t["zs"][:, :], in_=t["zcc"][:, :],
                                func=AF.Silu), 1)
        E.S(s_xz)
        ss_xz = ss.n

        def v_hs(v, kc=kc, sy_hf_n=sy_hf_n, ss_xz=ss_xz):
            v.wait_ge(syq.h, sy_hf_n)
            v.wait_ge(ss.h, ss_xz)
            if kc == 0:
                v.wait_ge(E.gx.h, gx_skp)
            vv.inc(v.scalar_tensor_tensor(t["hst"][:, :], t["xcc"][:, :],
                                          t["skp"][:, kc:kc + 1], t["hfb"][:, :],
                                          ALU.mult, ALU.add), 1)
            v.wait_ge(vv.h, vv.n)
            vv.inc(v.tensor_tensor(t["hst"][:, :], t["hst"][:, :],
                                   t["zs"][:, :], ALU.mult), 1)
        E.V(v_hs)
        vv_hs = vv.n

        psem = sye if kc % 2 == 0 else syo

        def sy_hs(sy, kc=kc, vv_hs=vv_hs, psem=psem):
            sy.wait_ge(vv.h, vv_hs)
            psem.inc(sy.dma_start(out=dd["hsds"][kc * 128:(kc + 1) * 128, :],
                                  in_=t["hst"][:, :]), 16)
        E.Y(sy_hs)
        hst_vv[kc] = vv.n

        def v_g2(v, psem=psem, pn=psem.n):
            v.wait_ge(psem.h, pn)
            vv.inc(v.engine_nop(), 1)
        E.V(v_g2)
        hst_vv[kc] = vv.n

    NG = TT // 4
    dp_ld = {}
    dp_tm = {}
    dp_vv = {}
    for tb in range(NG):
        for kc2 in range(16):
            dp_tm[tb * 16 + kc2] = tm.n + kc2 + 1

        def s_ld(s, tb=tb):
            if tb == 0:
                s.wait_ge(vv.h, vv.n)
            for kc in range(16):
                gi = tb * 16 + kc
                if gi >= 2:
                    s.wait_ge(tm.h, dp_tm[gi - 2])
                psem = E.sce if kc % 2 == 0 else E.sco
                psem.inc(s.dma_start(
                    out=t["hs_sl"][:, (kc % 2) * 512:(kc % 2) * 512 + 512],
                    in_=dd["hsds"][kc * 128:(kc + 1) * 128,
                                   tb * 512:(tb + 1) * 512]), 16)
                dp_ld[(tb, kc)] = (psem, psem.n)
        E.S(s_ld)
        tmb0 = tm.n

        def t_dp(te, tb=tb, tmb0=tmb0):
            te.wait_ge(gq.h, g_wd_n)
            if tb == 0:
                te.wait_ge(vv.h, vv.n)
            else:
                te.wait_ge(vv.h, dp_vv[tb - 1])
            for kc in range(16):
                lsem, lval = dp_ld[(tb, kc)]
                te.wait_ge(lsem.h, lval)
                for i in range(4):
                    for oc in range(2):
                        ins = te.matmul(
                            bank(i * 2 + oc),
                            t["hs_sl"][:, (kc % 2) * 512 + i * 128:(kc % 2) * 512 + (i + 1) * 128],
                            t["wd_sb"][:, kc * D + oc * 512:kc * D + (oc + 1) * 512],
                            start=(kc == 0), stop=(kc == 15))
                tm.inc(ins, 1)
        E.T(t_dp)
        assert dp_tm[tb * 16 + 15] == tm.n
        tm_after = tm.n

        for i in range(4):
            def v_r(v, tb=tb, i=i, tm_after=tm_after):
                v.wait_ge(tm.h, tm_after)
                v.wait_ge(vv.h, vv.n)
                for oc in range(2):
                    vv.inc(v.tensor_tensor(
                        t["ln_r"][:, oc * 512:(oc + 1) * 512],
                        bank(i * 2 + oc),
                        t["stream"][:, (tb * 4 + i) * D + oc * 512:(tb * 4 + i) * D + (oc + 1) * 512],
                        ALU.add), 1)
            E.V(v_r)
            vv_r = vv.n
            emit_ln(E, [(vv.h, vv_r)],
                    t["stream"][:, (tb * 4 + i) * D:(tb * 4 + i + 1) * D])
        dp_vv[tb] = vv.n


def emit_l1_ffn(E, bank):
    cfg = E.cfg
    S, D, T, TT, TC = cfg.S, cfg.D, cfg.T, cfg.TT, cfg.TC
    t, dd = E.t, E.d
    tm, vv, ss, gq, syq, scq, cc = E.tm, E.vv, E.ss, E.gq, E.syq, E.scq, E.cc
    gx, sye, syo = E.gx, E.sye, E.syo
    ps8 = t["ps8"]
    ABa, AFa = E.ABa, E.AFa
    H4 = 4 * D

    def ar(name, parts, n, fr=None):
        t[name] = (fr or ABa).alloc(parts, n)

    ABa.reset()
    AFa.reset()
    ar("gbc", 128, D)
    ar("bbc", 128, D)
    ar("b1bc", 128, H4)
    ar("b2bc", 128, D)
    ar("wslot", 128, 2 * H4)
    ar("xq_tm", 128, 2 * D)
    ar("xq_fm", 128, 8 * T)
    ar("hbig", 128, H4)
    ar("w1s", 128, 8 * 512)
    ar("hs_sl2", 128, 2 * 512)
    ar("wq32", 128, 4 * D, AFa)
    ar("amx", 128, TT, AFa)
    ar("ls1", 128, TT, AFa)
    ar("ls2", 128, TT, AFa)
    ar("mw1", 128, 1, AFa)
    ar("mw2", 128, 1, AFa)
    ar("wsum", 128, 1, AFa)
    ar("mtmp", 1, 2, AFa)

    def g_bc(g):
        gq.inc(g.dma_start(out=t["gbc"][:, :],
                           in_=dd["l1f_gamma"][0:1, :].partition_broadcast(128)), 16)
        gq.inc(g.dma_start(out=t["bbc"][:, :],
                           in_=dd["l1f_beta"][0:1, :].partition_broadcast(128)), 16)
        gq.inc(g.dma_start(out=t["b1bc"][:, :],
                           in_=dd["l1f_b1"][0:1, :].partition_broadcast(128)), 16)
        gq.inc(g.dma_start(out=t["b2bc"][:, :],
                           in_=dd["l1f_b2"][0:1, :].partition_broadcast(128)), 16)
    E.G(g_bc)
    g_bc_n = gq.n

    for wi, (wds, qds, nchunk, ncols, msl) in enumerate(
            (("l1f_w1T", "w1qds", 8, H4, "mw1"),
             ("l1f_w2T", "w2qds", 32, D, "mw2"))):
        def v_zero(v):
            vv.inc(v.memset(t["wsum"][:, :], 0.0), 1)
        E.V(v_zero)

        for c in range(nchunk):
            wsem = E.gpe if c % 2 == 0 else E.gpo
            ent_vv = vv.n
            ent_ye, ent_yo = sye.n, syo.n

            def g_wl(g, wi=wi, wds=wds, c=c, ncols=ncols, wsem=wsem,
                     ent_vv=ent_vv, ent_ye=ent_ye, ent_yo=ent_yo):
                if c == 0:
                    g.wait_ge(vv.h, ent_vv)
                    g.wait_ge(sye.h, ent_ye)
                    g.wait_ge(syo.h, ent_yo)
                if c >= 2:
                    g.wait_ge(vv.h, wq_vv[c - 2])
                wsem.inc(g.dma_start(out=t["wslot"][:, (c % 2) * H4:(c % 2) * H4 + ncols],
                                     in_=dd[wds][c * 128:(c + 1) * 128, :]), 16)
            E.G(g_wl)
            if c == 0:
                wq_vv = {}
            g_wl_n = wsem.n

            def v_ws(v, c=c, ncols=ncols, g_wl_n=g_wl_n, wsem=wsem):
                v.wait_ge(wsem.h, g_wl_n)
                v.wait_ge(vv.h, vv.n)
                vv.inc(v.tensor_reduce(t["ln_s"][:, 0:1],
                                       t["wslot"][:, (c % 2) * H4:(c % 2) * H4 + ncols],
                                       AX, ALU.add, apply_absolute_value=True), 1)
                v.wait_ge(vv.h, vv.n)
                vv.inc(v.tensor_tensor(t["wsum"][:, :], t["wsum"][:, :],
                                       t["ln_s"][:, 0:1], ALU.add), 1)
            E.V(v_ws)
            wq_vv[c] = vv.n
        vv_ws = vv.n

        def t_ws(te, vv_ws=vv_ws):
            te.wait_ge(vv.h, vv_ws)
            te.wait_ge(ss.h, ss.n)
            tm.inc(te.matmul(ps8[0:1, 0:1], t["onescol128f"][:, :],
                             t["wsum"][:, :], start=True, stop=True), 1)
        E.T(t_ws)
        tm_ws = tm.n

        def v_mean(v, wi=wi, nchunk=nchunk, ncols=ncols, msl=msl, tm_ws=tm_ws):
            v.wait_ge(tm.h, tm_ws)
            vv.inc(v.tensor_scalar(t["mtmp"][0:1, 0:1], ps8[0:1, 0:1],
                                   1.0 / (nchunk * 128 * ncols), None,
                                   ALU.mult), 1)
            v.wait_ge(vv.h, vv.n)
            vv.inc(v.tensor_scalar(t["mtmp"][0:1, 0:1], t["mtmp"][0:1, 0:1],
                                   1e-5, None, ALU.max), 1)
        E.V(v_mean)
        vv_mean = vv.n

        def sy_mbc(sy, msl=msl, vv_mean=vv_mean):
            sy.wait_ge(vv.h, vv_mean)
            syq.inc(sy.dma_start(out=dd["gds"][0:1], in_=t["mtmp"][0:1, 0:1]), 16)
            sy.wait_ge(syq.h, syq.n)
            syq.inc(sy.dma_start(out=t[msl][:, :],
                                 in_=dd["gds"][None, 0:1].partition_broadcast(128)), 16)
        E.Y(sy_mbc)
        sy_mbc_n = syq.n

        def v_rw(v, msl=msl, sy_mbc_n=sy_mbc_n):
            v.wait_ge(syq.h, sy_mbc_n)
            vv.inc(v.reciprocal(t["ln_s"][:, 0:1], t[msl][:, :]), 1)
        E.V(v_rw)
        vv_rw = vv.n

        for c in range(nchunk):
            wsem = E.gpe if c % 2 == 0 else E.gpo

            def g_wl2(g, wds=wds, c=c, ncols=ncols, wsem=wsem):
                if c == 0:
                    g.wait_ge(vv.h, vv_rw)
                if c >= 2:
                    g.wait_ge((sye if c % 2 == 0 else syo).h, wq_st[c - 2])
                wsem.inc(g.dma_start(out=t["wslot"][:, (c % 2) * H4:(c % 2) * H4 + ncols],
                                     in_=dd[wds][c * 128:(c + 1) * 128, :]), 16)
            E.G(g_wl2)
            if c == 0:
                wq_st = {}
            g_wl2_n = wsem.n

            def v_q(v, c=c, ncols=ncols, g_wl2_n=g_wl2_n, wsem=wsem):
                v.wait_ge(wsem.h, g_wl2_n)
                v.wait_ge(vv.h, vv.n)
                sl = t["wslot"][:, (c % 2) * H4:(c % 2) * H4 + ncols]
                vv.inc(v.tensor_scalar(t["wq32"][:, 0:ncols], sl,
                                       t["ln_s"][:, 0:1], MAGIC,
                                       ALU.mult, ALU.add), 1)
                v.wait_ge(vv.h, vv.n)
                vv.inc(v.tensor_scalar(t["wq32"][:, 0:ncols], t["wq32"][:, 0:ncols],
                                       MAGIC, None, ALU.subtract), 1)
                v.wait_ge(vv.h, vv.n)
                vv.inc(v.tensor_scalar(sl, t["wq32"][:, 0:ncols],
                                       1.0, -1.0, ALU.min, ALU.max), 1)
            E.V(v_q)
            vv_q = vv.n
            psem = sye if c % 2 == 0 else syo

            def sy_wq(sy, qds=qds, c=c, ncols=ncols, vv_q=vv_q, psem=psem):
                sy.wait_ge(vv.h, vv_q)
                psem.inc(sy.dma_start(out=dd[qds][c * 128:(c + 1) * 128, :],
                                      in_=t["wslot"][:, (c % 2) * H4:(c % 2) * H4 + ncols]), 16)
            E.Y(sy_wq)
            wq_st[c] = psem.n

    for i in range(TT):
        emit_ln(E, [(gq.h, g_bc_n)],
                t["ln_f"][:, :].bitcast(BF16)[:, 0:D] if False else t["ln_f"][:, :],
                src_ap=t["stream"][:, i * D:(i + 1) * D],
                gb=(t["gbc"][:, :], t["bbc"][:, :]),
                out_f32_ap=None)
        vv_ln = vv.n

        def v_q(v, i=i, vv_ln=vv_ln):
            v.wait_ge(vv.h, vv_ln)
            vv.inc(v.tensor_reduce(t["ln_a"][:, 0:1], t["ln_f"][:, :], AX,
                                   ALU.max, apply_absolute_value=True), 1)
            v.wait_ge(vv.h, vv.n)
            vv.inc(v.tensor_scalar(t["ln_a"][:, 0:1], t["ln_a"][:, 0:1],
                                   1e-5, None, ALU.max), 1)
            v.wait_ge(vv.h, vv.n)
            vv.inc(v.reciprocal(t["rqp"][:, 0:1], t["ln_a"][:, 0:1]), 1)
            vv.inc(v.tensor_scalar(t["ls1"][:, i:i + 1], t["ln_a"][:, 0:1],
                                   1.0 / 127.0, None, ALU.mult), 1)
            v.wait_ge(vv.h, vv.n)
            vv.inc(v.tensor_scalar(t["rqp"][:, 0:1], t["rqp"][:, 0:1],
                                   127.0, None, ALU.mult), 1)
            v.wait_ge(vv.h, vv.n)
            vv.inc(v.tensor_scalar(t["sq_scr"][:, :], t["ln_f"][:, :],
                                   t["rqp"][:, 0:1], MAGIC, ALU.mult, ALU.add), 1)
            v.wait_ge(vv.h, vv.n)
            if i >= 2:
                v.wait_ge((E.sye if i % 2 == 0 else E.syo).h, xq_sp[i - 2])
            vv.inc(v.tensor_scalar(t["xq_tm"][:, (i % 2) * D:(i % 2 + 1) * D],
                                   t["sq_scr"][:, :], MAGIC, None,
                                   ALU.subtract), 1)
        E.V(v_q)
        vv_q2 = vv.n
        if i == 0:
            xq_sp = {}
        psem = E.sye if i % 2 == 0 else E.syo

        def sy_sp(sy, i=i, vv_q2=vv_q2, psem=psem):
            sy.wait_ge(vv.h, vv_q2)
            psem.inc(sy.dma_start(out=dd["xqds"][i * 128:(i + 1) * 128, :],
                                  in_=t["xq_tm"][:, (i % 2) * D:(i % 2 + 1) * D]), 16)
        E.Y(sy_sp)
        xq_sp[i] = psem.n

    def sy_xqf(sy):
        sy.wait_ge(E.sye.h, E.sye.n)
        sy.wait_ge(E.syo.h, E.syo.n)
        for kc in range(8):
            syq.inc(sy.dma_start(out=t["xq_fm"][:, kc * T:(kc + 1) * T],
                                 in_=dd["xqds"][:, kc * 128:(kc + 1) * 128],
                                 transpose=True), 16)
    E.Y(sy_xqf)
    sy_xqf_n = syq.n

    def v_ls1(v):
        v.wait_ge(vv.h, vv.n)
        vv.inc(v.tensor_scalar(t["ls1"][:, :], t["ls1"][:, :],
                               t["mw1"][:, 0:1], None, ALU.mult), 1)
        vv.inc(v.memset(t["amx"][:, :], 0.0), 1)
    E.V(v_ls1)
    vv_ls1 = vv.n

    st_idx = 0
    for oc in range(8):
        def g_w1(g, oc=oc):
            if oc >= 1:
                g.wait_ge(tm.h, w1_mm[oc - 1])
            for kc in range(8):
                gx.inc(g.dma_start(
                    out=t["w1s"][:, kc * 512:(kc + 1) * 512],
                    in_=dd["w1qds"][kc * 128:(kc + 1) * 128,
                                    oc * 512:(oc + 1) * 512]), 16)
        E.G(g_w1)
        if oc == 0:
            w1_mm = {}
        g_w1_n = gx.n

        tm_oc0 = tm.n

        NBK = min(8, TT)
        if oc == 0:
            m1_ev = {}
        for i2 in range(TT):
            m1_ev[oc * TT + i2] = ss.n + i2 + 1

        def t_m1(te, oc=oc, g_w1_n=g_w1_n, NBK=NBK):
            te.wait_ge(gx.h, g_w1_n)
            if oc == 0:
                te.wait_ge(syq.h, sy_xqf_n)
                te.wait_ge(vv.h, vv_ls1)
            for i in range(TT):
                b = i % NBK
                if oc * TT + i >= NBK:
                    te.wait_ge(ss.h, m1_ev[oc * TT + i - NBK])
                for kc in range(8):
                    ins = te.matmul(
                        bank(b),
                        t["xq_fm"][:, kc * T + i * 128:kc * T + (i + 1) * 128],
                        t["w1s"][:, kc * 512:(kc + 1) * 512],
                        start=(kc == 0), stop=(kc == 7))
                tm.inc(ins, 1)
        E.T(t_m1)
        w1_mm[oc] = tm.n

        for i in range(TT):
            def v_e1(v, oc=oc, i=i, tm_oc0=tm_oc0, NBK=NBK):
                v.wait_ge(tm.h, tm_oc0 + i + 1)
                par = (oc * TT + i) % 2
                if oc * TT + i >= 1:
                    v.wait_ge(ss.h, m1_ev[oc * TT + i - 1])
                vv.inc(v.scalar_tensor_tensor(
                    t["sq_scr"][:, 0:512], bank(i % NBK),
                    t["ls1"][:, i:i + 1], t["b1bc"][:, oc * 512:(oc + 1) * 512],
                    ALU.mult, ALU.add), 1)
            E.V(v_e1)
            vv_e1 = vv.n

            def s_g1(s, oc=oc, i=i, vv_e1=vv_e1):
                s.wait_ge(vv.h, vv_e1)
                par = (oc * TT + i) % 2
                if oc * TT + i >= 2:
                    s.wait_ge((E.sye if par == 0 else E.syo).h,
                              m1_st[oc * TT + i - 2])
                ss.inc(s.activation(
                    out=t["hbig"][:, par * 512:(par + 1) * 512],
                    in_=t["sq_scr"][:, 0:512], func=AF.Gelu), 1)
            E.S(s_g1)
            ss_g1 = ss.n
            assert m1_ev[oc * TT + i] == ss.n

            def v_am(v, oc=oc, i=i, ss_g1=ss_g1):
                v.wait_ge(ss.h, ss_g1)
                par = (oc * TT + i) % 2
                vv.inc(v.tensor_reduce(t["ln_s"][:, 0:1],
                                       t["hbig"][:, par * 512:(par + 1) * 512],
                                       AX, ALU.max, apply_absolute_value=True), 1)
                v.wait_ge(vv.h, vv.n)
                vv.inc(v.tensor_tensor(t["amx"][:, i:i + 1], t["amx"][:, i:i + 1],
                                       t["ln_s"][:, 0:1], ALU.max), 1)
            E.V(v_am)
            vv_am = vv.n
            par = (oc * TT + i) % 2
            psem = E.sye if par == 0 else E.syo
            if oc == 0 and i == 0:
                m1_st = {}

            def sy_h(sy, oc=oc, i=i, vv_am=vv_am, psem=psem, par=par):
                sy.wait_ge(vv.h, vv_am)
                psem.inc(sy.dma_start(
                    out=dd["hds"][i * 128:(i + 1) * 128, oc * 512:(oc + 1) * 512],
                    in_=t["hbig"][:, par * 512:(par + 1) * 512]), 16)
            E.Y(sy_h)
            m1_st[oc * TT + i] = psem.n

    def v_ls2(v):
        v.wait_ge(vv.h, vv.n)
        vv.inc(v.tensor_scalar(t["amx"][:, :], t["amx"][:, :],
                               1e-5, None, ALU.max), 1)
        v.wait_ge(vv.h, vv.n)
        vv.inc(v.tensor_scalar(t["ls2"][:, :], t["amx"][:, :],
                               1.0 / 127.0, None, ALU.mult), 1)
        v.wait_ge(vv.h, vv.n)
        vv.inc(v.tensor_scalar(t["ls2"][:, :], t["ls2"][:, :],
                               t["mw2"][:, 0:1], None, ALU.mult), 1)
        vv.inc(v.reciprocal(t["amx"][:, :], t["amx"][:, :]), 1)
        v.wait_ge(vv.h, vv.n)
        vv.inc(v.tensor_scalar(t["amx"][:, :], t["amx"][:, :],
                               127.0, None, ALU.mult), 1)
    E.V(v_ls2)
    vv_ls2 = vv.n

    for i in range(TT):
        def g_hl(g, i=i):
            if i == 0:
                g.wait_ge(E.sye.h, E.sye.n)
                g.wait_ge(E.syo.h, E.syo.n)
            if i >= 1:
                g.wait_ge((E.sye if (i - 1) % 2 == 0 else E.syo).h, q2_st[i - 1])
            gx.inc(g.dma_start(out=t["hbig"][:, :],
                               in_=dd["hds"][i * 128:(i + 1) * 128, :]), 16)
        E.G(g_hl)
        if i == 0:
            q2_st = {}
        g_hl_n = gx.n

        def v_q2(v, i=i, g_hl_n=g_hl_n):
            v.wait_ge(gx.h, g_hl_n)
            if i == 0:
                v.wait_ge(vv.h, vv_ls2)
            vv.inc(v.tensor_scalar(t["wq32"][:, :], t["hbig"][:, :],
                                   t["amx"][:, i:i + 1], MAGIC,
                                   ALU.mult, ALU.add), 1)
            v.wait_ge(vv.h, vv.n)
            vv.inc(v.tensor_scalar(t["hbig"][:, :], t["wq32"][:, :],
                                   MAGIC, None, ALU.subtract), 1)
        E.V(v_q2)
        vv_q2b = vv.n
        psem = E.sye if i % 2 == 0 else E.syo

        def sy_q2(sy, i=i, vv_q2b=vv_q2b, psem=psem):
            sy.wait_ge(vv.h, vv_q2b)
            psem.inc(sy.dma_start(out=dd["xq2ds"][i * 128:(i + 1) * 128, :],
                                  in_=t["hbig"][:, :]), 16)
        E.Y(sy_q2)
        q2_st[i] = psem.n

    final = cfg.n_layers == 2
    NG = TT // 4
    x2_ld = {}
    x2_tm = {}
    w2_ld = {}
    m2_vv = {}
    vv_mm2_0 = vv.n
    ye_q2, yo_q2 = E.sye.n, E.syo.n
    w2_tm = {}
    for tb in range(NG):
        for kc2 in range(32):
            x2_tm[tb * 32 + kc2] = tm.n + kc2 + 1
        for g2 in range(8):
            w2_tm[tb * 8 + g2] = tm.n + (g2 + 1) * 4

        def sy_x2(sy, tb=tb):
            if tb == 0:
                sy.wait_ge(E.sye.h, ye_q2)
                sy.wait_ge(E.syo.h, yo_q2)
            for kc in range(32):
                gi = tb * 32 + kc
                if gi >= 2:
                    sy.wait_ge(tm.h, x2_tm[gi - 2])
                psem = E.qpe if kc % 2 == 0 else E.qpo
                psem.inc(sy.dma_start(
                    out=t["hs_sl2"][:, (kc % 2) * 512:(kc % 2) * 512 + 512],
                    in_=dd["xq2ds"][tb * 512:(tb + 1) * 512,
                                    kc * 128:(kc + 1) * 128],
                    transpose=True), 16)
                x2_ld[(tb, kc)] = (psem, psem.n)
        E.Y(sy_x2)

        def g_w2(g, tb=tb):
            for grp in range(8):
                gg = tb * 8 + grp
                if gg == 0:
                    g.wait_ge(vv.h, vv_mm2_0)
                if gg >= 2:
                    g.wait_ge(tm.h, w2_tm[gg - 2])
                psem = E.gpe if grp % 2 == 0 else E.gpo
                psem.inc(g.dma_start(
                    out=t["wslot"][:, (grp % 2) * H4:(grp % 2) * H4 + 4 * D]
                        .rearrange("p (a c) -> p a c", c=D),
                    in_=dd["w2qds"][grp * 512:(grp + 1) * 512, :]
                        .rearrange("(a p) c -> p a c", p=128)), 16)
                w2_ld[(tb, grp)] = (psem, psem.n)
        E.G(g_w2)
        tmb0 = tm.n

        def t_m2(te, tb=tb, tmb0=tmb0):
            if tb == 0:
                te.wait_ge(E.sye.h, E.sye.n)
                te.wait_ge(E.syo.h, E.syo.n)
            else:
                te.wait_ge(vv.h, m2_vv[tb - 1])
            for kc in range(32):
                grp, kin = kc // 4, kc % 4
                if kin == 0:
                    wsem, wval = w2_ld[(tb, grp)]
                    te.wait_ge(wsem.h, wval)
                xsem, xval = x2_ld[(tb, kc)]
                te.wait_ge(xsem.h, xval)
                for i in range(4):
                    for oc in range(2):
                        ins = te.matmul(
                            bank(i * 2 + oc),
                            t["hs_sl2"][:, (kc % 2) * 512 + i * 128:(kc % 2) * 512 + (i + 1) * 128],
                            t["wslot"][:, (grp % 2) * H4 + kin * D + oc * 512:(grp % 2) * H4 + kin * D + (oc + 1) * 512],
                            start=(kc == 0), stop=(kc == 31))
                tm.inc(ins, 1)
        E.T(t_m2)
        assert x2_tm[tb * 32 + 31] == tm.n
        tm_after = tm.n

        for i in range(4):
            def v_r(v, tb=tb, i=i, tm_after=tm_after):
                v.wait_ge(tm.h, tm_after)
                v.wait_ge(vv.h, vv.n)
                for oc in range(2):
                    vv.inc(v.scalar_tensor_tensor(
                        t["sq_scr"][:, oc * 512:(oc + 1) * 512],
                        bank(i * 2 + oc),
                        t["ls2"][:, (tb * 4 + i):(tb * 4 + i) + 1],
                        t["b2bc"][:, oc * 512:(oc + 1) * 512],
                        ALU.mult, ALU.add), 1)
                    v.wait_ge(vv.h, vv.n)
                    vv.inc(v.tensor_tensor(
                        t["ln_r"][:, oc * 512:(oc + 1) * 512],
                        t["sq_scr"][:, oc * 512:(oc + 1) * 512],
                        t["stream"][:, (tb * 4 + i) * D + oc * 512:(tb * 4 + i) * D + (oc + 1) * 512],
                        ALU.add), 1)
            E.V(v_r)
            vv_r = vv.n
            ii = tb * 4 + i
            if final:
                emit_ln(E, [(vv.h, vv_r)],
                        t["stream"][:, ii * D:(ii + 1) * D],
                        out_f32_ap=t["yout"][:, (ii % 2) * D:(ii % 2 + 1) * D])
                emit_ystore(E, ii, ii % 2)
            else:
                emit_ln(E, [(vv.h, vv_r)],
                        t["stream"][:, ii * D:(ii + 1) * D])
        m2_vv[tb] = vv.n



def emit_l2(E, bank):
    cfg = E.cfg
    S, D, T, TT, TC, ST = cfg.S, cfg.D, cfg.T, cfg.TT, cfg.TC, cfg.ST
    W = cfg.W
    t, dd = E.t, E.d
    tm, vv, ss, gq, syq, scq, cc = E.tm, E.vv, E.ss, E.gq, E.syq, E.scq, E.cc
    gx, sye, syo, rde, rdo = E.gx, E.sye, E.syo, E.rde, E.rdo
    ps8 = t["ps8"]
    ABa, AFa = E.ABa, E.AFa

    def ar(name, parts, n, fr=None):
        t[name] = (fr or ABa).alloc(parts, n)

    ABa.reset()
    AFa.reset()
    ar("x_fm", 128, 8 * T)
    ar("w4sb", 128, 2 * 8 * 512)
    ar("l2st", 128, T)
    ar("l2r", 128, 512)
    ar("l2m", 128, 512)
    ar("b4pm", 128, 32, AFa)
    ar("nb4pm", 128, 32, AFa)
    sy_fm = emit_stream_to_fm(E, t["x_fm"])

    def g_b4(g):
        gq.inc(g.dma_start(out=t["b4pm"][:, :], in_=dd["l2_b4pm"][:, :]), 16)
        gq.inc(g.dma_start(out=t["nb4pm"][:, :], in_=dd["l2_nb4pm"][:, :]), 16)
    E.G(g_b4)
    g_b4_n = gq.n

    b2_evac = {}
    b2_st = {}
    b2_mm = {}
    for og in range(8):
        wslot = og % 2
        wsem = E.gpe if wslot == 0 else E.gpo

        def g_w(g, og=og, wslot=wslot, wsem=wsem):
            if og >= 2:
                g.wait_ge(tm.h, b2_mm[og * 4 - 5])
            for kc in range(8):
                wsem.inc(g.dma_start(
                    out=t["w4sb"][:, wslot * 4096 + kc * 512:wslot * 4096 + (kc + 1) * 512],
                    in_=dd["l2_wT"][kc * 128:(kc + 1) * 128,
                                    og * 512:(og + 1) * 512]), 16)
        E.G(g_w)
        g_w_n = wsem.n

        for oc4 in range(4):
            oc = og * 4 + oc4
            pi, hc = oc // 8, oc % 8
            par = oc % 2

            def t_mm(te, og=og, oc4=oc4, oc=oc, par=par, wslot=wslot,
                     g_w_n=g_w_n, wsem=wsem):
                if oc == 0:
                    te.wait_ge(syq.h, sy_fm)
                    te.wait_ge(gq.h, g_b4_n)
                if oc4 == 0:
                    te.wait_ge(wsem.h, g_w_n)
                if oc >= 2:
                    te.wait_ge(ss.h, b2_evac[oc - 2])
                for kc in range(8):
                    for tc in range(TC):
                        ins = te.matmul(
                            bank(par * TC + tc)[:, 0:W],
                            t["w4sb"][:, wslot * 4096 + kc * 512 + oc4 * 128:
                                      wslot * 4096 + kc * 512 + (oc4 + 1) * 128],
                            t["x_fm"][:, kc * T + tc * W:kc * T + (tc + 1) * W],
                            start=(kc == 0), stop=(kc == 7))
                        if kc == 7:
                            tm.inc(ins, 1)
            E.T(t_mm)
            tm_after = tm.n
            b2_mm[oc] = tm.n

            ss_base = ss.n
            nact = 3 if pi < 2 else 1

            def s_ev(s, oc=oc, pi=pi, par=par, tm_after=tm_after,
                     ss_base=ss_base, nact=nact):
                if oc >= 1:
                    s.wait_ge((sye if (oc - 1) % 2 == 0 else syo).h,
                              b2_st[oc - 1])
                for tc in range(TC):
                    s.wait_ge(tm.h, tm_after - TC + tc + 1)
                    if pi < 2:
                        if tc == 0:
                            s.wait_ge(vv.h, cb_vv0)
                        if tc >= 1:
                            s.wait_ge(vv.h, cb_vv0 + tc)
                        ss.inc(s.activation(
                            out=t["l2r"][:, 0:W],
                            in_=bank(par * TC + tc)[:, 0:W], func=AF.Relu,
                            bias=t["b4pm"][:, oc:oc + 1]), 1)
                        s.wait_ge(ss.h, ss.n)
                        ss.inc(s.activation(
                            out=t["l2m"][:, 0:W],
                            in_=bank(par * TC + tc)[:, 0:W], func=AF.Relu,
                            scale=-1.0, bias=t["nb4pm"][:, oc:oc + 1]), 1)
                        s.wait_ge(ss.h, ss.n)
                        ss.inc(s.activation(out=t["l2m"][:, 0:W],
                                            in_=t["l2m"][:, 0:W],
                                            func=AF.Exp, scale=-1.0), 1)
                    else:
                        ss.inc(s.activation(
                            out=t["l2st"][:, tc * W:(tc + 1) * W],
                            in_=bank(par * TC + tc)[:, 0:W], func=AF.Silu,
                            bias=t["b4pm"][:, oc:oc + 1]), 1)
                b2_evac[oc] = ss.n
            cb_vv0 = vv.n
            E.S(s_ev)
            ss_ev = ss.n

            if pi < 2:
                for tc in range(TC):
                    def v_cb(v, tc=tc, base=ss_base + 3 * tc + 3):
                        v.wait_ge(ss.h, base)
                        vv.inc(v.scalar_tensor_tensor(
                            t["l2st"][:, tc * W:(tc + 1) * W],
                            t["l2r"][:, 0:W], -1.0, t["l2m"][:, 0:W],
                            ALU.add, ALU.add), 1)
                    E.V(v_cb)
                    assert vv.n == cb_vv0 + tc + 1
            vv_cb = vv.n

            psem = sye if oc % 2 == 0 else syo

            def sy_st(sy, oc=oc, pi=pi, hc=hc, vv_cb=vv_cb, ssn=ss.n, psem=psem):
                if pi < 2:
                    sy.wait_ge(vv.h, vv_cb)
                else:
                    sy.wait_ge(ss.h, ssn)
                if pi < 3:
                    psem.inc(sy.dma_start(
                        out=dd["a2a2i"][hc * 384 + pi * 128:hc * 384 + (pi + 1) * 128, :],
                        in_=t["l2st"][:, :]), 16)
                else:
                    psem.inc(sy.dma_start(
                        out=dd["ufds"][hc * 128:(hc + 1) * 128, :],
                        in_=t["l2st"][:, :]), 16)
            E.Y(sy_st)
            b2_st[oc] = psem.n

    def g_a2a2(g):
        g.wait_ge(sye.h, sye.n)
        g.wait_ge(syo.h, syo.n)
        cc.inc(g.collective_compute(
            "AllToAll", ALU.bypass, replica_groups=[list(range(8))],
            ins=[dd["a2a2i"][:, :]], outs=[dd["a2a2o"][:, :]]), 1)
    E.G(g_a2a2)

    E.barrier()
    emit_l2_units(E, bank)

    def g_a2a2r(g):
        g.wait_ge(rde.h, rde.n)
        g.wait_ge(rdo.h, rdo.n)
        cc.inc(g.collective_compute(
            "AllToAll", ALU.bypass, replica_groups=[list(range(8))],
            ins=[dd["a2a2ri"][:, :]], outs=[dd["a2a2ro"][:, :]]), 1)
    E.G(g_a2a2r)

    E.barrier()
    emit_l2_post(E, bank)
    E.barrier()
    emit_l2_ffn(E, bank)

    if cfg.n_layers >= 4:
        E.barrier()
        emit_l3(E, bank)


def emit_l2_units(E, bank):
    cfg = E.cfg
    S, D, T, TT, TC, ST = cfg.S, cfg.D, cfg.T, cfg.TT, cfg.TC, cfg.ST
    t, dd = E.t, E.d
    tm, vv, ss, gq, syq, scq, cc = E.tm, E.vv, E.ss, E.gq, E.syq, E.scq, E.cc
    gx, sye, syo, rde, rdo = E.gx, E.sye, E.syo, E.rde, E.rdo
    ps8 = t["ps8"]
    ABa, AFa = E.ABa, E.AFa
    NSC = S // 512

    def ar(name, parts, n, fr=None):
        t[name] = (fr or ABa).alloc(parts, n)

    ABa.reset()
    AFa.reset()
    ar("k_fm2", 128, S)
    ar("v_e2", 128, ST * 128)
    ar("q_sc2", 128, 2 * 512)
    ar("cbuf2", 128, 512)
    ar("hbuf3", 128, 2 * 128)
    ar("masksb", 128, 4 * 512)

    def g_msk(g):
        gq.inc(g.dma_start(out=t["masksb"][:, :], in_=dd["masks4"][:, :]), 16)
    E.G(g_msk)
    g_msk_n = gq.n

    hb_hist = []
    for u in range(4):
        slots = (2 * u, 2 * u + 1)

        def sy_k(sy, u=u, slots=slots):
            sy.wait_ge(cc.h, cc.n)
            if u >= 1:
                sy.wait_ge(tm.h, tm.n)
                sy.wait_ge(vv.h, vv.n)
            for half, r in enumerate(slots):
                E.kq.inc(sy.dma_start(
                    out=t["k_fm2"][:, half * T:half * T + T],
                    in_=dd["a2a2o"][r * 384 + 128:r * 384 + 256, :]), 16)
            for tt in range(ST):
                half = (tt * 128) // T
                r = slots[half]
                c0 = (tt * 128) % T
                E.vtq.inc(sy.dma_start(
                    out=t["v_e2"][:, tt * 128:(tt + 1) * 128],
                    in_=dd["a2a2o"][r * 384 + 256:r * 384 + 384, c0:c0 + 128],
                    transpose=True), 16)
        E.Y(sy_k)
        sy_k_n = E.kq.n
        vt_n = E.vtq.n

        q_used = {}
        for sc in range(NSC):
            NTT = 4 * sc + 4
            qsem = E.qpe if sc % 2 == 0 else E.qpo

            def sy_q(sy, sc=sc, slots=slots, qsem=qsem):
                par = sc % 2
                if sc >= 2:
                    sy.wait_ge(tm.h, q_used[sc - 2])
                seg = 0
                while seg < 512:
                    gtok = sc * 512 + seg
                    half = gtok // T
                    r = slots[half]
                    c0 = gtok % T
                    ln_ = min(512 - seg, T - c0)
                    qsem.inc(sy.dma_start(
                        out=t["q_sc2"][:, par * 512 + seg:par * 512 + seg + ln_],
                        in_=dd["a2a2o"][r * 384:r * 384 + 128, c0:c0 + ln_]), 16)
                    seg += ln_
            E.Y(sy_q)
            sy_q_n = qsem.n

            tm_sc0 = tm.n
            vv_sc0 = vv.n

            def t_sc(te, sc=sc, NTT=NTT, sy_q_n=sy_q_n, tm_sc0=tm_sc0,
                     vv_sc0=vv_sc0, qsem=qsem, u=u):
                par = sc % 2
                if sc == 0:
                    te.wait_ge(E.kq.h, sy_k_n)
                    te.wait_ge(E.vtq.h, vt_n)
                    if u == 0:
                        te.wait_ge(gq.h, g_msk_n)
                te.wait_ge(qsem.h, sy_q_n)
                if sc >= 1:
                    te.wait_ge(vv.h, vv_sc0)
                for tt in range(NTT):
                    sb = 5 + tt % 2
                    if tt >= 2:
                        te.wait_ge(vv.h, vv_sc0 + tt - 1)
                    ins = te.matmul(
                        bank(sb),
                        t["k_fm2"][:, tt * 128:(tt + 1) * 128],
                        t["q_sc2"][:, par * 512:par * 512 + 512],
                        start=True, stop=True)
                    tm.inc(ins, 1)
                    te.wait_ge(vv.h, vv_sc0 + tt + 1)
                    for k4 in range(4):
                        ins = te.matmul(
                            bank(k4)[:, 0:128],
                            t["cbuf2"][:, k4 * 128:(k4 + 1) * 128],
                            t["v_e2"][:, tt * 128:(tt + 1) * 128],
                            start=(tt == 0), stop=(tt == NTT - 1))
                    tm.inc(ins, 1)
            E.T(t_sc)
            tm_sc1 = tm.n

            def v_sc(v, sc=sc, NTT=NTT, tm_sc0=tm_sc0, vv_sc0=vv_sc0):
                for tt in range(NTT):
                    sb = 5 + tt % 2
                    v.wait_ge(tm.h, tm_sc0 + 2 * tt + 1)
                    if tt >= 1:
                        v.wait_ge(tm.h, tm_sc0 + 2 * tt)
                    if tt >= 4 * sc:
                        q = tt - 4 * sc
                        vv.inc(v.tensor_tensor(
                            t["cbuf2"][:, :], bank(sb),
                            t["masksb"][:, q * 512:(q + 1) * 512], ALU.mult), 1)
                    else:
                        vv.inc(v.tensor_copy(t["cbuf2"][:, :], bank(sb)), 1)
            E.V(v_sc)
            q_used[sc] = tm_sc1

            for k in range(4):
                par4 = k % 2
                guard = hb_hist[-2] if len(hb_hist) >= 2 else None

                def v_h(v, k=k, tm_sc1=tm_sc1, par4=par4, guard=guard):
                    v.wait_ge(tm.h, tm_sc1)
                    v.wait_ge(vv.h, vv.n)
                    if guard is not None:
                        v.wait_ge(guard[0].h, guard[1])
                    vv.inc(v.tensor_copy(
                        t["hbuf3"][:, par4 * 128:(par4 + 1) * 128],
                        bank(k)[:, 0:128]), 1)
                E.V(v_h)
                vv_h = vv.n
                gtok = sc * 512 + k * 128
                half = gtok // T
                r = slots[half]
                lt = gtok % T
                rsem = rde if (sc * 4 + k) % 2 == 0 else rdo

                def sy_h(sy, r=r, lt=lt, vv_h=vv_h, rsem=rsem, par4=par4):
                    sy.wait_ge(vv.h, vv_h)
                    rsem.inc(sy.dma_start(
                        out=dd["a2a2ri"][r * T + lt:r * T + lt + 128, :],
                        in_=t["hbuf3"][:, par4 * 128:(par4 + 1) * 128]), 16)
                E.Y(sy_h)
                hb_hist.append((rsem, rsem.n))


def emit_l2_post(E, bank):
    cfg = E.cfg
    S, D, T, TT, TC, ST = cfg.S, cfg.D, cfg.T, cfg.TT, cfg.TC, cfg.ST
    W = cfg.W
    t, dd = E.t, E.d
    tm, vv, ss, gq, syq, scq, cc = E.tm, E.vv, E.ss, E.gq, E.syq, E.scq, E.cc
    gx, sye, syo = E.gx, E.sye, E.syo
    ps8 = t["ps8"]
    ABa, AFa = E.ABa, E.AFa

    def ar(name, parts, n, fr=None):
        t[name] = (fr or ABa).alloc(parts, n)

    ABa.reset()
    AFa.reset()
    ar("att_fm", 128, 8 * T)
    ar("u_sb", 128, 2 * T)
    ar("wo_sb", 128, 8 * D)
    ar("l2sq", 128, 2 * 512, AFa)
    ar("rfbc", 128, T, AFa)
    ar("bo_bc", 128, D, AFa)
    ar("scpm", 128, 8, AFa)
    ar("rrow", 1, T, AFa)

    def sy_att(sy):
        sy.wait_ge(cc.h, cc.n)
        for h in range(8):
            for t2 in range(T // 128):
                syq.inc(sy.dma_start(
                    out=t["att_fm"][:, h * T + t2 * 128:h * T + (t2 + 1) * 128],
                    in_=dd["a2a2ro"][h * T + t2 * 128:h * T + (t2 + 1) * 128, 0:128],
                    transpose=True), 16)
    E.Y(sy_att)
    sy_att_n = syq.n

    def g_wo(g):
        for kc in range(8):
            gq.inc(g.dma_start(out=t["wo_sb"][:, kc * D:(kc + 1) * D],
                               in_=dd["l2_woT"][kc * 128:(kc + 1) * 128, :]), 16)
        gq.inc(g.dma_start(out=t["bo_bc"][:, :],
                           in_=dd["l2_bo"][0:1, :].partition_broadcast(128)), 16)
        gq.inc(g.dma_start(out=t["scpm"][:, :], in_=dd["l2_scale_pm"][:, :]), 16)
    E.G(g_wo)
    g_wo_n = gq.n

    tm_sq0 = tm.n
    ss_sq0 = ss.n
    idx = 0
    sq_pred = {}
    for h in range(8):
        for tc in range(TC):
            sq_pred[idx] = (ss_sq0 + idx + 1, tm_sq0 + idx + 1)
            idx += 1
    for h in range(8):
        for tc in range(TC):
            gi = h * TC + tc
            slot = gi % 2

            def s_sq(s, h=h, tc=tc, gi=gi, slot=slot):
                if gi == 0:
                    s.wait_ge(syq.h, sy_att_n)
                if gi >= 2:
                    s.wait_ge(tm.h, sq_pred[gi - 2][1])
                ss.inc(s.activation(
                    out=t["l2sq"][:, slot * 512:slot * 512 + W],
                    in_=t["att_fm"][:, h * T + tc * W:h * T + (tc + 1) * W],
                    func=AF.Square), 1)
                assert ss.n == sq_pred[gi][0]
            E.S(s_sq)

            def t_sq(te, h=h, tc=tc, gi=gi, slot=slot):
                te.wait_ge(ss.h, sq_pred[gi][0])
                tm.inc(te.matmul(
                    ps8[0:1, tc * 512:tc * 512 + W],
                    t["onescol128f"][:, :],
                    t["l2sq"][:, slot * 512:slot * 512 + W],
                    start=(h == 0), stop=(h == 7)), 1)
                assert tm.n == sq_pred[gi][1]
            E.T(t_sq)
    tm_sq1 = tm.n

    def s_rms(s):
        s.wait_ge(tm.h, tm_sq1)
        s.wait_ge(vv.h, vv.n)
        for tc in range(TC):
            ss.inc(s.activation(
                out=t["rrow"][0:1, tc * W:(tc + 1) * W],
                in_=ps8[0:1, tc * 512:tc * 512 + W],
                func=AF.Sqrt, scale=1.0 / 1024.0), 1)
    E.S(s_rms)
    ss_rms = ss.n

    def v_rf(v):
        v.wait_ge(ss.h, ss_rms)
        vv.inc(v.tensor_scalar(t["rrow"][0:1, :], t["rrow"][0:1, :],
                               1e-8, None, ALU.add), 1)
        v.wait_ge(vv.h, vv.n)
        vv.inc(v.reciprocal(t["rrow"][0:1, :], t["rrow"][0:1, :]), 1)
    E.V(v_rf)
    vv_rf = vv.n

    def sy_rf(sy):
        sy.wait_ge(vv.h, vv_rf)
        syq.inc(sy.dma_start(out=dd["gds"][0:T], in_=t["rrow"][0:1, :]), 16)
        sy.wait_ge(syq.h, syq.n)
        syq.inc(sy.dma_start(
            out=t["rfbc"][:, :],
            in_=dd["gds"][None, 0:T].partition_broadcast(128)), 16)
    E.Y(sy_rf)
    sy_rf_n = syq.n

    gx_u = {}
    pr_vv = {}
    for h in range(8):
        if h % 2 == 0:
            def g_u(g, hp=h):
                if hp == 0:
                    g.wait_ge(gq.h, g_wo_n)
                if hp >= 2:
                    g.wait_ge(vv.h, pr_vv[hp - 1])
                for hh in (hp, hp + 1):
                    gx.inc(g.dma_start(
                        out=t["u_sb"][:, (hh % 2) * T:(hh % 2) * T + T],
                        in_=dd["ufds"][hh * 128:(hh + 1) * 128, :]), 16)
            E.G(g_u)
            gx_u[h] = gx.n
            gx_u[h + 1] = gx.n

        def v_pr(v, h=h):
            v.wait_ge(gx.h, gx_u[h])
            if h == 0:
                v.wait_ge(syq.h, sy_rf_n)
                v.wait_ge(ss.h, ss.n)
            v.wait_ge(vv.h, vv.n)
            vv.inc(v.scalar_tensor_tensor(
                t["att_fm"][:, h * T:(h + 1) * T],
                t["att_fm"][:, h * T:(h + 1) * T],
                t["scpm"][:, h:h + 1], t["rfbc"][:, :],
                ALU.mult, ALU.mult), 1)
            v.wait_ge(vv.h, vv.n)
            vv.inc(v.tensor_tensor(
                t["att_fm"][:, h * T:(h + 1) * T],
                t["att_fm"][:, h * T:(h + 1) * T],
                t["u_sb"][:, (h % 2) * T:(h % 2) * T + T], ALU.mult), 1)
        E.V(v_pr)
        pr_vv[h] = vv.n
    vv_prod = vv.n

    wo_vv = {}
    for i in range(TT):
        tm_i0 = tm.n

        def t_wo(te, i=i):
            if i == 0:
                te.wait_ge(vv.h, vv_prod)
                te.wait_ge(ss.h, ss.n)
            if i >= 2:
                te.wait_ge(vv.h, wo_vv[i - 2])
            for h in range(8):
                for oc in range(2):
                    ins = te.matmul(
                        bank((i % 2) * 2 + oc),
                        t["att_fm"][:, h * T + i * 128:h * T + (i + 1) * 128],
                        t["wo_sb"][:, h * D + oc * 512:h * D + (oc + 1) * 512],
                        start=(h == 0), stop=(h == 7))
                tm.inc(ins, 1)
        E.T(t_wo)
        tm_i1 = tm.n

        def v_r(v, i=i, tm_i1=tm_i1):
            v.wait_ge(tm.h, tm_i1)
            v.wait_ge(vv.h, vv.n)
            for oc in range(2):
                vv.inc(v.tensor_tensor(
                    t["ln_r"][:, oc * 512:(oc + 1) * 512],
                    bank((i % 2) * 2 + oc),
                    t["bo_bc"][:, oc * 512:(oc + 1) * 512], ALU.add), 1)
                v.wait_ge(vv.h, vv.n)
                vv.inc(v.tensor_tensor(
                    t["ln_r"][:, oc * 512:(oc + 1) * 512],
                    t["ln_r"][:, oc * 512:(oc + 1) * 512],
                    t["stream"][:, i * D + oc * 512:i * D + (oc + 1) * 512],
                    ALU.add), 1)
        E.V(v_r)
        vv_r = vv.n
        emit_ln(E, [(vv.h, vv_r)], t["stream"][:, i * D:(i + 1) * D])
        wo_vv[i] = vv.n


def emit_l2_ffn(E, bank):
    cfg = E.cfg
    S, D, T, TT, TC = cfg.S, cfg.D, cfg.T, cfg.TT, cfg.TC
    t, dd = E.t, E.d
    tm, vv, ss, gq, syq, scq, cc = E.tm, E.vv, E.ss, E.gq, E.syq, E.scq, E.cc
    gx, sye, syo = E.gx, E.sye, E.syo
    ps8 = t["ps8"]
    ABa, AFa = E.ABa, E.AFa
    H4 = 4 * D

    def ar(name, parts, n, fr=None):
        t[name] = (fr or ABa).alloc(parts, n)

    ABa.reset()
    AFa.reset()
    ar("x_fm", 128, 8 * T)
    ar("wsl", 128, 2 * H4)
    ar("wsg", 128, H4)
    ar("w1s", 128, 8 * 512)
    ar("hbig", 128, 2 * 512)
    ar("hs_sl2", 128, 2 * 512)
    ar("b1bc", 128, H4)
    ar("b2bc", 128, D)
    ar("a1bc", 128, H4, AFa)
    ar("a2bc", 128, D, AFa)
    sy_fm = emit_stream_to_fm(E, t["x_fm"])

    def g_bc(g):
        gq.inc(g.dma_start(out=t["b1bc"][:, :],
                           in_=dd["l2f_b1"][0:1, :].partition_broadcast(128)), 16)
        gq.inc(g.dma_start(out=t["b2bc"][:, :],
                           in_=dd["l2f_b2"][0:1, :].partition_broadcast(128)), 16)
    E.G(g_bc)
    g_bc_n = gq.n

    for wi, (wds, sds, nchunk, ncols, abc) in enumerate(
            (("l2f_w1T", "sgn1ds", 8, H4, "a1bc"),
             ("l2f_w2T", "sgn2ds", 32, D, "a2bc"))):
        sg_st = {}
        for c in range(nchunk):
            wsem = E.gpe if c % 2 == 0 else E.gpo
            ent_ss, ent_tm = ss.n, tm.n
            ent_ye, ent_yo = sye.n, syo.n

            def g_wl(g, wds=wds, c=c, ncols=ncols, wsem=wsem,
                     ent_ss=ent_ss, ent_tm=ent_tm, ent_ye=ent_ye,
                     ent_yo=ent_yo):
                if c == 0:
                    g.wait_ge(ss.h, ent_ss)
                    g.wait_ge(tm.h, ent_tm)
                    g.wait_ge(sye.h, ent_ye)
                    g.wait_ge(syo.h, ent_yo)
                if c >= 2:
                    g.wait_ge(tm.h, ab_tm[c - 2])
                wsem.inc(g.dma_start(
                    out=t["wsl"][:, (c % 2) * H4:(c % 2) * H4 + ncols],
                    in_=dd[wds][c * 128:(c + 1) * 128, :]), 16)
            E.G(g_wl)
            if c == 0:
                ab_tm = {}
            g_wl_n = wsem.n

            def s_sg(s, c=c, ncols=ncols, g_wl_n=g_wl_n, wsem=wsem):
                s.wait_ge(wsem.h, g_wl_n)
                if c >= 1:
                    s.wait_ge((sye if (c - 1) % 2 == 0 else syo).h,
                              sg_st[c - 1])
                ss.inc(s.activation(out=t["wsg"][:, 0:ncols],
                                    in_=t["wsl"][:, (c % 2) * H4:(c % 2) * H4 + ncols],
                                    func=AF.Sign), 1)
                s.wait_ge(ss.h, ss.n)
                ss.inc(s.activation(out=t["wsl"][:, (c % 2) * H4:(c % 2) * H4 + ncols],
                                    in_=t["wsl"][:, (c % 2) * H4:(c % 2) * H4 + ncols],
                                    func=AF.Abs), 1)
            E.S(s_sg)
            ss_sg = ss.n

            def t_ab(te, c=c, ncols=ncols, ss_sg=ss_sg, nchunk=nchunk):
                te.wait_ge(ss.h, ss_sg)
                if c == 0:
                    te.wait_ge(vv.h, vv.n)
                    te.wait_ge(syq.h, syq.n)
                nmov = (ncols + 511) // 512
                for k in range(nmov):
                    w_ = min(512, ncols - k * 512)
                    ins = te.matmul(
                        ps8[0:1, k * 512:k * 512 + w_],
                        t["onescol128"][:, :],
                        t["wsl"][:, (c % 2) * H4 + k * 512:(c % 2) * H4 + k * 512 + w_],
                        start=(c == 0), stop=(c == nchunk - 1))
                tm.inc(ins, 1)
            E.T(t_ab)
            ab_tm[c] = tm.n

            psem = sye if c % 2 == 0 else syo

            def sy_sg(sy, sds=sds, c=c, ncols=ncols, ss_sg=ss_sg, psem=psem):
                sy.wait_ge(ss.h, ss_sg - 1)
                psem.inc(sy.dma_start(out=dd[sds][c * 128:(c + 1) * 128, :],
                                      in_=t["wsg"][:, 0:ncols]), 16)
            E.Y(sy_sg)
            sg_st[c] = psem.n
        tm_ab1 = tm.n

        for k in range((ncols + 511) // 512):
            w_ = min(512, ncols - k * 512)

            def v_cp(v, k=k, w_=w_, nchunk=nchunk, tm_ab1=tm_ab1):
                v.wait_ge(tm.h, tm_ab1)
                v.wait_ge(vv.h, vv.n)
                v.wait_ge(syq.h, syq.n)
                vv.inc(v.tensor_scalar(
                    t["sq_scr"][0:1, 0:w_], ps8[0:1, k * 512:k * 512 + w_],
                    1.0 / (nchunk * 128), None, ALU.mult), 1)
            E.V(v_cp)
            v_cp_n = vv.n

            def sy_cp(sy, k=k, w_=w_, v_cp_n=v_cp_n):
                sy.wait_ge(vv.h, v_cp_n)
                syq.inc(sy.dma_start(out=dd["alds"][k * 512:k * 512 + w_],
                                     in_=t["sq_scr"][0:1, 0:w_]), 16)
            E.Y(sy_cp)

        def sy_al(sy, abc=abc, ncols=ncols):
            sy.wait_ge(syq.h, syq.n)
            syq.inc(sy.dma_start(
                out=t[abc][:, :],
                in_=dd["alds"][None, 0:ncols].partition_broadcast(128)), 16)
        E.Y(sy_al)

    sy_al_n = syq.n
    vv_al_n = vv.n

    w1_mm = {}
    m1_ev = {}
    m1_st = {}
    NBK = min(8, TT)
    for oc in range(8):
        wsem2 = E.gpe if oc % 2 == 0 else E.gpo

        def g_w1(g, oc=oc, wsem2=wsem2):
            if oc < 2:
                g.wait_ge(sye.h, sye.n)
                g.wait_ge(syo.h, syo.n)
            if oc >= 2:
                g.wait_ge(tm.h, w1_mm[oc - 2])
            for kc in range(8):
                wsem2.inc(g.dma_start(
                    out=t["w1s"][:, kc * 512:(kc + 1) * 512] if oc % 2 == 0
                        else t["wsl"][:, kc * 512:(kc + 1) * 512],
                    in_=dd["sgn1ds"][kc * 128:(kc + 1) * 128,
                                     oc * 512:(oc + 1) * 512]), 16)
        E.G(g_w1)
        g_w1_n = wsem2.n
        for i2 in range(TT):
            m1_ev[oc * TT + i2] = ss.n + i2 + 1
        tm_oc0 = tm.n

        def t_m1(te, oc=oc, g_w1_n=g_w1_n, wsem2=wsem2):
            te.wait_ge(wsem2.h, g_w1_n)
            if oc == 0:
                te.wait_ge(syq.h, sy_fm)
                te.wait_ge(syq.h, sy_al_n)
            wbuf = t["w1s"] if oc % 2 == 0 else t["wsl"]
            for i in range(TT):
                b = i % NBK
                if oc * TT + i >= NBK:
                    te.wait_ge(ss.h, m1_ev[oc * TT + i - NBK])
                for kc in range(8):
                    ins = te.matmul(
                        bank(b),
                        t["x_fm"][:, kc * T + i * 128:kc * T + (i + 1) * 128],
                        wbuf[:, kc * 512:(kc + 1) * 512],
                        start=(kc == 0), stop=(kc == 7))
                tm.inc(ins, 1)
        E.T(t_m1)
        w1_mm[oc] = tm.n

        for i in range(TT):
            def v_e1(v, oc=oc, i=i, tm_oc0=tm_oc0):
                v.wait_ge(tm.h, tm_oc0 + i + 1)
                if oc == 0 and i == 0:
                    v.wait_ge(gq.h, g_bc_n)
                    v.wait_ge(vv.h, vv_al_n)
                if oc * TT + i >= 1:
                    v.wait_ge(ss.h, m1_ev[oc * TT + i - 1])
                v.wait_ge(vv.h, vv.n)
                vv.inc(v.tensor_tensor(
                    t["sq_scr"][:, 0:512], bank(i % NBK),
                    t["a1bc"][:, oc * 512:(oc + 1) * 512], ALU.mult), 1)
                v.wait_ge(vv.h, vv.n)
                vv.inc(v.tensor_tensor(
                    t["sq_scr"][:, 0:512], t["sq_scr"][:, 0:512],
                    t["b1bc"][:, oc * 512:(oc + 1) * 512], ALU.add), 1)
            E.V(v_e1)
            vv_e1 = vv.n
            par = (oc * TT + i) % 2

            def s_g1(s, oc=oc, i=i, vv_e1=vv_e1, par=par):
                s.wait_ge(vv.h, vv_e1)
                if oc * TT + i >= 2:
                    s.wait_ge((sye if par == 0 else syo).h,
                              m1_st[oc * TT + i - 2])
                ss.inc(s.activation(
                    out=t["hbig"][:, par * 512:(par + 1) * 512],
                    in_=t["sq_scr"][:, 0:512], func=AF.Gelu), 1)
            E.S(s_g1)
            assert m1_ev[oc * TT + i] == ss.n
            psem = sye if par == 0 else syo

            def sy_h(sy, oc=oc, i=i, psem=psem, par=par, ssn=ss.n):
                sy.wait_ge(ss.h, ssn)
                psem.inc(sy.dma_start(
                    out=dd["hds"][i * 128:(i + 1) * 128, oc * 512:(oc + 1) * 512],
                    in_=t["hbig"][:, par * 512:(par + 1) * 512]), 16)
            E.Y(sy_h)
            m1_st[oc * TT + i] = psem.n

    final = cfg.n_layers == 3
    NG = TT // 4
    x2_ld = {}
    x2_tm = {}
    w2_ld = {}
    m2_vv = {}
    w2_tm = {}
    vv_mm2_0 = vv.n
    ye_h, yo_h = sye.n, syo.n
    for tb in range(NG):
        for kc2 in range(32):
            x2_tm[tb * 32 + kc2] = tm.n + kc2 + 1
        for g2 in range(8):
            w2_tm[tb * 8 + g2] = tm.n + (g2 + 1) * 4

        def sy_x2(sy, tb=tb):
            if tb == 0:
                sy.wait_ge(sye.h, ye_h)
                sy.wait_ge(syo.h, yo_h)
            for kc in range(32):
                gi = tb * 32 + kc
                if gi >= 2:
                    sy.wait_ge(tm.h, x2_tm[gi - 2])
                psem = E.qpe if kc % 2 == 0 else E.qpo
                psem.inc(sy.dma_start(
                    out=t["hs_sl2"][:, (kc % 2) * 512:(kc % 2) * 512 + 512],
                    in_=dd["hds"][tb * 512:(tb + 1) * 512,
                                  kc * 128:(kc + 1) * 128],
                    transpose=True), 16)
                x2_ld[(tb, kc)] = (psem, psem.n)
        E.Y(sy_x2)

        def g_w2(g, tb=tb):
            for grp in range(8):
                gg = tb * 8 + grp
                if gg == 0:
                    g.wait_ge(vv.h, vv_mm2_0)
                    g.wait_ge(sye.h, sye.n)
                    g.wait_ge(syo.h, syo.n)
                if gg >= 2:
                    g.wait_ge(tm.h, w2_tm[gg - 2])
                psem = E.gpe if grp % 2 == 0 else E.gpo
                psem.inc(g.dma_start(
                    out=t["wsl"][:, (grp % 2) * H4:(grp % 2) * H4 + 4 * D]
                        .rearrange("p (a c) -> p a c", c=D),
                    in_=dd["sgn2ds"][grp * 512:(grp + 1) * 512, :]
                        .rearrange("(a p) c -> p a c", p=128)), 16)
                w2_ld[(tb, grp)] = (psem, psem.n)
        E.G(g_w2)
        tmb0 = tm.n

        def t_m2(te, tb=tb, tmb0=tmb0):
            if tb == 0:
                te.wait_ge(vv.h, vv_mm2_0)
            else:
                te.wait_ge(vv.h, m2_vv[tb - 1])
            for kc in range(32):
                grp, kin = kc // 4, kc % 4
                if kin == 0:
                    wsem, wval = w2_ld[(tb, grp)]
                    te.wait_ge(wsem.h, wval)
                xsem, xval = x2_ld[(tb, kc)]
                te.wait_ge(xsem.h, xval)
                for i in range(4):
                    for oc in range(2):
                        ins = te.matmul(
                            bank(i * 2 + oc),
                            t["hs_sl2"][:, (kc % 2) * 512 + i * 128:(kc % 2) * 512 + (i + 1) * 128],
                            t["wsl"][:, (grp % 2) * H4 + kin * D + oc * 512:(grp % 2) * H4 + kin * D + (oc + 1) * 512],
                            start=(kc == 0), stop=(kc == 31))
                tm.inc(ins, 1)
        E.T(t_m2)
        assert x2_tm[tb * 32 + 31] == tm.n
        tm_after = tm.n

        for i in range(4):
            def v_r(v, tb=tb, i=i, tm_after=tm_after):
                v.wait_ge(tm.h, tm_after)
                v.wait_ge(vv.h, vv.n)
                for oc in range(2):
                    vv.inc(v.tensor_tensor(
                        t["sq_scr"][:, oc * 512:(oc + 1) * 512],
                        bank(i * 2 + oc),
                        t["a2bc"][:, oc * 512:(oc + 1) * 512], ALU.mult), 1)
                    v.wait_ge(vv.h, vv.n)
                    vv.inc(v.tensor_tensor(
                        t["sq_scr"][:, oc * 512:(oc + 1) * 512],
                        t["sq_scr"][:, oc * 512:(oc + 1) * 512],
                        t["b2bc"][:, oc * 512:(oc + 1) * 512], ALU.add), 1)
                    v.wait_ge(vv.h, vv.n)
                    vv.inc(v.tensor_tensor(
                        t["ln_r"][:, oc * 512:(oc + 1) * 512],
                        t["sq_scr"][:, oc * 512:(oc + 1) * 512],
                        t["stream"][:, (tb * 4 + i) * D + oc * 512:(tb * 4 + i) * D + (oc + 1) * 512],
                        ALU.add), 1)
            E.V(v_r)
            vv_r = vv.n
            ii = tb * 4 + i
            if final:
                emit_ln(E, [(vv.h, vv_r)],
                        t["stream"][:, ii * D:(ii + 1) * D],
                        out_f32_ap=t["yout"][:, (ii % 2) * D:(ii % 2 + 1) * D])
                emit_ystore(E, ii, ii % 2)
            else:
                emit_ln(E, [(vv.h, vv_r)],
                        t["stream"][:, ii * D:(ii + 1) * D])
        m2_vv[tb] = vv.n



def emit_l3(E, bank):
    cfg = E.cfg
    S, D, T, TT, TC, ST = cfg.S, cfg.D, cfg.T, cfg.TT, cfg.TC, cfg.ST
    W = cfg.W
    t, dd = E.t, E.d
    tm, vv, ss, gq, syq, scq, cc = E.tm, E.vv, E.ss, E.gq, E.syq, E.scq, E.cc
    gx, sye, syo, rde, rdo = E.gx, E.sye, E.syo, E.rde, E.rdo
    ps8 = t["ps8"]
    ABa, AFa = E.ABa, E.AFa
    R3 = 544
    pidx_sb = t["pidx_sb"]

    def ar(name, parts, n, fr=None):
        t[name] = (fr or ABa).alloc(parts, n)

    ABa.reset()
    AFa.reset()
    ar("x_fm", 128, 8 * T)
    ar("wsb3", 128, 8 * 160)
    ar("wdt2", 64, 8 * 128)
    ar("dtr_sb", 64, T)
    ar("l3st", 128, T)
    ar("bc_st", 16, 2 * T)
    ar("bdt2pm", 128, 8, AFa)
    sy_fm = emit_stream_to_fm(E, t["x_fm"])

    def g_w3(g):
        for kc in range(8):
            gq.inc(g.dma_start(out=t["wsb3"][:, kc * 160:kc * 160 + 16],
                               in_=dd["l3_wB"][kc * 128:(kc + 1) * 128, :]), 16)
            gq.inc(g.dma_start(out=t["wsb3"][:, kc * 160 + 16:kc * 160 + 32],
                               in_=dd["l3_wC"][kc * 128:(kc + 1) * 128, :]), 16)
            gq.inc(g.dma_start(out=t["wsb3"][:, kc * 160 + 32:kc * 160 + 96],
                               in_=dd["l3_wdt1"][kc * 128:(kc + 1) * 128, :]), 16)
        gq.inc(g.dma_start(out=t["wdt2"][:, :], in_=dd["l3_wdt2T"][:, :]), 16)
        gq.inc(g.dma_start(out=t["bdt2pm"][:, :], in_=dd["l3_bdt2pm"][:, :]), 16)
    E.G(g_w3)
    g_w3_n = gq.n

    def sy_xf(sy):
        sy.wait_ge(syq.h, sy_fm)
        for kc in range(8):
            syq.inc(sy.dma_start(out=dd["xfds"][kc * 128:(kc + 1) * 128, :],
                                 in_=t["x_fm"][:, kc * T:(kc + 1) * T]), 16)
    E.Y(sy_xf)
    sy_xf_n = syq.n

    def t_bc(te):
        te.wait_ge(gq.h, g_w3_n)
        te.wait_ge(syq.h, sy_fm)
        for kc in range(8):
            for tc in range(TC):
                ins = te.matmul(
                    ps8[0:16, tc * 512:tc * 512 + W],
                    t["wsb3"][:, kc * 160:kc * 160 + 16],
                    t["x_fm"][:, kc * T + tc * W:kc * T + (tc + 1) * W],
                    start=(kc == 0), stop=(kc == 7))
                ins = te.matmul(
                    ps8[0:16, (4 + tc) * 512:(4 + tc) * 512 + W],
                    t["wsb3"][:, kc * 160 + 16:kc * 160 + 32],
                    t["x_fm"][:, kc * T + tc * W:kc * T + (tc + 1) * W],
                    start=(kc == 0), stop=(kc == 7))
                if kc == 7:
                    tm.inc(ins, 1)
    E.T(t_bc)
    tm_bc = tm.n

    def v_bc(v):
        v.wait_ge(tm.h, tm_bc)
        for tc in range(TC):
            vv.inc(v.tensor_copy(t["bc_st"][:, tc * W:(tc + 1) * W],
                                 ps8[0:16, tc * 512:tc * 512 + W]), 1)
            vv.inc(v.tensor_copy(t["bc_st"][:, T + tc * W:T + (tc + 1) * W],
                                 ps8[0:16, (4 + tc) * 512:(4 + tc) * 512 + W]), 1)
    E.V(v_bc)
    vv_bc = vv.n

    def sy_bc(sy):
        sy.wait_ge(vv.h, vv_bc)
        for dst, base in ((dd["keep3"], 0), (dd["agin3"], 0)):
            for u in range(2):
                syq.inc(sy.dma_start(
                    out=dst[u * R3 + 512:u * R3 + 528, :],
                    in_=t["bc_st"][:, 0:T]), 16)
                syq.inc(sy.dma_start(
                    out=dst[u * R3 + 528:u * R3 + 544, :],
                    in_=t["bc_st"][:, T:2 * T]), 16)
    E.Y(sy_bc)
    sy_bc_n = syq.n

    def t_dtr(te):
        te.wait_ge(vv.h, vv_bc)
        for kc in range(8):
            for tc in range(TC):
                ins = te.matmul(
                    ps8[0:64, tc * 512:tc * 512 + W],
                    t["wsb3"][:, kc * 160 + 32:kc * 160 + 96],
                    t["x_fm"][:, kc * T + tc * W:kc * T + (tc + 1) * W],
                    start=(kc == 0), stop=(kc == 7))
                if kc == 7:
                    tm.inc(ins, 1)
    E.T(t_dtr)
    tm_dtr = tm.n

    def v_dtr(v):
        v.wait_ge(tm.h, tm_dtr)
        v.wait_ge(vv.h, vv.n)
        for tc in range(TC):
            vv.inc(v.tensor_copy(t["dtr_sb"][:, tc * W:(tc + 1) * W],
                                 ps8[0:64, tc * 512:tc * 512 + W]), 1)
    E.V(v_dtr)
    vv_dtr = vv.n

    dt_st = {}
    for dc in range(8):
        tm_dc0 = tm.n

        def t_dt(te, dc=dc):
            if dc == 0:
                te.wait_ge(vv.h, vv_dtr)
            if dc >= 1:
                te.wait_ge(ss.h, dt_ev[dc - 1])
            for tc in range(TC):
                tm.inc(te.matmul(
                    bank(4 + tc)[:, 0:W],
                    t["wdt2"][:, dc * 128:(dc + 1) * 128],
                    t["dtr_sb"][:, tc * W:(tc + 1) * W],
                    start=True, stop=True), 1)
        E.T(t_dt)
        if dc == 0:
            dt_ev = {}

        ss_dc0 = ss.n

        def s_dt(s, dc=dc, tm_dc0=tm_dc0):
            if dc >= 1:
                s.wait_ge((sye if (dc - 1) % 2 == 0 else syo).h, dt_st[dc - 1])
            for tc in range(TC):
                s.wait_ge(tm.h, tm_dc0 + tc + 1)
                if tc >= 1:
                    s.wait_ge(vv.h, sp_vv0 + tc)
                ss.inc(s.activation(
                    out=t["sq_scr"][:, 0:W],
                    in_=bank(4 + tc)[:, 0:W], func=AF.Relu,
                    bias=t["bdt2pm"][:, dc:dc + 1]), 1)
                s.wait_ge(ss.h, ss.n)
                ss.inc(s.activation(
                    out=t["ln_f"][:, 0:W],
                    in_=bank(4 + tc)[:, 0:W], func=AF.Abs,
                    bias=t["bdt2pm"][:, dc:dc + 1]), 1)
                s.wait_ge(ss.h, ss.n)
                ss.inc(s.activation(out=t["ln_f"][:, 0:W], in_=t["ln_f"][:, 0:W],
                                    func=AF.Exp, scale=-1.0), 1)
                s.wait_ge(ss.h, ss.n)
                ss.inc(s.activation(out=t["ln_f"][:, 0:W], in_=t["ln_f"][:, 0:W],
                                    func=AF.Ln, bias=1.0), 1)
            dt_ev[dc] = ss.n
        sp_vv0 = vv.n
        E.S(s_dt)
        ss_dt = ss.n

        for tc in range(TC):
            def v_sp(v, dc=dc, tc=tc, base=ss_dc0 + 4 * tc + 4):
                v.wait_ge(ss.h, base)
                vv.inc(v.tensor_tensor(
                    t["l3st"][:, tc * W:(tc + 1) * W],
                    t["sq_scr"][:, 0:W], t["ln_f"][:, 0:W], ALU.add), 1)
            E.V(v_sp)
            assert vv.n == sp_vv0 + tc + 1
        vv_sp = vv.n
        psem = sye if dc % 2 == 0 else syo

        def sy_dt(sy, dc=dc, vv_sp=vv_sp, psem=psem):
            sy.wait_ge(vv.h, vv_sp)
            psem.inc(sy.dma_start(out=dd["dtds"][dc * 128:(dc + 1) * 128, :],
                                  in_=t["l3st"][:, :]), 16)
        E.Y(sy_dt)
        dt_st[dc] = psem.n

    def g_slab(g):
        g.wait_ge(sye.h, sye.n)
        g.wait_ge(syo.h, syo.n)
        g.wait_ge(syq.h, syq.n)
        for dc in range(8):
            dst = dd["keep3"] if dc < 4 else dd["agin3"]
            u2 = (dc % 4) // 2
            ro = (dc % 2) * 128
            gq.inc(g.dma_start(
                out=dst[u2 * R3 + ro:u2 * R3 + ro + 128, :],
                in_=dd["dtds"][dc * 128:(dc + 1) * 128, :]), 16)
        with g.register("l3kx") as hr:
            g.reg_load(hr, pidx_sb[0:1, 0:1])
            off = g.snap(hr) * 512
            gx.inc(g.dma_start(
                out=dd["keep3"].rearrange("(u r) t -> u r t", u=2)[:, 256:512, :],
                in_=dd["xfds"][bass.ds(off, 512), :]
                    .rearrange("(u r) t -> u r t", u=2)), 16)
        with g.register("l3ax") as hr2:
            g.reg_load(hr2, pidx_sb[0:1, 1:2])
            off2 = g.snap(hr2) * 512
            gx.inc(g.dma_start(
                out=dd["agin3"].rearrange("(u r) t -> u r t", u=2)[:, 256:512, :],
                in_=dd["xfds"][bass.ds(off2, 512), :]
                    .rearrange("(u r) t -> u r t", u=2)), 16)
    E.G(g_slab)
    gq_slab = gq.n
    gx_slab = gx.n

    def g_asm3(g):
        g.wait_ge(gq.h, gq_slab)
        g.wait_ge(gx.h, gx_slab)
        cc.inc(g.collective_compute(
            "AllGather", ALU.bypass,
            replica_groups=[[0, 1], [2, 3], [4, 5], [6, 7]],
            ins=[dd["agin3"][:, :]], outs=[dd["agout3"][:, :]]), 1)
        view = dd["a2a3o"].rearrange("(u s r) t -> u s r t", u=2, s=2)
        with g.register("l3keep") as hr:
            g.reg_load(hr, pidx_sb[0:1, 0:1])
            off = g.snap(hr)
            gx.inc(g.dma_start(
                out=view[:, bass.ds(off, 1), :, :],
                in_=dd["keep3"].rearrange("(u o r) t -> u o r t",
                                          u=2, o=1)), 16)
        g.wait_ge(cc.h, cc.n)
        g.wait_ge(gx.h, gx.n)
        with g.register("l3part") as hr2:
            g.reg_load(hr2, pidx_sb[0:1, 1:2])
            off2 = g.snap(hr2)
            gx.inc(g.dma_start(
                out=view[:, bass.ds(off2, 1), :, :],
                in_=dd["agout3"].rearrange("(gg u r) t -> u gg r t",
                                           gg=2, u=2)[:, bass.ds(off2, 1), :, :]),
                16)
    E.G(g_asm3)
    gx_asm3 = gx.n

    E.barrier()
    emit_l3_units(E, bank)

    def g_ret3(g):
        g.wait_ge(rde.h, rde.n)
        g.wait_ge(rdo.h, rdo.n)
        riv = dd["a2ri3"].rearrange("(u s r) t -> u s r t", u=2, s=2)
        yv = dd["yret"].rearrange("(s u r) t -> u s r t", s=2, u=2)
        with g.register("l3rs") as hr:
            g.reg_load(hr, pidx_sb[0:1, 1:2])
            off = g.snap(hr)
            gx.inc(g.dma_start(
                out=dd["ri3_in"].rearrange("(u o r) t -> u o r t", u=2, o=1),
                in_=riv[:, bass.ds(off, 1), :, :]), 16)
        with g.register("l3rl") as hr2:
            g.reg_load(hr2, pidx_sb[0:1, 0:1])
            off2 = g.snap(hr2)
            gx.inc(g.dma_start(
                out=yv[:, bass.ds(off2, 1), :, :],
                in_=riv[:, bass.ds(off2, 1), :, :]), 16)
        g.wait_ge(gx.h, gx.n)
        cc.inc(g.collective_compute(
            "AllGather", ALU.bypass,
            replica_groups=[[0, 1], [2, 3], [4, 5], [6, 7]],
            ins=[dd["ri3_in"][:, :]], outs=[dd["ri3_out"][:, :]]), 1)
        g.wait_ge(cc.h, cc.n)
        with g.register("l3rp") as hr3:
            g.reg_load(hr3, pidx_sb[0:1, 1:2])
            off3 = g.snap(hr3)
            gx.inc(g.dma_start(
                out=yv[:, bass.ds(off3, 1), :, :],
                in_=dd["ri3_out"].rearrange("(gg u r) t -> u gg r t",
                                            gg=2, u=2)[:, bass.ds(off3, 1), :, :]),
                16)
    E.G(g_ret3)

    E.barrier()
    emit_l3_post(E, bank)
    E.barrier()
    emit_l3_ffn(E, bank)


def emit_l3_units(E, bank):
    cfg = E.cfg
    S, D, T, TT, TC, ST = cfg.S, cfg.D, cfg.T, cfg.TT, cfg.TC, cfg.ST
    t, dd = E.t, E.d
    tm, vv, ss, gq, syq, scq, cc = E.tm, E.vv, E.ss, E.gq, E.syq, E.scq, E.cc
    gx, sye, syo, rde, rdo = E.gx, E.sye, E.syo, E.rde, E.rdo
    ABa, AFa = E.ABa, E.AFa
    R3 = 544

    def ar(name, parts, n, fr=None):
        t[name] = (fr or ABa).alloc(parts, n)

    ABa.reset()
    AFa.reset()
    ar("dt_u", 128, 2 * S)
    ar("x_u", 128, 2 * S)
    ar("B_bc", 128, S)
    ar("C_bc", 128, S)
    ar("xB", 128, S)
    ar("hbf", 128, S)
    ar("Bbar", 128, S)
    ar("ya", 128, 2 * S)
    ar("af", 128, S, AFa)
    ar("A_sb", 128, 64, AFa)
    ar("iA_sb", 128, 64, AFa)
    ar("lgA", 128, 64, AFa)
    yacc = t["ya"].bitcast(F32)

    def g_A(g):
        for q4 in range(4):
            gx.inc(g.dma_start(out=t["lgA"][:, q4 * 16:(q4 + 1) * 16],
                               in_=dd["l3_logA"][q4 * 128:(q4 + 1) * 128, :]), 16)
    E.G(g_A)
    g_A_n = gx.n

    def s_A(s):
        s.wait_ge(gx.h, g_A_n)
        ss.inc(s.activation(out=t["A_sb"][:, :], in_=t["lgA"][:, :],
                            func=AF.Exp), 1)
    E.S(s_A)
    ss_A = ss.n

    def v_A(v):
        v.wait_ge(ss.h, ss_A)
        vv.inc(v.tensor_scalar(t["A_sb"][:, :], t["A_sb"][:, :],
                               -1.0, None, ALU.mult), 1)
        v.wait_ge(vv.h, vv.n)
        vv.inc(v.tensor_scalar(t["iA_sb"][:, :], t["A_sb"][:, :],
                               1e-8, None, ALU.add), 1)
        v.wait_ge(vv.h, vv.n)
        vv.inc(v.reciprocal(t["iA_sb"][:, :], t["iA_sb"][:, :]), 1)
    E.V(v_A)
    vv_A = vv.n

    for u in range(2):
        slots = (2 * u, 2 * u + 1)

        def g_ld(g, u=u, slots=slots):
            g.wait_ge(gx.h, gx.n)
            if u == 1:
                g.wait_ge(vv.h, vv.n)
                g.wait_ge(rde.h, rde.n)
                g.wait_ge(rdo.h, rdo.n)
            for half, r in enumerate(slots):
                for dc2 in range(2):
                    gq.inc(g.dma_start(
                        out=t["dt_u"][:, dc2 * S + half * T:dc2 * S + half * T + T],
                        in_=dd["a2a3o"][r * R3 + dc2 * 128:r * R3 + (dc2 + 1) * 128, :]), 16)
                    gq.inc(g.dma_start(
                        out=t["x_u"][:, dc2 * S + half * T:dc2 * S + half * T + T],
                        in_=dd["a2a3o"][r * R3 + 256 + dc2 * 128:r * R3 + 256 + (dc2 + 1) * 128, :]), 16)
        E.G(g_ld)
        g_ld_n = gq.n

        ev_hist = []
        for dc2 in range(2):
            qc = u * 2 + dc2
            for n in range(16):
                def sy_b(sy, u=u, slots=slots, n=n, dc2=dc2):
                    if dc2 == 0 and n == 0:
                        sy.wait_ge(cc.h, cc.n)
                        if u == 0:
                            sy.wait_ge(vv.h, vv_A)
                        else:
                            sy.wait_ge(vv.h, vv.n)
                    else:
                        sy.wait_ge(vv.h, bc_used[(u, dc2, n - 1) if n >= 1
                                                 else (u, dc2 - 1, 15)])
                    for half, r in enumerate(slots):
                        E.kq.inc(sy.dma_start(
                            out=t["B_bc"][:, half * T:half * T + T],
                            in_=dd["a2a3o"][r * R3 + 512 + n:r * R3 + 513 + n, :]
                                .partition_broadcast(128)), 16)
                        E.kq.inc(sy.dma_start(
                            out=t["C_bc"][:, half * T:half * T + T],
                            in_=dd["a2a3o"][r * R3 + 528 + n:r * R3 + 529 + n, :]
                                .partition_broadcast(128)), 16)
                E.Y(sy_b)
                if dc2 == 0 and n == 0 and u == 0:
                    bc_used = {}
                sy_b_n = E.kq.n

                def v_a(v, u=u, dc2=dc2, n=n, qc=qc, sy_b_n=sy_b_n,
                        g_ld_n=g_ld_n):
                    if n == 0 and dc2 == 0:
                        v.wait_ge(gq.h, g_ld_n)
                        if u == 0:
                            v.wait_ge(vv.h, vv_A)
                    v.wait_ge(vv.h, vv.n)
                    v.wait_ge(ss.h, ss.n)
                    vv.inc(v.tensor_scalar(
                        t["af"][:, :], t["dt_u"][:, dc2 * S:(dc2 + 1) * S],
                        t["A_sb"][:, qc * 16 + n:qc * 16 + n + 1], None,
                        ALU.mult), 1)
                E.V(v_a)
                vv_a = vv.n

                def s_e(s, vv_a=vv_a):
                    s.wait_ge(vv.h, vv_a)
                    ss.inc(s.activation(out=t["af"][:, :], in_=t["af"][:, :],
                                        func=AF.Exp), 1)
                E.S(s_e)
                ss_e = ss.n

                ev_guard = ev_hist[-1] if (n == 0 and ev_hist) else None

                def v_b(v, u=u, dc2=dc2, n=n, qc=qc, ss_e=ss_e, sy_b_n=sy_b_n,
                        ev_guard=ev_guard):
                    v.wait_ge(ss.h, ss_e)
                    v.wait_ge(E.kq.h, sy_b_n)
                    if ev_guard is not None:
                        v.wait_ge(ev_guard[0].h, ev_guard[1])
                    vv.inc(v.tensor_tensor(t["xB"][:, :],
                                           t["x_u"][:, dc2 * S:(dc2 + 1) * S],
                                           t["B_bc"][:, :], ALU.mult), 1)
                    v.wait_ge(vv.h, vv.n)
                    vv.inc(v.tensor_scalar(
                        t["xB"][:, :], t["xB"][:, :],
                        t["iA_sb"][:, qc * 16 + n:qc * 16 + n + 1], None,
                        ALU.mult), 1)
                    v.wait_ge(vv.h, vv.n)
                    vv.inc(v.scalar_tensor_tensor(
                        t["Bbar"][:, :], t["af"][:, :], -1.0, t["xB"][:, :],
                        ALU.add, ALU.mult), 1)
                    v.wait_ge(vv.h, vv.n)
                    CW = S
                    for ch in range(S // CW):
                        init = (0.0 if ch == 0
                                else t["hbf"][:, ch * CW - 1:ch * CW])
                        vv.inc(v.tensor_tensor_scan(
                            t["hbf"][:, ch * CW:(ch + 1) * CW],
                            t["af"][:, ch * CW:(ch + 1) * CW],
                            t["Bbar"][:, ch * CW:(ch + 1) * CW],
                            init, ALU.mult, ALU.add), 1)
                        v.wait_ge(vv.h, vv.n)
                    vv.inc(v.tensor_tensor(t["xB"][:, :], t["hbf"][:, :],
                                           t["C_bc"][:, :], ALU.mult), 1)
                    v.wait_ge(vv.h, vv.n)
                    if n == 0:
                        vv.inc(v.tensor_copy(yacc[:, :], t["xB"][:, :]), 1)
                    else:
                        vv.inc(v.tensor_tensor(yacc[:, :], yacc[:, :],
                                               t["xB"][:, :], ALU.add), 1)
                E.V(v_b)
                bc_used[(u, dc2, n)] = vv.n

            def v_ev(v, u=u, dc2=dc2):
                v.wait_ge(vv.h, vv.n)
                if len(ev_hist) >= 2:
                    sm, val = ev_hist[-2]
                    v.wait_ge(sm.h, val)
                vv.inc(v.tensor_copy(t["Bbar"][:, :], yacc[:, :]), 1)
            E.V(v_ev)
            vv_ev = vv.n
            rsem = rde if dc2 % 2 == 0 else rdo

            def sy_y(sy, u=u, dc2=dc2, slots=slots, vv_ev=vv_ev, rsem=rsem):
                sy.wait_ge(vv.h, vv_ev)
                for half, r in enumerate(slots):
                    rsem.inc(sy.dma_start(
                        out=dd["a2ri3"][r * 256 + dc2 * 128:r * 256 + (dc2 + 1) * 128, :],
                        in_=t["Bbar"][:, half * T:half * T + T]), 16)
            E.Y(sy_y)
            ev_hist.append((rsem, rsem.n))


def emit_l3_post(E, bank):
    cfg = E.cfg
    S, D, T, TT, TC = cfg.S, cfg.D, cfg.T, cfg.TT, cfg.TC
    t, dd = E.t, E.d
    tm, vv, ss, gq, syq, scq, cc = E.tm, E.vv, E.ss, E.gq, E.syq, E.scq, E.cc
    gx, sye, syo = E.gx, E.sye, E.syo
    ABa, AFa = E.ABa, E.AFa

    def ar(name, parts, n, fr=None):
        t[name] = (fr or ABa).alloc(parts, n)

    ABa.reset()
    AFa.reset()
    ar("ytile", 128, 2 * D)
    ar("dskbc", 128, D, AFa)

    def g_dsk(g):
        gq.inc(g.dma_start(out=t["dskbc"][:, :],
                           in_=dd["l3_dskip"][0:1, :].partition_broadcast(128)), 16)
    E.G(g_dsk)
    g_dsk_n = gq.n

    y_vv = {}
    for i in range(TT):
        par = i % 2
        ysem = E.vtq if par == 0 else E.kq

        def sy_y(sy, i=i, par=par, ysem=ysem):
            if i == 0:
                sy.wait_ge(cc.h, cc.n)
                sy.wait_ge(gx.h, gx.n)
            if i >= 2:
                sy.wait_ge(vv.h, y_vv[i - 2])
            for dc in range(8):
                ysem.inc(sy.dma_start(
                    out=t["ytile"][:, par * D + dc * 128:par * D + (dc + 1) * 128],
                    in_=dd["yret"][dc * 128:(dc + 1) * 128,
                                   i * 128:(i + 1) * 128],
                    transpose=True), 16)
        E.Y(sy_y)
        sy_y_n = ysem.n

        def v_r(v, i=i, par=par, sy_y_n=sy_y_n, ysem=ysem):
            v.wait_ge(ysem.h, sy_y_n)
            if i == 0:
                v.wait_ge(gq.h, g_dsk_n)
            v.wait_ge(vv.h, vv.n)
            v.wait_ge(ss.h, ss.n)
            for oc in range(2):
                vv.inc(v.tensor_tensor(
                    t["sq_scr"][:, oc * 512:(oc + 1) * 512],
                    t["stream"][:, i * D + oc * 512:i * D + (oc + 1) * 512],
                    t["dskbc"][:, oc * 512:(oc + 1) * 512], ALU.mult), 1)
                v.wait_ge(vv.h, vv.n)
                vv.inc(v.tensor_tensor(
                    t["sq_scr"][:, oc * 512:(oc + 1) * 512],
                    t["sq_scr"][:, oc * 512:(oc + 1) * 512],
                    t["ytile"][:, par * D + oc * 512:par * D + (oc + 1) * 512],
                    ALU.add), 1)
                v.wait_ge(vv.h, vv.n)
                vv.inc(v.tensor_tensor(
                    t["ln_r"][:, oc * 512:(oc + 1) * 512],
                    t["sq_scr"][:, oc * 512:(oc + 1) * 512],
                    t["stream"][:, i * D + oc * 512:i * D + (oc + 1) * 512],
                    ALU.add), 1)
        E.V(v_r)
        vv_r = vv.n
        emit_ln(E, [(vv.h, vv_r)], t["stream"][:, i * D:(i + 1) * D])
        y_vv[i] = vv.n


def emit_l3_ffn(E, bank):
    cfg = E.cfg
    S, D, T, TT, TC = cfg.S, cfg.D, cfg.T, cfg.TT, cfg.TC
    t, dd = E.t, E.d
    tm, vv, ss, gq, syq, scq, cc = E.tm, E.vv, E.ss, E.gq, E.syq, E.scq, E.cc
    gx, sye, syo = E.gx, E.sye, E.syo
    ABa, AFa = E.ABa, E.AFa
    H4 = 4 * D

    def ar(name, parts, n, fr=None):
        t[name] = (fr or ABa).alloc(parts, n)

    ABa.reset()
    AFa.reset()
    ar("x_fm", 128, 8 * T)
    ar("w1s", 128, 8 * 512)
    ar("w1s2", 128, 8 * 512)
    ar("wsl", 128, 2 * H4)
    ar("hbig", 128, 2 * 512)
    ar("hs_sl2", 128, 2 * 512)
    ar("b1bc", 128, H4)
    ar("b2bc", 128, D)
    sy_fm = emit_stream_to_fm(E, t["x_fm"])

    def g_bc(g):
        gq.inc(g.dma_start(out=t["b1bc"][:, :],
                           in_=dd["l3f_b1"][0:1, :].partition_broadcast(128)), 16)
        gq.inc(g.dma_start(out=t["b2bc"][:, :],
                           in_=dd["l3f_b2"][0:1, :].partition_broadcast(128)), 16)
    E.G(g_bc)
    g_bc_n = gq.n

    w1_mm = {}
    m1_ev = {}
    m1_st = {}
    NBK = min(8, TT)
    for oc in range(8):
        wsem2 = E.gpe if oc % 2 == 0 else E.gpo

        def g_w1(g, oc=oc, wsem2=wsem2):
            if oc >= 2:
                g.wait_ge(tm.h, w1_mm[oc - 2])
            wbuf = t["w1s"] if oc % 2 == 0 else t["w1s2"]
            for kc in range(8):
                wsem2.inc(g.dma_start(
                    out=wbuf[:, kc * 512:(kc + 1) * 512],
                    in_=dd["l3f_w1T"][kc * 128:(kc + 1) * 128,
                                      oc * 512:(oc + 1) * 512]), 16)
        E.G(g_w1)
        g_w1_n = wsem2.n
        for i2 in range(TT):
            m1_ev[oc * TT + i2] = ss.n + i2 + 1
        tm_oc0 = tm.n

        def t_m1(te, oc=oc, g_w1_n=g_w1_n, wsem2=wsem2):
            te.wait_ge(wsem2.h, g_w1_n)
            if oc == 0:
                te.wait_ge(syq.h, sy_fm)
            wbuf = t["w1s"] if oc % 2 == 0 else t["w1s2"]
            for i in range(TT):
                b = i % NBK
                if oc * TT + i >= NBK:
                    te.wait_ge(ss.h, m1_ev[oc * TT + i - NBK])
                for kc in range(8):
                    ins = te.matmul(
                        bank(b),
                        t["x_fm"][:, kc * T + i * 128:kc * T + (i + 1) * 128],
                        wbuf[:, kc * 512:(kc + 1) * 512],
                        start=(kc == 0), stop=(kc == 7))
                tm.inc(ins, 1)
        E.T(t_m1)
        w1_mm[oc] = tm.n

        for i in range(TT):
            def v_e1(v, oc=oc, i=i, tm_oc0=tm_oc0):
                v.wait_ge(tm.h, tm_oc0 + i + 1)
                if oc == 0 and i == 0:
                    v.wait_ge(gq.h, g_bc_n)
                if oc * TT + i >= 1:
                    v.wait_ge(ss.h, m1_ev[oc * TT + i - 1])
                v.wait_ge(vv.h, vv.n)
                vv.inc(v.tensor_tensor(
                    t["sq_scr"][:, 0:512], bank(i % NBK),
                    t["b1bc"][:, oc * 512:(oc + 1) * 512], ALU.add), 1)
            E.V(v_e1)
            vv_e1 = vv.n
            par = (oc * TT + i) % 2

            def s_g1(s, oc=oc, i=i, vv_e1=vv_e1, par=par):
                s.wait_ge(vv.h, vv_e1)
                if oc * TT + i >= 2:
                    s.wait_ge((sye if par == 0 else syo).h,
                              m1_st[oc * TT + i - 2])
                ss.inc(s.activation(
                    out=t["hbig"][:, par * 512:(par + 1) * 512],
                    in_=t["sq_scr"][:, 0:512], func=AF.Gelu), 1)
            E.S(s_g1)
            assert m1_ev[oc * TT + i] == ss.n
            psem = sye if par == 0 else syo

            def sy_h(sy, oc=oc, i=i, psem=psem, par=par, ssn=ss.n):
                sy.wait_ge(ss.h, ssn)
                psem.inc(sy.dma_start(
                    out=dd["hds"][i * 128:(i + 1) * 128, oc * 512:(oc + 1) * 512],
                    in_=t["hbig"][:, par * 512:(par + 1) * 512]), 16)
            E.Y(sy_h)
            m1_st[oc * TT + i] = psem.n

    final = cfg.n_layers == 4
    NG = TT // 4
    x2_ld = {}
    x2_tm = {}
    w2_ld = {}
    m2_vv = {}
    w2_tm = {}
    vv_mm2_0 = vv.n
    ye_h, yo_h = sye.n, syo.n
    for tb in range(NG):
        for kc2 in range(32):
            x2_tm[tb * 32 + kc2] = tm.n + kc2 + 1
        for g2 in range(8):
            w2_tm[tb * 8 + g2] = tm.n + (g2 + 1) * 4

        def sy_x2(sy, tb=tb):
            if tb == 0:
                sy.wait_ge(sye.h, ye_h)
                sy.wait_ge(syo.h, yo_h)
            for kc in range(32):
                gi = tb * 32 + kc
                if gi >= 2:
                    sy.wait_ge(tm.h, x2_tm[gi - 2])
                psem = E.qpe if kc % 2 == 0 else E.qpo
                psem.inc(sy.dma_start(
                    out=t["hs_sl2"][:, (kc % 2) * 512:(kc % 2) * 512 + 512],
                    in_=dd["hds"][tb * 512:(tb + 1) * 512,
                                  kc * 128:(kc + 1) * 128],
                    transpose=True), 16)
                x2_ld[(tb, kc)] = (psem, psem.n)
        E.Y(sy_x2)

        def g_w2(g, tb=tb):
            for grp in range(8):
                gg = tb * 8 + grp
                if gg == 0:
                    g.wait_ge(vv.h, vv_mm2_0)
                    g.wait_ge(sye.h, sye.n)
                    g.wait_ge(syo.h, syo.n)
                if gg >= 2:
                    g.wait_ge(tm.h, w2_tm[gg - 2])
                psem = E.gpe if grp % 2 == 0 else E.gpo
                psem.inc(g.dma_start(
                    out=t["wsl"][:, (grp % 2) * H4:(grp % 2) * H4 + 4 * D]
                        .rearrange("p (a c) -> p a c", c=D),
                    in_=dd["l3f_w2T"][grp * 512:(grp + 1) * 512, :]
                        .rearrange("(a p) c -> p a c", p=128)), 16)
                w2_ld[(tb, grp)] = (psem, psem.n)
        E.G(g_w2)
        tmb0 = tm.n

        def t_m2(te, tb=tb, tmb0=tmb0):
            if tb == 0:
                te.wait_ge(vv.h, vv_mm2_0)
            else:
                te.wait_ge(vv.h, m2_vv[tb - 1])
            for kc in range(32):
                grp, kin = kc // 4, kc % 4
                if kin == 0:
                    wsem, wval = w2_ld[(tb, grp)]
                    te.wait_ge(wsem.h, wval)
                xsem, xval = x2_ld[(tb, kc)]
                te.wait_ge(xsem.h, xval)
                for i in range(4):
                    for oc in range(2):
                        ins = te.matmul(
                            bank(i * 2 + oc),
                            t["hs_sl2"][:, (kc % 2) * 512 + i * 128:(kc % 2) * 512 + (i + 1) * 128],
                            t["wsl"][:, (grp % 2) * H4 + kin * D + oc * 512:(grp % 2) * H4 + kin * D + (oc + 1) * 512],
                            start=(kc == 0), stop=(kc == 31))
                tm.inc(ins, 1)
        E.T(t_m2)
        assert x2_tm[tb * 32 + 31] == tm.n
        tm_after = tm.n

        for i in range(4):
            def v_r(v, tb=tb, i=i, tm_after=tm_after):
                v.wait_ge(tm.h, tm_after)
                v.wait_ge(vv.h, vv.n)
                for oc in range(2):
                    vv.inc(v.tensor_tensor(
                        t["sq_scr"][:, oc * 512:(oc + 1) * 512],
                        bank(i * 2 + oc),
                        t["b2bc"][:, oc * 512:(oc + 1) * 512], ALU.add), 1)
                    v.wait_ge(vv.h, vv.n)
                    vv.inc(v.tensor_tensor(
                        t["ln_r"][:, oc * 512:(oc + 1) * 512],
                        t["sq_scr"][:, oc * 512:(oc + 1) * 512],
                        t["stream"][:, (tb * 4 + i) * D + oc * 512:(tb * 4 + i) * D + (oc + 1) * 512],
                        ALU.add), 1)
            E.V(v_r)
            vv_r = vv.n
            ii = tb * 4 + i
            if final:
                emit_ln(E, [(vv.h, vv_r)],
                        t["stream"][:, ii * D:(ii + 1) * D],
                        out_f32_ap=t["yout"][:, (ii % 2) * D:(ii % 2 + 1) * D])
                emit_ystore(E, ii, ii % 2)
            else:
                emit_ln(E, [(vv.h, vv_r)],
                        t["stream"][:, ii * D:(ii + 1) * D])
        m2_vv[tb] = vv.n

def kernel(**inputs):
    cfg = Cfg(S=4096, n_layers=4)
    E = build_program(cfg)
    in_maps = host_prepare(inputs, cfg)
    res = run_bass_kernel_spmd(E.nc, in_maps, core_ids=list(range(8)))
    T, D = cfg.T, cfg.D
    out = np.empty((cfg.B * cfg.S, D), dtype=np.float32)
    for c in range(8):
        out[c * T:(c + 1) * T] = res.results[c]["y"]
    return out.reshape(cfg.B, cfg.S, D)
